# version 08
# baseline (speedup 1.0000x reference)
import numpy as np
from contextlib import ExitStack

import concourse.bass as bass
import concourse.mybir as mybir
from concourse.bass_utils import run_bass_kernel_spmd

F32 = mybir.dt.float32
BF16 = mybir.dt.bfloat16
U8 = mybir.dt.uint8
Alu = mybir.AluOpType
Act = mybir.ActivationFunctionType
AxX = mybir.AxisListType.X

P = 128
S = 2048
DM = 1024
NH, DH = 16, 64
KC = DM // P
TT = S // P
NQ = 4
QW = 512
LN2 = float(np.log(2.0))


class _Chan:
    __slots__ = ("h", "n")

    def __init__(self, h):
        self.h = h
        self.n = 0


class B:

    ENGS = ("sync", "tensor", "vector", "scalar", "gpsimd")

    def __init__(self, nc, es):
        self.nc = nc
        self.es = es
        self.ops = {e: [] for e in self.ENGS}
        self.chan = {e: _Chan(es.enter_context(nc.semaphore("c_" + e)))
                     for e in self.ENGS}
        self.dma = {}
        self._seen = {}

    def dch(self, name):
        if name not in self.dma:
            self.dma[name] = _Chan(
                self.es.enter_context(self.nc.semaphore("d_" + name)))
        return self.dma[name]

    def sb(self, name, shape, dtype=F32, scope=None):
        scope = self.es if scope is None else scope
        return scope.enter_context(self.nc.sbuf_tensor(name, list(shape), dtype))

    def ps(self, name, shape):
        return self.es.enter_context(self.nc.psum_tensor(name, list(shape), F32))

    def _wait(self, ename, eng, waits):
        for w in waits:
            if w is None:
                continue
            c, n = w
            if n is None or n <= 0:
                continue
            key = (ename, id(c))
            if self._seen.get(key, -1) >= n:
                continue
            self._seen[key] = n
            eng.wait_ge(c.h, n)

    def do(self, ename, f, wait=(), inc=False):
        ch = self.chan[ename]
        prev = ch.n
        if inc:
            ch.n += 2 if ename == "scalar" else 1
        wait = tuple(wait)

        def g(eng, f=f, wait=wait, inc=inc, ch=ch, ename=ename, prev=prev):
            self._wait(ename, eng, wait)
            if inc and prev > 0:
                self._wait(ename, eng, [(ch, prev)])
            ins = f(eng)
            if inc:
                ins.then_inc(ch.h, 1)
                if ename == "scalar":
                    self._wait(ename, eng, [(ch, prev + 1)])
                    eng.nop().then_inc(ch.h, 1)
        self.ops[ename].append(g)
        return ch.n

    qmap = {}

    def dma_start(self, chan, dst, src, wait=(), transpose=False, q="sync"):
        ch = self.dch(chan)
        ch.n += 16
        if not transpose:
            q = self.qmap.get(chan, q)
        wait = tuple(wait)

        def g(eng, dst=dst, src=src, wait=wait, ch=ch, q=q, tr=transpose):
            self._wait(q, eng, wait)
            if tr:
                eng.dma_start(dst, src, transpose=True).then_inc(ch.h, 16)
            else:
                eng.dma_start(dst, src).then_inc(ch.h, 16)
        self.ops[q].append(g)
        return ch.n

    def barrier(self):
        targets = []
        for e in self.ENGS:
            self.do(e, lambda eng: eng.nop(), inc=True)
            targets.append((self.chan[e], self.chan[e].n))
        for ch in self.dma.values():
            targets.append((ch, ch.n))
        for e in self.ENGS:
            self.do(e, lambda eng: eng.nop(), wait=list(targets))

    def finalize(self):
        nc = self.nc
        with nc.Block() as blk:
            @blk.sync
            def _(eng):
                for g in self.ops["sync"]:
                    g(eng)

            @blk.tensor
            def _(eng):
                for g in self.ops["tensor"]:
                    g(eng)

            @blk.vector
            def _(eng):
                for g in self.ops["vector"]:
                    g(eng)

            @blk.scalar
            def _(eng):
                for g in self.ops["scalar"]:
                    g(eng)

            @blk.gpsimd
            def _(eng):
                for g in self.ops["gpsimd"]:
                    g(eng)


def build(dev_tap=None, dev_dump=False):
    nc = bass.Bass("TRN2", target_bir_lowering=False)
    es = ExitStack()
    b = B(nc, es)
    b.qmap = {c: "gpsimd" for c in (
        "wc0", "wc1", "xf", "vtl", "wv", "gb0", "gb1", "kvld", "kvld2",
        "pqld", "cvld", "cwld", "dld", "c4", "c_ap", "dbcl", "dbc2l",
        "ldx0", "ldx1", "kqld", "hq2", "xsl")}

    x_in = nc.dram_tensor("x", [S, DM], F32, kind="ExternalInput")
    y_out = nc.dram_tensor("y", [S, DM], F32, kind="ExternalOutput")
    ident_in = nc.dram_tensor("ident", [P, P], F32, kind="ExternalInput")
    W = {}
    for nm, shp in [
        ("L0_attn_Wq", [DM, DM]), ("L0_attn_bq", [DM]),
        ("L0_attn_Wk", [DM, DM]), ("L0_attn_bk", [DM]),
        ("L0_attn_Wv", [DM, DM]), ("L0_attn_bv", [DM]),
        ("L0_attn_Wo", [DM, DM]), ("L0_attn_bo", [DM]),
        ("L0_ffn_weight", [DM, DM]), ("L0_ffn_weight_scale", [1]),
        ("L0_ffn_bias", [DM]),
        ("L1_attn_Wq", [DM, DM]), ("L1_attn_bq", [DM]),
        ("L1_attn_gq", [DM]), ("L1_attn_betaq", [DM]),
        ("L1_attn_Wk", [DM, DM]), ("L1_attn_bk", [DM]),
        ("L1_attn_gk", [DM]), ("L1_attn_betak", [DM]),
        ("L1_attn_Wv", [DM, DM]), ("L1_attn_bv", [DM]),
        ("L1_attn_gv", [DM]), ("L1_attn_betav", [DM]),
        ("L1_attn_Wfq", [DM, DM]), ("L1_attn_bfq", [DM]),
        ("L1_attn_Wfk", [DM, DM]), ("L1_attn_bfk", [DM]),
        ("L1_attn_ga", [DM]), ("L1_attn_ba", [DM]),
        ("L1_attn_W1", [DM, 4096]), ("L1_attn_b1", [4096]),
        ("L1_attn_W2", [4096, DM]), ("L1_attn_b2", [DM]),
        ("L1_ffn_w1", [4096, DM]), ("L1_ffn_conv_w", [4096 * 3]),
        ("L1_ffn_conv_b", [4096]), ("L1_ffn_w2", [DM, 4096]),
    ]:
        dt = U8 if nm == "L0_ffn_weight" else F32
        W[nm] = nc.dram_tensor(nm, shp, dt, kind="ExternalInput")

    KIND = "ExternalOutput" if dev_dump else "Internal"
    h_xf = nc.dram_tensor("h_xf", [KC, P, S], F32, kind=KIND)
    h_wT = {w: nc.dram_tensor("h_" + w, [KC, P, DM], F32, kind="Internal")
            for w in ("wq0", "wk0", "wv0", "wo0")}
    h_qf = nc.dram_tensor("h_qf", [NH, DH, S], F32, kind=KIND)
    h_kf = nc.dram_tensor("h_kf", [NH, DH, S], F32, kind=KIND)
    h_vt = nc.dram_tensor("h_vt", [NH, S, DH], F32, kind=KIND)
    h_q2 = nc.dram_tensor("h_q2", [NH, S], F32, kind=KIND)
    h_k2 = nc.dram_tensor("h_k2", [NH, S], F32, kind=KIND)
    h_den = nc.dram_tensor("h_den", [NH, S], F32, kind=KIND)
    h_x2f = nc.dram_tensor("h_x2f", [KC, P, S], F32, kind="Internal")
    h_qraw = nc.dram_tensor("h_qraw", [KC, P, S], F32, kind=KIND)
    h_kraw = nc.dram_tensor("h_kraw", [KC, P, S], F32, kind="Internal")
    h_q1 = nc.dram_tensor("h_q1", [KC, P, S], F32, kind=KIND)
    h_k1 = nc.dram_tensor("h_k1", [KC, P, S], F32, kind=KIND)
    h_v1t = nc.dram_tensor("h_v1t", [S, DM], F32, kind=KIND)
    h_pqf = nc.dram_tensor("h_pqf", [KC, P, S], F32, kind=KIND)
    h_pkt = nc.dram_tensor("h_pkt", [S, DM], F32, kind=KIND)
    h_af = nc.dram_tensor("h_af", [KC, P, S], F32, kind=KIND)
    h_h1 = nc.dram_tensor("h_h1", [32, P, S], F32, kind=KIND)
    h_rows = nc.dram_tensor("h_rows", [8, S], F32, kind=KIND)
    h_cw = nc.dram_tensor("h_cw", [3, 4096], F32, kind=KIND)
    h_dbg = nc.dram_tensor("h_dbg", [8, P, S], F32, kind=KIND) if dev_dump else None

    idF = b.sb("idF", [P, P], F32)
    xs = [b.sb(f"xs{t}", [P, DM], F32) for t in range(TT)]
    cols = b.sb("cols", [P, 32], F32)
    cONE = b.sb("cONE", [P, 1], F32)
    cEPS = b.sb("cEPS", [P, 4], F32)

    T = b.chan["tensor"]
    V = b.chan["vector"]
    Sc = b.chan["scalar"]
    G = b.chan["vector"]

    cCH = b.dch("const")
    b.dma_start("const", idF[:], ident_in.ap())
    for i, nm in enumerate(("L0_attn_bq", "L0_attn_bk", "L0_attn_bv", "L0_attn_bo")):
        for c in range(KC):
            b.dma_start("const", cols[:, 8 * i + c:8 * i + c + 1],
                        bass.AP(W[nm], c * P, [[1, P], [1, 1]]))
    n_cols = cCH.n
    b.do("vector", lambda e: e.memset(cONE[:, 0:1], 1.0))
    b.do("vector", lambda e: e.memset(cEPS[:, 0:1], 1e-8))
    b.do("vector", lambda e: e.memset(cEPS[:, 1:2], 1e-5), inc=True)
    n_const = V.n

    ps4a = b.ps("ps4a", [P, 2048])
    ps4b = b.ps("ps4b", [P, 2048])

    def pbank(i, rows=P, w=512):
        t = ps4a if i < 4 else ps4b
        c = (i % 4) * 512
        return t[0:rows, c:c + w]

    pb = [pbank(i) for i in range(8)]
    bank_free = {i: [(V, 0)] for i in range(8)}
    bank_ready = {i: (T, 0) for i in range(8)}

    tr_ev = b.sb("tr_ev", [P, 2 * P], F32)
    lnsq = b.sb("lnsq", [P, DM], F32)
    stat = b.sb("stat", [P, 8], F32)
    xadd = b.sb("xadd", [P, DM], F32)

    esA0 = ExitStack()
    ld_a = b.sb("ld_a", [P, DM], F32, scope=esA0)
    ld_b = b.sb("ld_b", [P, DM], F32, scope=esA0)
    bufs = (ld_a, ld_b)
    spt = [b.dch("spt0"), b.dch("spt1")]
    ldx = [b.dch("ldx0"), b.dch("ldx1")]
    ev_slot_busy = {0: 0, 1: 0}

    def a0_tiles(h_dst, dst_col0, buf, bslot):
        for kc in range(KC):
            bank_i = 6 + (kc % 2)
            slot = kc % 2
            tn = b.do("tensor",
                      lambda e, kc=kc, bi=bank_i:
                      e.transpose(pb[bi][0:P, 0:P],
                                  buf[:, kc * P:(kc + 1) * P], idF[:]),
                      wait=bank_free[bank_i] + [(cCH, n_cols),
                                               (ldx[bslot], ldx[bslot].n)],
                      inc=True)
            vn = b.do("vector",
                      lambda e, bi=bank_i, sl=slot: e.tensor_copy(
                          tr_ev[:, sl * P:(sl + 1) * P], pb[bi][0:P, 0:P]),
                      wait=[(T, tn), (spt[slot], ev_slot_busy[slot])],
                      inc=True)
            bank_free[bank_i] = [(V, vn)]
            dn = b.dma_start("spt%d" % slot,
                             h_dst[kc, :, dst_col0:dst_col0 + P],
                             tr_ev[:, slot * P:(slot + 1) * P],
                             wait=[(V, vn)])
            ev_slot_busy[slot] = dn

    ld_cons = [(T, 0), (T, 0)]
    for t in range(TT):
        bslot = t % 2
        b.dma_start("ldx%d" % bslot, bufs[bslot][:],
                    x_in.ap()[t * P:(t + 1) * P, :], wait=[ld_cons[bslot]])
        a0_tiles(h_xf, t * P, bufs[bslot], bslot)
        ld_cons[bslot] = (T, T.n)
    for wi, (wname, hname) in enumerate((("L0_attn_Wq", "wq0"),
                                         ("L0_attn_Wk", "wk0"),
                                         ("L0_attn_Wv", "wv0"),
                                         ("L0_attn_Wo", "wo0"))):
        for dc in range(KC):
            bslot = dc % 2
            b.dma_start("ldx%d" % bslot, bufs[bslot][:],
                        W[wname].ap()[dc * P:(dc + 1) * P, :],
                        wait=[ld_cons[bslot]])
            a0_tiles(h_wT[hname], dc * P, bufs[bslot], bslot)
            ld_cons[bslot] = (T, T.n)

    b.barrier()

    esA0.close()
    mcol = b.sb("mcol", [P, TT * NH], F32)
    esA1 = ExitStack()
    xsl = [b.sb(f"xsl{kc}", [P, QW], F32, scope=esA1) for kc in range(KC)]
    wti = b.sb("wti", [P, 2 * DM], F32, scope=esA1)
    ev1 = b.sb("ev1", [P, QW], F32, scope=esA1)
    sq1 = b.sb("sq1", [P, QW], F32, scope=esA1)
    q2r = b.sb("q2r", [NH, S], F32, scope=esA1)
    k2r = b.sb("k2r", [NH, S], F32, scope=esA1)
    r1tmp = b.sb("r1tmp", [1, QW], F32, scope=esA1)
    mrows = b.sb("mrows", [NH, S], F32, scope=esA1)

    wslot_free = [[(T, 0)], [(T, 0)]]
    ev_free = [(T, 0)]
    sq_free = [(T, 0)]
    r1_free = [(T, 0)]

    projs = (("Q", "wq0", 0, h_qf, q2r), ("K", "wk0", 8, h_kf, k2r),
             ("V", "wv0", 16, h_vt, None))

    chx = b.dch("xsl")
    chw = [b.dch("wti0"), b.dch("wti1")]
    chqk = b.dch("spqk")
    chq2s = b.dch("q2s")
    for tq in range(NQ):
        xw = (T, T.n)
        xsl_last = 0
        for kc in range(KC):
            xsl_last = b.dma_start("xsl", xsl[kc][:],
                                   h_xf[kc, :, tq * QW:(tq + 1) * QW],
                                   wait=[xw])
        for (pj, wn, bcol0, hdst, rows) in projs:
            for dc in range(KC):
                slot = dc % 2
                wld = 0
                for kc in range(KC):
                    wld = b.dma_start(
                        "wti%d" % slot,
                        wti[:, slot * DM + kc * P:slot * DM + (kc + 1) * P],
                        h_wT[wn][kc, :, dc * P:(dc + 1) * P],
                        wait=wslot_free[slot])
                bank_i = dc % 2
                tn = 0
                for kc in range(KC):
                    tn = b.do(
                        "tensor",
                        lambda e, kc=kc, s=slot, bi=bank_i:
                        e.matmul(pb[bi][:, 0:QW],
                                 wti[:, s * DM + kc * P:s * DM + (kc + 1) * P],
                                 xsl[kc][:],
                                 start=(kc == 0), stop=(kc == KC - 1)),
                        wait=(bank_free[bank_i] + [(chw[slot], wld),
                                                   (chx, xsl_last)])
                        if kc == 0 else (),
                        inc=(kc == KC - 1))
                wslot_free[slot] = [(T, tn)]
                vn = b.do("vector",
                          lambda e, bi=bank_i, c0=bcol0 + dc:
                          e.tensor_scalar(ev1[:], pb[bi][:, 0:QW],
                                          cols[:, c0:c0 + 1], None, Alu.add),
                          wait=[(T, tn), (cCH, n_cols)] + ev_free, inc=True)
                bank_free[bank_i] = [(V, vn)]
                if pj in ("Q", "K"):
                    sn = b.do("scalar",
                              lambda e: e.activation(sq1[:], ev1[:], Act.Square),
                              wait=[(V, vn)] + sq_free, inc=True)
                    b.dma_start("spqk",
                                hdst[2 * dc, :, tq * QW:(tq + 1) * QW],
                                ev1[0:DH, :], wait=[(V, vn)])
                    d2 = b.dma_start("spqk",
                                     hdst[2 * dc + 1, :, tq * QW:(tq + 1) * QW],
                                     ev1[DH:P, :], wait=[(V, vn)])
                    ev_free = [(chqk, d2), (Sc, sn)]
                    tq2 = 0
                    for half in range(2):
                        h = 2 * dc + half
                        tq2 = b.do("tensor",
                                   lambda e, hf=half: e.matmul(
                                       pb[5][0:1, 0:QW],
                                       cONE[hf * DH:(hf + 1) * DH, 0:1],
                                       sq1[hf * DH:(hf + 1) * DH, :],
                                       start=True, stop=True),
                                   wait=[(Sc, sn)] + bank_free[5], inc=True)
                        vq2 = b.do("vector",
                                   lambda e: e.tensor_copy(r1tmp[:],
                                                           pb[5][0:1, 0:QW]),
                                   wait=[(T, tq2)] + r1_free, inc=True)
                        bank_free[5] = [(V, vq2)]
                        dsh = b.dma_start("q2s",
                                          rows[h:h + 1, tq * QW:(tq + 1) * QW],
                                          r1tmp[:], wait=[(V, vq2)])
                        r1_free = [(chq2s, dsh)]
                    sq_free = [(T, tq2)]
                else:
                    for sub in range(4):
                        bank_i2 = 6 + (sub % 2)
                        slot2 = sub % 2
                        tn2 = b.do("tensor",
                                   lambda e, s=sub, bi=bank_i2:
                                   e.transpose(pb[bi][0:P, 0:P],
                                               ev1[:, s * P:(s + 1) * P],
                                               idF[:]),
                                   wait=bank_free[bank_i2] + [(V, vn)],
                                   inc=True)
                        vn2 = b.do("vector",
                                   lambda e, bi=bank_i2, sl=slot2:
                                   e.tensor_copy(
                                       tr_ev[:, sl * P:(sl + 1) * P],
                                       pb[bi][0:P, 0:P]),
                                   wait=[(T, tn2),
                                         (spt[slot2], ev_slot_busy[slot2])],
                                   inc=True)
                        bank_free[bank_i2] = [(V, vn2)]
                        row0 = tq * QW + sub * P
                        b.dma_start("spt%d" % slot2,
                                    h_vt[2 * dc, row0:row0 + P, :],
                                    tr_ev[:, slot2 * P:slot2 * P + DH],
                                    wait=[(V, vn2)])
                        d2 = b.dma_start("spt%d" % slot2,
                                         h_vt[2 * dc + 1, row0:row0 + P, :],
                                         tr_ev[:, slot2 * P + DH:
                                               (slot2 + 1) * P],
                                         wait=[(V, vn2)])
                        ev_slot_busy[slot2] = d2
                    ev_free = [(T, T.n)]

    b.barrier()
    vq2n = b.do("vector", lambda e: e.tensor_scalar(
        q2r[:], q2r[:], -0.5, None, Alu.mult), inc=True)
    vk2n = b.do("vector", lambda e: e.tensor_scalar(
        k2r[:], k2r[:], -0.5, None, Alu.mult), inc=True)
    chhq2 = b.dch("hq2")
    dq2 = b.dma_start("hq2", h_q2.ap()[:, :], q2r[:], wait=[(V, vq2n)])
    dk2 = b.dma_start("hq2", h_k2.ap()[:, :], k2r[:], wait=[(V, vk2n)])
    msq = b.do("scalar", lambda e: e.activation(mrows[:], k2r[:], Act.Sqrt,
                                                scale=-2.0),
               wait=[(V, vk2n)], inc=True)
    b.barrier()
    for jc in range(TT):
        bank_i = 6 + (jc % 2)
        tn = b.do("tensor",
                  lambda e, j=jc, bi=bank_i:
                  e.transpose(pbank(bi, P, NH),
                              mrows[0:NH, j * P:(j + 1) * P],
                              idF[0:NH, 0:NH]),
                  wait=bank_free[bank_i] + [(Sc, msq)],
                  inc=True)
        vn = b.do("vector",
                  lambda e, j=jc, bi=bank_i:
                  e.tensor_copy(mcol[:, j * NH:(j + 1) * NH],
                                pbank(bi, P, NH)),
                  wait=[(T, tn)], inc=True)
        bank_free[bank_i] = [(V, vn)]

    b.barrier()
    if dev_dump:
        b.dma_start("dbg", h_dbg[0, :, 0:TT * NH], mcol[:])

    esA1.close()
    esA2O = ExitStack()
    obuf = [b.sb(f"obuf{c}", [P, S], F32, scope=esA2O) for c in range(KC)]
    esA2 = ExitStack()
    kaug = b.sb("kaug", [DH + 2, S], F32, scope=esA2)
    qaug = b.sb("qaug", [DH + 2, S], F32, scope=esA2)
    vt = [b.sb(f"vt{i}", [P, DH], F32, scope=esA2) for i in range(2)]
    aug = [b.sb(f"aug{i}", [P, DH + 1], F32, scope=esA2) for i in range(2)]
    ub = [b.sb(f"ub{i}", [P, S], F32, scope=esA2) for i in range(2)]
    tbw = [b.sb(f"tbw{i}", [P, S], F32, scope=esA2) for i in range(2)]
    dtmp = b.sb("dtmp", [1, S], F32, scope=esA2)
    dbc = b.sb("dbc", [DH, QW], F32, scope=esA2)
    osh = b.sb("osh", [DH, QW], F32, scope=esA2)

    vload = b.do("vector", lambda e: e.memset(kaug[DH:DH + 2, :], 1.0),
                 inc=True)
    vload = b.do("vector", lambda e: e.memset(qaug[DH:DH + 2, :], 1.0),
                 inc=True)

    chkq = b.dch("kqld")
    chvt = [b.dch("vt0"), b.dch("vt1")]
    chden = b.dch("den")
    chdbc = b.dch("dbcl")
    chosh = b.dch("osh")
    kv_free = [[(V, 0)], [(V, 0)]]
    aug_free = [[(T, 0)], [(T, 0)]]
    ub_free = [[(Sc, 0)], [(Sc, 0)]]
    tb_free = [[(T, 0)], [(T, 0)]]
    kq_cons = [(T, 0)]
    dtmp_free = {q: [(chden, 0)] for q in range(NQ)}
    dbc_free = [(V, 0)]
    osh_free = [(chosh, 0)]

    for h in range(NH):
        k1 = b.dma_start("kqld", kaug[0:DH, :], h_kf[h, :, :], wait=kq_cons)
        k2_ = b.dma_start("kqld", kaug[DH + 1:DH + 2, :],
                          h_k2.ap()[h:h + 1, :],
                          wait=[(chhq2, dk2), (V, vload)])
        q1 = b.dma_start("kqld", qaug[0:DH, :], h_qf[h, :, :], wait=kq_cons)
        q2_ = b.dma_start("kqld", qaug[DH:DH + 1, :],
                          h_q2.ap()[h:h + 1, :],
                          wait=[(chhq2, dq2), (V, vload)])
        kqn = q2_
        pv_started = False

        def a2_prep(jc):
            sl = jc % 2
            vn_ld = b.dma_start("vt%d" % sl, vt[sl][:],
                                h_vt[h, jc * P:(jc + 1) * P, :],
                                wait=kv_free[sl])
            av = b.do("vector",
                      lambda e, s=sl, j=jc, h=h: e.tensor_scalar(
                          aug[s][:, 0:DH], vt[s][:],
                          mcol[:, j * NH + h:j * NH + h + 1], None,
                          Alu.mult),
                      wait=[(chvt[sl], vn_ld), (V, vload)] + aug_free[sl],
                      inc=True)
            av2 = b.do("vector",
                       lambda e, s=sl, j=jc, h=h: e.tensor_copy(
                           aug[s][:, DH:DH + 1],
                           mcol[:, j * NH + h:j * NH + h + 1]),
                       inc=True)
            kv_free[sl] = [(V, av2)]
            return av2

        def a2_score(jc):
            ts = 0
            for tq in range(NQ):
                ts = b.do("tensor",
                          lambda e, j=jc, q=tq: e.matmul(
                              ps4a[:, q * QW:(q + 1) * QW],
                              kaug[:, j * P:(j + 1) * P],
                              qaug[:, q * QW:(q + 1) * QW],
                              start=True, stop=True),
                          wait=(bank_free[0] + [(chkq, kqn)])
                          if tq == 0 else (),
                          inc=(tq == NQ - 1))
            return ts

        av2_cur = a2_prep(0)
        ts_cur = a2_score(0)
        for jc in range(TT):
            sl = jc % 2
            s1 = b.do("scalar",
                      lambda e, s=sl: e.activation(
                          ub[s][:], ps4a[:], Act.Sqrt, scale=-2.0),
                      wait=[(T, ts_cur)] + ub_free[sl], inc=True)
            bank_free[0] = [(Sc, s1)]
            if jc + 1 < TT:
                av2_next = a2_prep(jc + 1)
                ts_next = a2_score(jc + 1)
            s2 = b.do("scalar",
                      lambda e, s=sl: e.activation(
                          ub[s][:], ub[s][:], Act.Ln, bias=cEPS[:, 0:1]),
                      wait=[(V, n_const)], inc=True)
            vr = b.do("scalar",
                      lambda e, s=sl: e.activation(
                          tbw[s][:], ub[s][:], Act.Exp, scale=-2.0),
                      wait=tb_free[sl], inc=True)
            ub_free[sl] = [(Sc, vr)]
            pvw = [(Sc, vr), (V, av2_cur)]
            if not pv_started:
                pvw = pvw + bank_free[4] + bank_free[5] + bank_free[6] \
                    + bank_free[7]
            tp = 0
            for tq in range(NQ):
                tp = b.do("tensor",
                          lambda e, s=sl, q=tq, first=(not pv_started),
                          last=(jc == TT - 1): e.matmul(
                              ps4b[0:DH + 1, q * QW:(q + 1) * QW],
                              aug[s][:], tbw[s][:, q * QW:(q + 1) * QW],
                              start=first, stop=last),
                          wait=pvw if tq == 0 else (),
                          inc=(tq == NQ - 1))
            pv_started = True
            tb_free[sl] = [(T, tp)]
            aug_free[sl] = [(T, tp)]
            if jc + 1 < TT:
                av2_cur, ts_cur = av2_next, ts_next
        kq_cons = [(T, tp)]
        bank_ready[4] = (T, tp)
        hc, half = h // 2, h % 2
        for tq in range(NQ):
            vd = b.do("vector",
                      lambda e, q=tq: e.reciprocal(
                          dtmp[:, q * QW:(q + 1) * QW],
                          ps4b[DH:DH + 1, q * QW:(q + 1) * QW]),
                      wait=[bank_ready[4]] + dtmp_free[tq], inc=True)
            dwr = b.dma_start("den",
                              h_den.ap()[h:h + 1, tq * QW:(tq + 1) * QW],
                              dtmp[:, tq * QW:(tq + 1) * QW],
                              wait=[(V, vd)])
            dtmp_free[tq] = [(chden, dwr)]
            dbn = b.dma_start("dbcl", dbc[:],
                              bass.AP(h_den, h * S + tq * QW,
                                      [[0, DH], [1, QW]]),
                              wait=[(chden, dwr)] + dbc_free)
            if half == 0:
                vo = b.do("vector",
                          lambda e, q=tq, c=hc: e.tensor_mul(
                              obuf[c][0:DH, q * QW:(q + 1) * QW],
                              ps4b[0:DH, q * QW:(q + 1) * QW], dbc[:]),
                          wait=[(chdbc, dbn)], inc=True)
                dbc_free = [(V, vo)]
            else:
                vo = b.do("vector",
                          lambda e, q=tq: e.tensor_mul(
                              osh[:], ps4b[0:DH, q * QW:(q + 1) * QW],
                              dbc[:]),
                          wait=[(chdbc, dbn)] + osh_free, inc=True)
                dbc_free = [(V, vo)]
                dsh = b.dma_start("osh",
                                  obuf[hc][DH:P, tq * QW:(tq + 1) * QW],
                                  osh[:], wait=[(V, vo)])
                osh_free = [(chosh, dsh)]
        for i in (4, 5, 6, 7):
            bank_free[i] = [(V, vo)]

    b.barrier()

    esA2.close()
    esA3 = ExitStack()
    wo_sl = b.sb("wo_sl", [P, 2 * P], F32, scope=esA3)
    wev = b.sb("wev", [P, QW], F32, scope=esA3)
    wo_free = [[(T, 0)], [(T, 0)]]
    wev_free = [(T, 0)]

    for tq in range(NQ):
        for dc in range(KC):
            bank_i = dc % 2
            tn = 0
            for ec in range(KC):
                slot = ec % 2
                wl = b.dma_start("wo%d" % slot,
                                 wo_sl[:, slot * P:(slot + 1) * P],
                                 h_wT["wo0"][ec, :, dc * P:(dc + 1) * P],
                                 wait=wo_free[slot])
                tn = b.do("tensor",
                          lambda e, ecc=ec, s=slot, bi=bank_i, q=tq: e.matmul(
                              pb[bi][:, 0:QW],
                              wo_sl[:, s * P:(s + 1) * P],
                              obuf[ecc][:, q * QW:(q + 1) * QW],
                              start=(ecc == 0), stop=(ecc == KC - 1)),
                          wait=(bank_free[bank_i] +
                                [(b.dch("wo%d" % slot), wl)])
                          if ec == 0 else [(b.dch("wo%d" % slot), wl)],
                          inc=True)
                wo_free[slot] = [(T, tn)]
            vn = b.do("vector",
                      lambda e, bi=bank_i, c0=24 + dc: e.tensor_scalar(
                          wev[:], pb[bi][:, 0:QW],
                          cols[:, c0:c0 + 1], None, Alu.add),
                      wait=[(T, tn)] + wev_free, inc=True)
            bank_free[bank_i] = [(V, vn)]
            for sub in range(4):
                t_idx = tq * 4 + sub
                bank_i2 = 6 + (sub % 2)
                tn2 = b.do("tensor",
                           lambda e, s=sub, bi=bank_i2: e.transpose(
                               pb[bi][0:P, 0:P], wev[:, s * P:(s + 1) * P],
                               idF[:]),
                           wait=bank_free[bank_i2] + [(V, vn)], inc=True)
                vn2 = b.do("vector",
                           lambda e, ti=t_idx, d=dc, bi=bank_i2:
                           e.tensor_copy(xs[ti][:, d * P:(d + 1) * P],
                                         pb[bi][0:P, 0:P]),
                           wait=[(T, tn2)], inc=True)
                bank_free[bank_i2] = [(V, vn2)]
            wev_free = [(T, T.n)]

    def emit_ln_tile(tile_ap, add_src, add_wait):
        wl = list(add_wait)
        if add_src is not None:
            va = b.do("vector",
                      lambda e: e.tensor_add(tile_ap, tile_ap, add_src),
                      wait=wl, inc=True)
            wl = [(V, va)]
        vs = b.do("vector", lambda e: e.tensor_reduce(
            stat[:, 0:1], tile_ap, AxX, Alu.add), wait=wl, inc=True)
        ss = b.do("scalar", lambda e: e.activation(
            lnsq[:], tile_ap, Act.Square, accum_out=stat[:, 1:2]),
            wait=wl, inc=True)
        b.do("vector", lambda e: e.tensor_scalar(
            stat[:, 2:3], stat[:, 0:1], 1.0 / DM, None, Alu.mult),
            wait=[(V, vs)], inc=True)
        b.do("vector", lambda e: e.tensor_mul(
            stat[:, 3:4], stat[:, 2:3], stat[:, 2:3]), inc=True)
        v3 = b.do("vector", lambda e: e.scalar_tensor_tensor(
            stat[:, 4:5], stat[:, 1:2], 1.0 / DM, stat[:, 3:4],
            Alu.mult, Alu.subtract), wait=[(Sc, ss)], inc=True)
        s2 = b.do("scalar", lambda e: e.activation(
            stat[:, 5:6], stat[:, 4:5], Act.Sqrt, bias=cEPS[:, 1:2]),
            wait=[(V, v3)], inc=True)
        vr = b.do("vector", lambda e: e.reciprocal(
            stat[:, 6:7], stat[:, 5:6]), wait=[(Sc, s2)], inc=True)
        v4 = b.do("vector", lambda e: e.tensor_scalar(
            tile_ap, tile_ap, stat[:, 2:3], stat[:, 6:7],
            Alu.subtract, Alu.mult), inc=True)
        return v4

    chxa = b.dch("xadd")
    for t in range(TT):
        xl = b.dma_start("xadd", xadd[:], x_in.ap()[t * P:(t + 1) * P, :],
                         wait=[(V, V.n)])
        emit_ln_tile(xs[t][:], xadd[:], [(chxa, xl)])

    b.barrier()

    if dev_tap == "x1":
        chy = b.dch("yst")
        for t in range(TT):
            b.dma_start("yst", y_out.ap()[t * P:(t + 1) * P, :], xs[t][:],
                        wait=[(V, V.n)])
        b.do("sync", lambda e: e.nop(), wait=[(chy, chy.n)])
        b.finalize()
        return nc

    esA3.close()
    esA2O.close()
    esB = ExitStack()
    w8T = b.sb("w8T", [P, KC * DM], BF16, scope=esB)
    biasb = b.sb("biasb", [P, DM], F32, scope=esB)
    wsc = b.sb("wsc", [P, 1], F32, scope=esB)
    cB = b.sb("cB", [P, 4], F32, scope=esB)
    u8a = b.sb("u8a", [P, DM], U8, scope=esB)
    u8b = b.sb("u8b", [P, DM], U8, scope=esB)
    qb = [b.sb(f"qb{i}", [P, DM], F32, scope=esB) for i in range(6)]
    qcs = b.sb("qcs", [P, 8], F32, scope=esB)
    xscs = b.sb("xscs", [P, TT], F32, scope=esB)
    qtb = b.sb("qtb", [P, DM], BF16, scope=esB)
    qft = b.sb("qft", [P, DM], BF16, scope=esB)
    qdmp = b.sb("qdmp", [P, DM], F32, scope=esB) if dev_dump else None

    vcb = b.do("vector", lambda e: e.memset(cB[:, 0:1], -10.0 * LN2))
    vcb = b.do("vector", lambda e: e.memset(cB[:, 1:2], 9.0 * LN2))
    vcb = b.do("vector", lambda e: e.memset(cB[:, 2:3], -9.0 * LN2), inc=True)
    vcb = b.do("vector", lambda e: e.memset(cB[:, 3:4], 448.0), inc=True)
    vcb = b.do("vector", lambda e: e.reciprocal(cB[:, 3:4], cB[:, 3:4]),
               wait=[(V, vcb)], inc=True)

    def emit_pow2(eng, dst, src, tc_, tf_, wait):
        ch = b.chan[eng]
        g = 0
        first = True
        for s in (8.0, 4.0, 2.0, 1.0):
            g = b.do(eng, lambda e, s=s: e.tensor_scalar(
                tc_, src, s, s, Alu.is_ge, Alu.mult),
                wait=wait if first else [(ch, g)], inc=True)
            g = b.do(eng, lambda e: e.tensor_sub(src, src, tc_),
                     wait=[(ch, g)], inc=True)
            g = b.do(eng, lambda e, s=s: e.tensor_scalar(
                tf_, tc_, (2.0 ** s - 1.0) / s, 1.0, Alu.mult, Alu.add),
                wait=[(ch, g)], inc=True)
            if first:
                g = b.do(eng, lambda e: e.tensor_copy(dst, tf_),
                         wait=[(ch, g)], inc=True)
            else:
                g = b.do(eng, lambda e: e.tensor_mul(dst, dst, tf_),
                         wait=[(ch, g)], inc=True)
            first = False
        return g

    chw8 = b.dch("w8ld")
    chtr8 = b.dch("w8tr")
    b.dma_start("w8ld", biasb[:], bass.AP(W["L0_ffn_bias"], 0, [[0, P], [1, DM]]))
    b.dma_start("w8ld", wsc[:], bass.AP(W["L0_ffn_weight_scale"], 0,
                                        [[0, P], [1, 1]]))
    w8_cons = [(V, 0)]
    tr_busy = [(chtr8, 0)]
    for nc2 in range(KC):
        ldn = b.dma_start("w8ld", u8a[:],
                          W["L0_ffn_weight"].ap()[nc2 * P:(nc2 + 1) * P, :],
                          wait=w8_cons)
        v = b.do("vector", lambda e: e.tensor_scalar(
            u8b[:], u8a[:], 3, 15, Alu.logical_shift_right, Alu.bitwise_and),
            wait=[(chw8, ldn)], inc=True)
        v = b.do("vector", lambda e: e.tensor_copy(qb[0][:], u8b[:]),
                 wait=[(V, v)], inc=True)
        v = b.do("vector", lambda e: e.tensor_scalar(
            u8b[:], u8a[:], 7, None, Alu.bitwise_and), wait=[(V, v)], inc=True)
        v = b.do("vector", lambda e: e.tensor_copy(qb[1][:], u8b[:]),
                 wait=[(V, v)], inc=True)
        v = b.do("vector", lambda e: e.tensor_scalar(
            u8b[:], u8a[:], 7, None, Alu.logical_shift_right),
            wait=[(V, v)], inc=True)
        vdec = b.do("vector", lambda e: e.tensor_copy(qb[2][:], u8b[:]),
                    wait=[(V, v)], inc=True)
        w8_cons = [(V, vdec)]
        v = b.do("vector", lambda e: e.tensor_scalar(
            qb[3][:], qb[0][:], 1.0, 8.0, Alu.is_ge, Alu.mult),
            wait=[(V, vdec)], inc=True)
        v = b.do("vector", lambda e: e.tensor_add(qb[1][:], qb[1][:], qb[3][:]),
                 wait=[(V, v)], inc=True)
        vbase = v
        vemax = b.do("vector", lambda e: e.tensor_scalar(
            qb[0][:], qb[0][:], 1.0, None, Alu.max), wait=[(V, v)], inc=True)
        vpw = emit_pow2("vector", qb[4][:], qb[0][:], qb[3][:], qb[5][:],
                        [(V, vemax)])
        vpw = b.do("vector", lambda e: e.tensor_scalar(
            qb[4][:], qb[4][:], 1.0 / 1024.0, None, Alu.mult),
            wait=[(V, vpw)], inc=True)
        vsgn = b.do("vector", lambda e: e.tensor_scalar(
            qb[2][:], qb[2][:], -2.0, 1.0, Alu.mult, Alu.add),
            wait=[(V, vemax)], inc=True)
        v = b.do("vector", lambda e: e.tensor_mul(qb[1][:], qb[1][:], qb[4][:]),
                 wait=[(V, vpw), (V, vsgn)], inc=True)
        v = b.do("vector", lambda e: e.tensor_mul(qb[1][:], qb[1][:], qb[2][:]),
                 wait=[(V, v)], inc=True)
        vbf = b.do("vector", lambda e: e.tensor_copy(qtb[:], qb[1][:]),
                   wait=[(V, v)] + tr_busy, inc=True)
        for kc in range(KC):
            dn = b.dma_start("w8tr",
                             w8T[:, kc * DM + nc2 * P:kc * DM + (nc2 + 1) * P],
                             qtb[:, kc * P:(kc + 1) * P],
                             wait=[(V, vbf)] + tr_busy, transpose=True)
        tr_busy = [(chtr8, dn)]

    b.barrier()

    chqtr = b.dch("qtr")
    INVLN2 = 1.0 / LN2
    qft_cons = [(T, 0)]
    qtb_free = [(chqtr, 0)]
    import os as _os
    _BT = int(_os.environ.get('BTILES', TT))
    for t in range(_BT):
        tile = xs[t][:]
        v = b.do("vector", lambda e, tile=tile: e.tensor_reduce(
            qcs[:, 0:1], tile, AxX, Alu.max, apply_absolute_value=True),
            wait=[(V, V.n), (Sc, Sc.n), (T, T.n)], inc=True)
        v = b.do("vector", lambda e: e.tensor_scalar(
            qcs[:, 1:2], qcs[:, 0:1], 1e-12, None, Alu.max),
            wait=[(V, v)], inc=True)
        vamax = v
        v = b.do("vector", lambda e: e.reciprocal(qcs[:, 2:3], qcs[:, 1:2]),
                 wait=[(V, v)], inc=True)
        v = b.do("vector", lambda e: e.tensor_scalar(
            qcs[:, 3:4], qcs[:, 2:3], 448.0, None, Alu.mult),
            wait=[(V, v)], inc=True)
        vysc = v
        b.do("vector", lambda e, t=t: e.tensor_scalar(
            xscs[:, t:t + 1], qcs[:, 1:2], cB[:, 3:4], None, Alu.mult),
            wait=[(V, vamax)], inc=True)
        vy = b.do("vector", lambda e, tile=tile: e.tensor_scalar(
            qb[0][:], tile, qcs[:, 3:4], None, Alu.mult),
            wait=[(V, vysc)] + ([(b.dch("dbg"), b.dch("dbg").n)]
                                if dev_dump else []), inc=True)
        if dev_dump and t == 0:
            b.dma_start("dbg", h_dbg[4, :, 0:DM], qb[0][:], wait=[(V, vy)])
        su = b.do("scalar", lambda e: e.activation(
            qb[1][:], qb[0][:], Act.Abs), wait=[(V, vy)], inc=True)
        vu = b.do("vector", lambda e: e.tensor_scalar(
            qb[1][:], qb[1][:], 2.0 ** -30, None, Alu.max),
            wait=[(Sc, su)], inc=True)
        sl1 = b.do("scalar", lambda e: e.activation(
            qb[1][:], qb[1][:], Act.Ln), wait=[(V, vu)], inc=True)
        g = b.do("vector", lambda e: e.tensor_scalar(
            qb[2][:], qb[1][:], INVLN2, 6.0, Alu.mult, Alu.add),
            wait=[(Sc, sl1)] + ([(b.dch("dbg"), b.dch("dbg").n)]
                                if dev_dump else []), inc=True)
        g = b.do("vector", lambda e: e.tensor_scalar(
            qb[2][:], qb[2][:], 0.0, 14.99, Alu.max, Alu.min),
            wait=[(G, g)], inc=True)
        g = b.do("vector", lambda e: e.tensor_copy(qb[3][:], qb[2][:]),
                 wait=[(G, g)], inc=True)
        for s in (8.0, 4.0, 2.0, 1.0):
            g = b.do("vector", lambda e, s=s: e.tensor_scalar(
                qb[4][:], qb[3][:], s, s, Alu.is_ge, Alu.mult),
                wait=[(G, g)] + ([(b.dch("dbg"), b.dch("dbg").n)]
                                 if dev_dump else []), inc=True)
            g = b.do("vector", lambda e: e.tensor_sub(
                qb[3][:], qb[3][:], qb[4][:]), wait=[(G, g)], inc=True)
        gfl = b.do("vector", lambda e: e.tensor_sub(
            qb[2][:], qb[2][:], qb[3][:]), wait=[(G, g)], inc=True)
        if dev_dump and t == 0:
            b.dma_start("dbg", h_dbg[5, :, 0:DM], qb[2][:], wait=[(G, gfl)])
        g = b.do("vector", lambda e: e.tensor_scalar(
            qb[3][:], qb[2][:], -1.0, 14.0, Alu.mult, Alu.add),
            wait=[(G, gfl)], inc=True)
        gpw = emit_pow2("vector", qb[4][:], qb[3][:], qb[1][:], qb[5][:],
                        [(G, g)])
        gpw = b.do("vector", lambda e: e.tensor_scalar(
            qb[3][:], qb[4][:], 1.0 / 32.0, None, Alu.mult),
            wait=[(G, gpw)], inc=True)
        vz = b.do("vector", lambda e: e.tensor_mul(
            qb[4][:], qb[0][:], qb[3][:]), wait=[(G, gpw)], inc=True)
        if dev_dump and t == 0:
            b.dma_start("dbg", h_dbg[6, :, 0:DM], qb[4][:], wait=[(V, vz)])
        sw_ = b.do("scalar", lambda e: e.activation(
            qb[5][:], qb[4][:], Act.Abs), wait=[(V, vz)], inc=True)
        vw = b.do("vector", lambda e: e.tensor_scalar(
            qb[5][:], qb[5][:], 0.5, None, Alu.add),
            wait=[(Sc, sw_)], inc=True)
        g = b.do("vector", lambda e: e.tensor_copy(qb[0][:], qb[5][:]),
                 wait=[(V, vw)] + ([(b.dch("dbg"), b.dch("dbg").n)]
                                   if dev_dump else []), inc=True)
        for s in (16.0, 8.0, 4.0, 2.0, 1.0):
            g = b.do("vector", lambda e, s=s: e.tensor_scalar(
                qb[1][:], qb[0][:], s, s, Alu.is_ge, Alu.mult),
                wait=[(G, g)], inc=True)
            g = b.do("vector", lambda e: e.tensor_sub(
                qb[0][:], qb[0][:], qb[1][:]), wait=[(G, g)], inc=True)
        g3 = b.do("vector", lambda e: e.tensor_sub(
            qb[5][:], qb[5][:], qb[0][:]), wait=[(G, g)], inc=True)
        if dev_dump and t == 0:
            b.dma_start("dbg", h_dbg[7, :, 0:DM], qb[5][:], wait=[(G, g3)])
        ssg = b.do("scalar", lambda e: e.activation(
            qb[1][:], qb[4][:], Act.Sign), wait=[(V, vz), (G, g3)], inc=True)
        gpi = emit_pow2("vector", qb[3][:], qb[2][:], qb[0][:], qb[4][:],
                        [(Sc, ssg), (G, g3)])
        gpi = b.do("vector", lambda e: e.tensor_scalar(
            qb[3][:], qb[3][:], 1.0 / 512.0, None, Alu.mult),
            wait=[(G, gpi)], inc=True)
        vq = b.do("vector", lambda e: e.tensor_mul(
            qb[5][:], qb[5][:], qb[1][:]),
            wait=[(G, g3), (Sc, ssg)] + ([(b.dch("dbg"), b.dch("dbg").n)]
                                         if dev_dump else []), inc=True)
        vq = b.do("vector", lambda e: e.tensor_mul(
            qb[5][:], qb[5][:], qb[3][:]), wait=[(V, vq), (G, gpi)],
            inc=True)
        vqc = b.do("vector", lambda e: e.tensor_copy(qtb[:], qb[5][:]),
                   wait=[(V, vq)] + qtb_free, inc=True)
        if dev_dump and t == 0:
            vdq = b.do("vector", lambda e: e.tensor_copy(qdmp[:], qtb[:]),
                       wait=[(V, vqc)], inc=True)
            b.dma_start("dbg", h_dbg[1, :, 0:DM], qdmp[:], wait=[(V, vdq)])
            b.dma_start("dbg", h_dbg[2, :, 0:DM], tile, wait=[(V, vqc)])
        trn = 0
        for kc in range(KC):
            trn = b.dma_start("qtr", qft[:, kc * P:(kc + 1) * P],
                              qtb[:, kc * P:(kc + 1) * P],
                              wait=[(V, vqc)] + qft_cons, transpose=True)
        qtb_free = [(b.dch("qtr"), trn)]
        lnadds = []
        for n2 in range(2):
            bank_i = n2
            tn = 0
            for kc in range(KC):
                tn = b.do("tensor",
                          lambda e, kc=kc, n2=n2, bi=bank_i: e.matmul(
                              pbank(bi),
                              qft[:, kc * P:(kc + 1) * P],
                              w8T[:, kc * DM + n2 * QW:kc * DM + (n2 + 1) * QW],
                              start=(kc == 0), stop=(kc == KC - 1)),
                          wait=(bank_free[bank_i] + [(b.dch("qtr"), trn),
                                                     (chtr8, chtr8.n)])
                          if kc == 0 else (),
                          inc=(kc == KC - 1))
            ve = b.do("vector",
                      lambda e, t=t, bi=bank_i: e.tensor_scalar(
                          qb[0][:, 0:QW], pbank(bi),
                          xscs[:, t:t + 1], wsc[:, 0:1],
                          Alu.mult, Alu.mult),
                      wait=[(T, tn), (chw8, chw8.n)], inc=True)
            bank_free[bank_i] = [(V, ve)]
            ve = b.do("vector", lambda e, n2=n2: e.tensor_add(
                qb[0][:, 0:QW], qb[0][:, 0:QW],
                biasb[:, n2 * QW:(n2 + 1) * QW]), wait=[(V, ve)], inc=True)
            if dev_dump and t == 0:
                vdq2 = b.do("vector",
                            lambda e, n2=n2: e.tensor_copy(
                                qdmp[:, n2 * QW:(n2 + 1) * QW],
                                qb[0][:, 0:QW]),
                            wait=[(V, ve),
                                  (b.dch("dbg"), b.dch("dbg").n)], inc=True)
                b.dma_start("dbg", h_dbg[3, :, n2 * QW:(n2 + 1) * QW],
                            qdmp[:, n2 * QW:(n2 + 1) * QW],
                            wait=[(V, vdq2)])
            va = b.do("vector", lambda e, t=t, n2=n2: e.tensor_add(
                xs[t][:, n2 * QW:(n2 + 1) * QW],
                xs[t][:, n2 * QW:(n2 + 1) * QW], qb[0][:, 0:QW]),
                wait=[(V, ve), (b.dch("dbg"), b.dch("dbg").n)]
                if dev_dump and t == 0 else [(V, ve)], inc=True)
            lnadds.append(va)
        qft_cons = [(T, T.n)]
        emit_ln_tile(xs[t][:], None, [(V, lnadds[-1])])

    b.barrier()

    if dev_tap == "x2":
        chy = b.dch("yst")
        for t in range(TT):
            b.dma_start("yst", y_out.ap()[t * P:(t + 1) * P, :], xs[t][:],
                        wait=[(V, V.n)])
        b.do("sync", lambda e: e.nop(), wait=[(chy, chy.n)])
        b.finalize()
        return nc

    esB.close()

    for t in range(TT):
        for kc in range(KC):
            bank_i = 6 + (kc % 2)
            slot = kc % 2
            tn = b.do("tensor",
                      lambda e, t=t, kc=kc, bi=bank_i: e.transpose(
                          pbank(bi, P, P), xs[t][:, kc * P:(kc + 1) * P],
                          idF[:]),
                      wait=bank_free[bank_i] + [(V, V.n)], inc=True)
            vn = b.do("vector",
                      lambda e, bi=bank_i, sl=slot: e.tensor_copy(
                          tr_ev[:, sl * P:(sl + 1) * P], pbank(bi, P, P)),
                      wait=[(T, tn), (spt[slot], ev_slot_busy[slot])],
                      inc=True)
            bank_free[bank_i] = [(V, vn)]
            dn = b.dma_start("spt%d" % slot, h_x2f[kc, :, t * P:(t + 1) * P],
                             tr_ev[:, slot * P:(slot + 1) * P],
                             wait=[(V, vn)])
            ev_slot_busy[slot] = dn

    b.barrier()

    esCmid = ExitStack()
    evc = b.sb("evc", [P, QW], F32, scope=esCmid)
    sqc = b.sb("sqc", [P, QW], F32, scope=esCmid)
    evcB = b.sb("evcB", [P, QW], F32, scope=esCmid)
    sqcB = b.sb("sqcB", [P, QW], F32, scope=esCmid)
    rtmpB = b.sb("rtmpB", [1, QW], F32, scope=esCmid)
    srows = b.sb("srows", [1, 2 * S], F32, scope=esCmid)
    rtmp = b.sb("rtmp", [1, QW], F32, scope=esCmid)
    colsC = b.sb("colsC", [P, 64], F32, scope=esCmid)
    mbc = b.sb("mbc", [P, QW], F32, scope=esCmid)
    rbc = b.sb("rbc", [P, QW], F32, scope=esCmid)
    qpc = b.sb("qpc", [P, KC], F32, scope=esCmid)
    esC1 = ExitStack()
    xf = [b.sb(f"xf{kc}", [P, QW], F32, scope=esC1) for kc in range(KC)]
    wtc = b.sb("wtc", [P, 2 * DM], F32, scope=esC1)

    chc = b.dch("c1")
    for i, nm in enumerate(("L1_attn_bq", "L1_attn_gq", "L1_attn_betaq",
                            "L1_attn_bk", "L1_attn_gk", "L1_attn_betak",
                            "L1_attn_ga", "L1_attn_ba")):
        for c in range(KC):
            b.dma_start("c1", colsC[:, 8 * i + c:8 * i + c + 1],
                        bass.AP(W[nm], c * P, [[1, P], [1, 1]]))
    nc1 = chc.n
    vz0 = b.do("vector", lambda e: e.memset(qpc[:], 0.0), inc=True)

    chxf = b.dch("xf")
    chwc = [b.dch("wc0"), b.dch("wc1")]
    wc_free = [[(T, 0)], [(T, 0)]]
    evc_free = [(Sc, 0), (b.dch("spQK2"), 0)]
    sq_free2 = [(T, 0)]
    rt_free = [(b.dch("q2s"), 0)]
    chsp2 = b.dch("spQK2")

    evsl = (evc, evcB)
    sqsl = (sqc, sqcB)
    rtsl = (rtmp, rtmpB)

    def c_proj_fmajor(wname, hraw, bcol0):
        vsz = b.do("vector", lambda e: e.memset(srows[:], 0.0),
                   wait=[(V, V.n), (T, T.n)], inc=True)
        srows_dep = [(V, vsz)]
        evfL = {0: list(evc_free), 1: list(evc_free)}
        sqfL = {0: list(sq_free2), 1: list(sq_free2)}
        rtfL = {0: list(rt_free), 1: list(rt_free)}
        for tt in range(NQ):
            xw = (T, T.n)
            xl = 0
            for kc in range(KC):
                xl = b.dma_start("xf", xf[kc][:, 0:QW],
                                 h_x2f[kc, :, tt * QW:(tt + 1) * QW],
                                 wait=[xw])
            for dc in range(KC):
                slot = dc % 2
                wld = 0
                for kc in range(KC):
                    wld = b.dma_start(
                        "wc%d" % slot,
                        wtc[:, slot * DM + kc * P:slot * DM + (kc + 1) * P],
                        W[wname].ap()[kc * P:(kc + 1) * P,
                                      dc * P:(dc + 1) * P],
                        wait=wc_free[slot])
                bank_i = dc % 2
                tn = 0
                for kc in range(KC):
                    tn = b.do("tensor",
                              lambda e, kc=kc, s=slot, bi=bank_i: e.matmul(
                                  pbank(bi),
                                  wtc[:, s * DM + kc * P:
                                      s * DM + (kc + 1) * P],
                                  xf[kc][:, 0:QW],
                                  start=(kc == 0), stop=(kc == KC - 1)),
                              wait=(bank_free[bank_i] +
                                    [(chwc[slot], wld), (chxf, xl)])
                              if kc == 0 else (), inc=(kc == KC - 1))
                wc_free[slot] = [(T, tn)]
                ev_ = evsl[slot]
                sq_ = sqsl[slot]
                rt_ = rtsl[slot]
                stb = 5 - slot
                vn = b.do("vector",
                          lambda e, bi=bank_i, c0=bcol0 + dc, ev_=ev_:
                          e.tensor_scalar(
                              ev_[:], pbank(bi), colsC[:, c0:c0 + 1], None,
                              Alu.add),
                          wait=[(T, tn), (chc, nc1)] + evfL[slot], inc=True)
                bank_free[bank_i] = [(V, vn)]
                spch = "spQK2" if slot == 0 else "spQK2b"
                d1 = b.dma_start(spch, hraw[dc, :, tt * QW:(tt + 1) * QW],
                                 ev_[:], wait=[(V, vn)])
                sn = b.do("scalar",
                          lambda e, ev_=ev_, sq_=sq_: e.activation(
                              sq_[:], ev_[:], Act.Square),
                          wait=[(V, vn)] + sqfL[slot], inc=True)
                tq2 = 0
                for (src_t, row) in ((ev_, 0), (sq_, 1)):
                    tq2 = b.do("tensor",
                               lambda e, s=src_t, stb=stb: e.matmul(
                                   pbank(stb, 1), cONE[:, 0:1], s[:],
                                   start=True, stop=True),
                               wait=([(V, vn)] if row == 0 else [(Sc, sn)])
                               + bank_free[stb], inc=True)
                    vq2 = b.do("vector",
                               lambda e, rt_=rt_, stb=stb: e.tensor_copy(
                                   rt_[:], pbank(stb, 1)),
                               wait=[(T, tq2)] + rtfL[slot], inc=True)
                    bank_free[stb] = [(V, vq2)]
                    va = b.do("vector",
                              lambda e, row=row, tt=tt, rt_=rt_:
                              e.tensor_add(
                                  srows[0:1, row * S + tt * QW:
                                        row * S + (tt + 1) * QW],
                                  srows[0:1, row * S + tt * QW:
                                        row * S + (tt + 1) * QW],
                                  rt_[:]),
                              wait=srows_dep, inc=True)
                    srows_dep = [(V, va)]
                    rtfL[slot] = [(V, va)]
                evfL[slot] = [(Sc, sn), (b.dch(spch), d1), (T, tq2)]
                sqfL[slot] = [(T, T.n)]
        evc_free[:] = evfL[0] + evfL[1]
        sq_free2[:] = sqfL[0] + sqfL[1]
        rt_free[:] = rtfL[0] + rtfL[1]
        return srows_dep

    def c_rows_stats(srows_dep):
        v1 = b.do("vector", lambda e: e.tensor_scalar(
            srows[0:1, 0:S], srows[0:1, 0:S], 1.0 / DM, None, Alu.mult),
            wait=srows_dep, inc=True)
        vlast = b.do("vector", lambda e: e.tensor_scalar(
            srows[0:1, S:2 * S], srows[0:1, S:2 * S], 1.0 / DM, None,
            Alu.mult), wait=srows_dep, inc=True)
        for tt in range(NQ):
            va = b.do("vector", lambda e, tt=tt: e.tensor_mul(
                rtmp[:], srows[0:1, tt * QW:(tt + 1) * QW],
                srows[0:1, tt * QW:(tt + 1) * QW]),
                wait=[(V, v1), (V, vlast)] + rt_free, inc=True)
            vlast = b.do("vector", lambda e, tt=tt: e.tensor_sub(
                srows[0:1, S + tt * QW:S + (tt + 1) * QW],
                srows[0:1, S + tt * QW:S + (tt + 1) * QW], rtmp[:]),
                wait=[(V, va)], inc=True)
        s1 = b.do("scalar", lambda e: e.activation(
            srows[0:1, S:2 * S], srows[0:1, S:2 * S], Act.Sqrt,
            bias=cEPS[0:1, 1:2]), wait=[(V, vlast)], inc=True)
        v5 = b.do("vector", lambda e: e.reciprocal(
            srows[0:1, S:2 * S], srows[0:1, S:2 * S]),
            wait=[(Sc, s1)], inc=True)
        d1 = b.dma_start("q2s", h_rows.ap()[0:1, :], srows[0:1, 0:S],
                         wait=[(V, v5)])
        d2 = b.dma_start("q2s", h_rows.ap()[1:2, :], srows[0:1, S:2 * S],
                         wait=[(V, v5)])
        rt_free[:] = [(V, vlast)]
        return d2

    def c_apply_fmajor(hraw, hout, gcol0, bcol0b, probe=False):
        chap = b.dch("c_ap")
        ap_free = [(V, 0)]
        for tt in range(NQ):
            dm_ = b.dma_start("c_ap", mbc[:],
                              bass.AP(h_rows, tt * QW, [[0, P], [1, QW]]),
                              wait=ap_free)
            dr_ = b.dma_start("c_ap", rbc[:],
                              bass.AP(h_rows, S + tt * QW, [[0, P], [1, QW]]),
                              wait=ap_free)
            for dc in range(KC):
                ld = b.dma_start("xf", xf[dc][:, 0:QW],
                                 hraw[dc, :, tt * QW:(tt + 1) * QW],
                                 wait=[(V, V.n)])
                v1 = b.do("vector",
                          lambda e, dc=dc: e.tensor_sub(
                              evc[:], xf[dc][:, 0:QW], mbc[:]),
                          wait=[(chxf, ld), (b.dch("c_ap"), dr_)] + evc_free,
                          inc=True)
                v2 = b.do("vector",
                          lambda e, c0=gcol0 + dc: e.scalar_tensor_tensor(
                              evc[:], evc[:], colsC[:, c0:c0 + 1], rbc[:],
                              Alu.mult, Alu.mult),
                          wait=[(V, v1)], inc=True)
                v3 = b.do("vector",
                          lambda e, c0=bcol0b + dc: e.tensor_scalar(
                              evc[:], evc[:], colsC[:, c0:c0 + 1], None,
                              Alu.add),
                          wait=[(V, v2)], inc=True)
                dsp = b.dma_start("spQK2", hout[dc, :, tt * QW:(tt + 1) * QW],
                                  evc[:], wait=[(V, v3)])
                evc_free[:] = [(chsp2, dsp)]
                if probe:
                    v4 = b.do("vector",
                              lambda e, dc=dc: e.tensor_reduce(
                                  rtmp[0:1, 0:1] if False else sqc[:, 0:1],
                                  evc[:], AxX, Alu.add),
                              wait=[(V, v3)], inc=True)
                    v5 = b.do("vector",
                              lambda e, dc=dc: e.tensor_add(
                                  qpc[:, dc:dc + 1], qpc[:, dc:dc + 1],
                                  sqc[:, 0:1]),
                              wait=[(V, v4), (V, vz0)], inc=True)
                    evc_free[:] = [(chsp2, dsp), (V, v5)]
            ap_free = [(V, V.n)]

    sd = c_proj_fmajor("L1_attn_Wq", h_qraw, 0)
    dn_ = c_rows_stats(sd)
    b.barrier()
    c_apply_fmajor(h_qraw, h_q1, 8, 16, probe=True)
    b.barrier()
    vqp = b.do("vector", lambda e: e.tensor_scalar(
        qpc[:], qpc[:], 1.0 / (S * 8.0), None, Alu.mult), inc=True)
    sd = c_proj_fmajor("L1_attn_Wk", h_kraw, 24)
    dn_ = c_rows_stats(sd)
    b.barrier()
    c_apply_fmajor(h_kraw, h_k1, 32, 40)
    b.barrier()

    esC1b = ExitStack()
    vrows = b.sb("vrows", [P, 3 * DM], F32, scope=esC1b)
    vtile = b.sb("vtile", [P, DM], F32, scope=esC1b)
    vbuf = b.sb("vbuf", [P, DM], F32, scope=esC1b)
    chvr = b.dch("vrows")
    b.dma_start("vrows", vrows[:, 0:DM],
                bass.AP(W["L1_attn_gv"], 0, [[0, P], [1, DM]]))
    b.dma_start("vrows", vrows[:, DM:2 * DM],
                bass.AP(W["L1_attn_betav"], 0, [[0, P], [1, DM]]))
    b.dma_start("vrows", vrows[:, 2 * DM:3 * DM],
                bass.AP(W["L1_attn_bv"], 0, [[0, P], [1, DM]]))
    nvr = chvr.n
    chvt2 = b.dch("vtl")
    wvres = b.sb("wvres", [P, KC * DM], F32, scope=esC1b)
    chwv = b.dch("wv")
    for kc in range(KC):
        b.dma_start("wv", wvres[:, kc * DM:(kc + 1) * DM],
                    W["L1_attn_Wv"].ap()[kc * P:(kc + 1) * P, :],
                    wait=[(T, T.n)])
    nwv = chwv.n
    vb_free = [(b.dch("spQK2"), 0)]
    for ts in range(TT):
        xl = 0
        for kc in range(KC):
            xl = b.dma_start("vtl", vtile[:, kc * P:(kc + 1) * P],
                             h_x2f[kc, :, ts * P:(ts + 1) * P],
                             wait=[(T, T.n)])
        for n2 in range(2):
            bank_i = n2
            tn = 0
            for kc in range(KC):
                tn = b.do("tensor",
                          lambda e, kc=kc, n2=n2, bi=bank_i: e.matmul(
                              pbank(bi),
                              vtile[:, kc * P:(kc + 1) * P],
                              wvres[:, kc * DM + n2 * QW:
                                    kc * DM + (n2 + 1) * QW],
                              start=(kc == 0), stop=(kc == KC - 1)),
                          wait=(bank_free[bank_i] + [(chwv, nwv),
                                                     (chvt2, xl)])
                          if kc == 0 else (), inc=(kc == KC - 1))
            vn = b.do("vector",
                      lambda e, n2=n2, bi=bank_i: e.tensor_add(
                          vbuf[:, n2 * QW:(n2 + 1) * QW], pbank(bi),
                          vrows[:, 2 * DM + n2 * QW:2 * DM + (n2 + 1) * QW]),
                      wait=[(T, tn), (chvr, nvr)] + vb_free, inc=True)
            bank_free[bank_i] = [(V, vn)]
        vs = b.do("vector", lambda e: e.tensor_reduce(
            stat[:, 0:1], vbuf[:], AxX, Alu.add), wait=[(V, vn)], inc=True)
        ss = b.do("scalar", lambda e: e.activation(
            lnsq[:], vbuf[:], Act.Square, accum_out=stat[:, 1:2]),
            wait=[(V, vn)], inc=True)
        b.do("vector", lambda e: e.tensor_scalar(
            stat[:, 2:3], stat[:, 0:1], 1.0 / DM, None, Alu.mult),
            wait=[(V, vs)], inc=True)
        b.do("vector", lambda e: e.tensor_mul(
            stat[:, 3:4], stat[:, 2:3], stat[:, 2:3]), inc=True)
        v3 = b.do("vector", lambda e: e.scalar_tensor_tensor(
            stat[:, 4:5], stat[:, 1:2], 1.0 / DM, stat[:, 3:4],
            Alu.mult, Alu.subtract), wait=[(Sc, ss)], inc=True)
        s2 = b.do("scalar", lambda e: e.activation(
            stat[:, 5:6], stat[:, 4:5], Act.Sqrt, bias=cEPS[:, 1:2]),
            wait=[(V, v3)], inc=True)
        vr_ = b.do("vector", lambda e: e.reciprocal(
            stat[:, 6:7], stat[:, 5:6]), wait=[(Sc, s2)], inc=True)
        v4 = b.do("vector", lambda e: e.tensor_scalar(
            vbuf[:], vbuf[:], stat[:, 2:3], stat[:, 6:7],
            Alu.subtract, Alu.mult), wait=[(V, vr_)], inc=True)
        v5 = b.do("vector", lambda e: e.tensor_mul(
            vbuf[:], vbuf[:], vrows[:, 0:DM]), wait=[(V, v4)], inc=True)
        v6 = b.do("vector", lambda e: e.tensor_add(
            vbuf[:], vbuf[:], vrows[:, DM:2 * DM]), wait=[(V, v5)], inc=True)
        dv = b.dma_start("spQK2", h_v1t.ap()[ts * P:(ts + 1) * P, :],
                         vbuf[:], wait=[(V, v6)])
        vb_free = [(chsp2, dv)]

    b.barrier()

    colsC2 = b.sb("colsC2", [P, 8], F32, scope=esC1b)
    for c in range(KC):
        b.dma_start("c1", colsC2[:, c:c + 1],
                    bass.AP(W["L1_attn_bfq"], c * P, [[1, P], [1, 1]]))
    nc2_ = chc.n
    for tt in range(NQ):
        xw = (T, T.n)
        xl = 0
        for kc in range(KC):
            xl = b.dma_start("xf", xf[kc][:, 0:QW],
                             h_q1[kc, :, tt * QW:(tt + 1) * QW], wait=[xw])
        for dc in range(KC):
            slot = dc % 2
            wld = 0
            for kc in range(KC):
                wld = b.dma_start(
                    "wc%d" % slot,
                    wtc[:, slot * DM + kc * P:slot * DM + (kc + 1) * P],
                    W["L1_attn_Wfq"].ap()[kc * P:(kc + 1) * P,
                                          dc * P:(dc + 1) * P],
                    wait=wc_free[slot])
            bank_i = dc % 2
            tn = 0
            for kc in range(KC):
                tn = b.do("tensor",
                          lambda e, kc=kc, s=slot, bi=bank_i: e.matmul(
                              pbank(bi),
                              wtc[:, s * DM + kc * P:s * DM + (kc + 1) * P],
                              xf[kc][:, 0:QW],
                              start=(kc == 0), stop=(kc == KC - 1)),
                          wait=(bank_free[bank_i] + [(chwc[slot], wld),
                                                     (chxf, xl)])
                          if kc == 0 else (), inc=(kc == KC - 1))
            wc_free[slot] = [(T, tn)]
            sn = b.do("scalar",
                      lambda e, bi=bank_i, c0=dc: e.activation(
                          evc[:], pbank(bi), Act.Tanh,
                          bias=colsC2[:, c0:c0 + 1]),
                      wait=[(T, tn), (chc, nc2_)] + evc_free, inc=True)
            bank_free[bank_i] = [(Sc, sn)]
            dsp = b.dma_start("spQK2", h_pqf[dc, :, tt * QW:(tt + 1) * QW],
                              evc[:], wait=[(Sc, sn)])
            evc_free[:] = [(chsp2, dsp)]

    b.dma_start("vrows", vrows[:, 0:DM],
                bass.AP(W["L1_attn_bfk"], 0, [[0, P], [1, DM]]),
                wait=[(V, V.n)])
    nvr2 = chvr.n
    for kc in range(KC):
        b.dma_start("wv", wvres[:, kc * DM:(kc + 1) * DM],
                    W["L1_attn_Wfk"].ap()[kc * P:(kc + 1) * P, :],
                    wait=[(T, T.n)])
    nwv2 = chwv.n
    for ts in range(TT):
        xl = 0
        for kc in range(KC):
            xl = b.dma_start("vtl", vtile[:, kc * P:(kc + 1) * P],
                             h_k1[kc, :, ts * P:(ts + 1) * P],
                             wait=[(T, T.n)])
        for n2 in range(2):
            bank_i = n2
            tn = 0
            for kc in range(KC):
                tn = b.do("tensor",
                          lambda e, kc=kc, n2=n2, bi=bank_i: e.matmul(
                              pbank(bi),
                              vtile[:, kc * P:(kc + 1) * P],
                              wvres[:, kc * DM + n2 * QW:
                                    kc * DM + (n2 + 1) * QW],
                              start=(kc == 0), stop=(kc == KC - 1)),
                          wait=(bank_free[bank_i] + [(chwv, nwv2),
                                                     (chvt2, xl)])
                          if kc == 0 else (), inc=(kc == KC - 1))
            vn = b.do("vector",
                      lambda e, n2=n2, bi=bank_i: e.tensor_add(
                          vbuf[:, n2 * QW:(n2 + 1) * QW], pbank(bi),
                          vrows[:, n2 * QW:(n2 + 1) * QW]),
                      wait=[(T, tn), (chvr, nvr2)] + vb_free, inc=True)
            bank_free[bank_i] = [(V, vn)]
            sn = b.do("scalar",
                      lambda e, n2=n2: e.activation(
                          vbuf[:, n2 * QW:(n2 + 1) * QW],
                          vbuf[:, n2 * QW:(n2 + 1) * QW], Act.Tanh),
                      wait=[(V, vn)], inc=True)
        dv = b.dma_start("spQK2", h_pkt.ap()[ts * P:(ts + 1) * P, :],
                         vbuf[:], wait=[(Sc, sn)])
        vb_free = [(chsp2, dv)]

    b.barrier()

    if dev_tap == "c3":
        chy = b.dch("yst")
        for t in range(TT):
            b.dma_start("yst", y_out.ap()[t * P:(t + 1) * P, :], xs[t][:],
                        wait=[(V, V.n)])
        b.do("sync", lambda e: e.nop(), wait=[(b.dch("yst"), b.dch("yst").n)])
        b.finalize()
        return nc

    esC1b.close()
    esC1.close()
    esC4 = ExitStack()
    obufC = [b.sb(f"obc{c}", [P, S], F32, scope=esC4) for c in range(KC)]
    lgr = b.sb("lgr", [NH, S], F32, scope=esC4)
    sstat = b.sb("sstat", [NH, 8], F32, scope=esC4)
    sccol = b.sb("sccol", [P, TT * NH], F32, scope=esC4)
    s0b = b.sb("s0b", [P, NH], F32, scope=esC4)
    pk0h = b.sb("pk0h", [DH, NH], F32, scope=esC4)
    kf2 = b.sb("kf2", [P, QW], F32, scope=esC4)
    rt2 = b.sb("rt2", [1, QW], F32, scope=esC4)
    pks = b.sb("pks", [P, DH], F32, scope=esC4)
    vs2 = b.sb("vs2", [P, DH], F32, scope=esC4)
    svb = b.sb("svb", [P, DH + 1], F32, scope=esC4)
    kvb = b.sb("kvb", [DH, DH + 1], F32, scope=esC4)
    csb = b.sb("csb", [DH, DH + 1], F32, scope=esC4)
    ctmp = b.sb("ctmp", [DH, 2], F32, scope=esC4)
    ccol = b.sb("ccol", [DH + 1, 1], F32, scope=esC4)
    pqs = b.sb("pqs", [DH, QW], F32, scope=esC4)
    ndb = b.sb("ndb", [DH + 1, QW], F32, scope=esC4)
    dr2 = b.sb("dr2", [1, QW], F32, scope=esC4)
    dbc2 = b.sb("dbc2", [DH, QW], F32, scope=esC4)
    osh2 = b.sb("osh2", [DH, QW], F32, scope=esC4)

    chc4 = b.dch("c4")
    chq2s = b.dch("q2s")
    rt2_free = [(chq2s, 0)]
    kf_free = [(T, 0)]
    for hc in range(KC):
        for tt in range(NQ):
            kl = b.dma_start("c4", kf2[:], h_k1[hc, :, tt * QW:(tt + 1) * QW],
                             wait=kf_free)
            for half in range(2):
                h = 2 * hc + half
                tq2 = b.do("tensor",
                           lambda e, hf=half, hc=hc: e.matmul(
                               pbank(5, 1),
                               qpc[hf * DH:(hf + 1) * DH, hc:hc + 1],
                               kf2[hf * DH:(hf + 1) * DH, :],
                               start=True, stop=True),
                           wait=[(chc4, kl), (V, vqp)] + bank_free[5],
                           inc=True)
                vq2 = b.do("vector",
                           lambda e: e.tensor_copy(rt2[:], pbank(5, 1)),
                           wait=[(T, tq2)] + rt2_free, inc=True)
                bank_free[5] = [(V, vq2)]
                dsh = b.dma_start("q2s", lgr[h:h + 1, tt * QW:(tt + 1) * QW],
                                  rt2[:], wait=[(V, vq2)])
                rt2_free = [(chq2s, dsh)]
            kf_free = [(T, T.n)]
    b.barrier()
    v1 = b.do("vector", lambda e: e.tensor_reduce(
        sstat[:, 0:1], lgr[:], AxX, Alu.max), inc=True)
    v1 = b.do("vector", lambda e: e.tensor_scalar(
        sstat[:, 0:1], sstat[:, 0:1], 0.0, None, Alu.max),
        wait=[(V, v1)], inc=True)
    v1 = b.do("vector", lambda e: e.tensor_scalar(
        sstat[:, 1:2], sstat[:, 0:1], -1.0, None, Alu.mult),
        wait=[(V, v1)], inc=True)
    s1 = b.do("scalar", lambda e: e.activation(
        lgr[:], lgr[:], Act.Exp, bias=sstat[:, 1:2],
        accum_out=sstat[:, 2:3]), wait=[(V, v1)], inc=True)
    s2 = b.do("scalar", lambda e: e.activation(
        sstat[:, 3:4], sstat[:, 1:2], Act.Exp), wait=[(V, v1)], inc=True)
    v2 = b.do("vector", lambda e: e.tensor_add(
        sstat[:, 4:5], sstat[:, 2:3], sstat[:, 3:4]),
        wait=[(Sc, s1), (Sc, s2)], inc=True)
    v2 = b.do("vector", lambda e: e.reciprocal(
        sstat[:, 5:6], sstat[:, 4:5]), wait=[(V, v2)], inc=True)
    v3 = b.do("vector", lambda e: e.tensor_scalar(
        lgr[:], lgr[:], sstat[:, 5:6], None, Alu.mult),
        wait=[(V, v2)], inc=True)
    v4 = b.do("vector", lambda e: e.tensor_mul(
        sstat[:, 6:7], sstat[:, 3:4], sstat[:, 5:6]),
        wait=[(V, v2)], inc=True)
    d0 = b.dma_start("q2s", h_rows.ap()[2:3, 0:NH], sstat[:, 6:7],
                     wait=[(V, v4)] + rt2_free)
    dsb = b.dma_start("c4", s0b[:], bass.AP(h_rows, 2 * S, [[0, P], [1, NH]]),
                      wait=[(chq2s, d0)])
    for jc in range(TT):
        bank_i = 6 + (jc % 2)
        tn = b.do("tensor",
                  lambda e, j=jc, bi=bank_i: e.transpose(
                      pbank(bi, P, NH), lgr[0:NH, j * P:(j + 1) * P],
                      idF[0:NH, 0:NH]),
                  wait=bank_free[bank_i] + [(V, v3)], inc=True)
        vn = b.do("vector",
                  lambda e, j=jc, bi=bank_i: e.tensor_copy(
                      sccol[:, j * NH:(j + 1) * NH], pbank(bi, P, NH)),
                  wait=[(T, tn)], inc=True)
        bank_free[bank_i] = [(V, vn)]
    for h in range(NH):
        b.dma_start("c4", pk0h[:, h:h + 1],
                    bass.AP(W["L1_attn_bfk"], h * DH, [[1, DH], [1, 1]]))
    npk0 = chc4.n
    sp0 = b.do("scalar", lambda e: e.activation(pk0h[:], pk0h[:], Act.Tanh),
               wait=[(chc4, npk0)], inc=True)
    b.barrier()

    chkv = b.dch("kvld")
    chkv2 = b.dch("kvld2")
    chpq = b.dch("pqld")
    chden2 = b.dch("den2")
    chdb2 = b.dch("dbc2l")
    chosh2 = b.dch("osh2l")
    sv_free = [(T, 0)]
    kvld_free = [(V, 0)]
    kvb_free = [(T, 0)]
    csb_free = [(V, 0)]
    ccol_free = [(V, 0)]
    nd_free = [(V, 0)]
    dr2_free = [(chden2, 0)]
    db2_free = [(V, 0)]
    osh2_free = [(chosh2, 0)]
    ct_free = [(V, 0)]
    for h in range(NH):
        hc, half = h // 2, h % 2
        kv_started = False
        tkv = 0
        for jc in range(TT):
            l1 = b.dma_start("kvld", pks[:],
                             bass.AP(h_pkt, jc * P * DM + h * DH,
                                     [[DM, P], [1, DH]]),
                             wait=kvld_free)
            l2 = b.dma_start("kvld2", vs2[:],
                             bass.AP(h_v1t, jc * P * DM + h * DH,
                                     [[DM, P], [1, DH]]),
                             wait=kvld_free)
            av = b.do("vector",
                      lambda e, j=jc, h=h: e.tensor_scalar(
                          svb[:, 0:DH], vs2[:],
                          sccol[:, j * NH + h:j * NH + h + 1], None,
                          Alu.mult),
                      wait=[(chkv2, l2)] + sv_free, inc=True)
            av2 = b.do("vector",
                       lambda e, j=jc, h=h: e.tensor_copy(
                           svb[:, DH:DH + 1],
                           sccol[:, j * NH + h:j * NH + h + 1]),
                       wait=[(V, av)], inc=True)
            w0 = [(V, av2), (chkv, l1)]
            if not kv_started:
                w0 = w0 + kvb_free + csb_free
            b.do("tensor",
                 lambda e, first=(not kv_started), last=(jc == TT - 1):
                 e.matmul(pbank(2, DH, DH + 1), pks[:], svb[:],
                          start=first, stop=last),
                 wait=w0, inc=False)
            tkv = b.do("tensor",
                       lambda e, first=(not kv_started),
                       last=(jc == TT - 1):
                       e.matmul(pbank(3, 1, DH + 1), cONE[:, 0:1], svb[:],
                                start=first, stop=last),
                       inc=True)
            kv_started = True
            kvld_free = [(T, tkv)]
            sv_free = [(T, tkv)]
        ve1 = b.do("vector", lambda e: e.tensor_copy(
            kvb[:], pbank(2, DH, DH + 1)), wait=[(T, tkv)], inc=True)
        ve2 = b.do("vector", lambda e: e.tensor_copy(
            rt2[0:1, 0:DH + 1], pbank(3, 1, DH + 1)),
            wait=[(T, tkv)] + rt2_free, inc=True)
        dwr = b.dma_start("den2", h_rows.ap()[3:4, 0:DH + 1],
                          rt2[0:1, 0:DH + 1], wait=[(V, ve2)])
        rt2_free = [(chden2, dwr)]
        dcs = b.dma_start("den2", csb[:],
                          bass.AP(h_rows, 3 * S, [[0, DH], [1, DH + 1]]),
                          wait=[(chden2, dwr)] + csb_free)
        vc1 = b.do("vector", lambda e: e.tensor_add(kvb[:], kvb[:], csb[:]),
                   wait=[(V, ve1), (chden2, dcs)], inc=True)
        csb_free = [(V, vc1)]
        vc2 = b.do("vector",
                   lambda e, h=h: e.tensor_mul(
                       ctmp[:, 0:1], pk0h[:, h:h + 1], s0b[0:DH, h:h + 1]),
                   wait=[(Sc, sp0), (chc4, dsb)] + ct_free, inc=True)
        vc3 = b.do("vector",
                   lambda e, h=h: e.tensor_add(
                       ctmp[:, 0:1], ctmp[:, 0:1], s0b[0:DH, h:h + 1]),
                   wait=[(V, vc2)], inc=True)
        vc4 = b.do("vector", lambda e: e.tensor_add(
            kvb[:, DH:DH + 1], kvb[:, DH:DH + 1], ctmp[:, 0:1]),
            wait=[(V, vc3), (V, vc1)], inc=True)
        ct_free = [(V, vc4)]
        tcc = b.do("tensor", lambda e: e.matmul(
            pbank(5, DH + 1, 1), kvb[:], cONE[0:DH, 0:1],
            start=True, stop=True),
            wait=[(V, vc4)] + bank_free[5], inc=True)
        vcc = b.do("vector", lambda e: e.tensor_copy(
            ccol[:], pbank(5, DH + 1, 1)), wait=[(T, tcc)] + ccol_free,
            inc=True)
        bank_free[5] = [(V, vcc)]
        kvb_free = [(T, 0)]
        for tt in range(NQ):
            lq = b.dma_start("pqld", pqs[:],
                             h_pqf[hc, half * DH:(half + 1) * DH,
                                   tt * QW:(tt + 1) * QW],
                             wait=[(T, T.n)])
            tnd = b.do("tensor", lambda e: e.matmul(
                pbank(4, DH + 1), kvb[:], pqs[:],
                start=True, stop=True),
                wait=[(chpq, lq), (V, vc4)] + bank_free[4], inc=True)
            vnd = b.do("vector", lambda e: e.tensor_scalar(
                ndb[:], pbank(4, DH + 1), ccol[:], None, Alu.add),
                wait=[(T, tnd), (V, vcc)] + nd_free, inc=True)
            bank_free[4] = [(V, vnd)]
            vdd = b.do("vector", lambda e: e.tensor_scalar(
                dr2[:], ndb[DH:DH + 1, :], 1e-6, None, Alu.add),
                wait=[(V, vnd)] + dr2_free, inc=True)
            vdr = b.do("vector", lambda e: e.reciprocal(dr2[:], dr2[:]),
                       wait=[(V, vdd)], inc=True)
            dwr2 = b.dma_start("den2", h_rows.ap()[4:5, 0:QW], dr2[:],
                               wait=[(V, vdr)])
            dr2_free = [(chden2, dwr2)]
            dbn2 = b.dma_start("dbc2l", dbc2[:],
                               bass.AP(h_rows, 4 * S, [[0, DH], [1, QW]]),
                               wait=[(chden2, dwr2)] + db2_free)
            if half == 0:
                vo = b.do("vector",
                          lambda e, q=tt, c=hc: e.tensor_mul(
                              obufC[c][0:DH, q * QW:(q + 1) * QW],
                              ndb[0:DH, :], dbc2[:]),
                          wait=[(chdb2, dbn2), (V, vnd)], inc=True)
                db2_free = [(V, vo)]
                nd_free = [(V, vo)]
            else:
                vo = b.do("vector",
                          lambda e: e.tensor_mul(osh2[:], ndb[0:DH, :],
                                                 dbc2[:]),
                          wait=[(chdb2, dbn2), (V, vnd)] + osh2_free,
                          inc=True)
                db2_free = [(V, vo)]
                nd_free = [(V, vo)]
                dsh2 = b.dma_start("osh2l",
                                   obufC[hc][DH:P, tt * QW:(tt + 1) * QW],
                                   osh2[:], wait=[(V, vo)])
                osh2_free = [(chosh2, dsh2)]
        kvb_free = [(T, T.n)]

    b.barrier()

    if dev_tap == "attno":
        chy = b.dch("yst")
        for c in range(KC):
            for t2 in range(2):
                b.dma_start("yst",
                            h_af[c, :, t2 * 1024:(t2 + 1) * 1024],
                            obufC[c][:, t2 * 1024:(t2 + 1) * 1024],
                            wait=[(V, V.n)])
        for t in range(TT):
            b.dma_start("yst", y_out.ap()[t * P:(t + 1) * P, :], xs[t][:],
                        wait=[(V, V.n)])
        b.do("sync", lambda e: e.nop(), wait=[(b.dch("yst"), b.dch("yst").n)])
        b.finalize()
        return nc

    vsz = b.do("vector", lambda e: e.memset(srows[:], 0.0),
               wait=[(V, V.n), (T, T.n)], inc=True)
    srows_dep = [(V, vsz)]
    sq5 = [(T, 0)]
    for tt in range(NQ):
        tsum = 0
        for dc in range(KC):
            tsum = b.do("tensor",
                        lambda e, dc=dc, tt=tt: e.matmul(
                            pbank(5, 1), cONE[:, 0:1],
                            obufC[dc][:, tt * QW:(tt + 1) * QW],
                            start=(dc == 0), stop=(dc == KC - 1)),
                        wait=bank_free[5] if dc == 0 else (),
                        inc=(dc == KC - 1))
        tsq = 0
        for dc in range(KC):
            sn = b.do("scalar",
                      lambda e, dc=dc, tt=tt: e.activation(
                          sqc[:], obufC[dc][:, tt * QW:(tt + 1) * QW],
                          Act.Square),
                      wait=sq5, inc=True)
            tsq = b.do("tensor",
                       lambda e, dc=dc: e.matmul(
                           pbank(4, 1), cONE[:, 0:1], sqc[:],
                           start=(dc == 0), stop=(dc == KC - 1)),
                       wait=[(Sc, sn)] + (bank_free[4] if dc == 0 else []),
                       inc=True)
            sq5 = [(T, tsq)]
        v1 = b.do("vector",
                  lambda e, tt=tt: e.tensor_copy(
                      srows[0:1, tt * QW:(tt + 1) * QW], pbank(5, 1)),
                  wait=[(T, tsum)] + srows_dep, inc=True)
        bank_free[5] = [(V, v1)]
        v2 = b.do("vector",
                  lambda e, tt=tt: e.tensor_copy(
                      srows[0:1, S + tt * QW:S + (tt + 1) * QW], pbank(4, 1)),
                  wait=[(T, tsq)], inc=True)
        bank_free[4] = [(V, v2)]
        srows_dep = [(V, v2)]
    dn5 = c_rows_stats(srows_dep)
    b.barrier()
    ap5 = [(V, 0)]
    ev5 = [(V, 0)]
    chap = b.dch("c_ap")
    for tt in range(NQ):
        dm_ = b.dma_start("c_ap", mbc[:],
                          bass.AP(h_rows, tt * QW, [[0, P], [1, QW]]),
                          wait=ap5)
        dr_ = b.dma_start("c_ap", rbc[:],
                          bass.AP(h_rows, S + tt * QW, [[0, P], [1, QW]]),
                          wait=ap5)
        vl = 0
        for dc in range(KC):
            v1 = b.do("vector",
                      lambda e, dc=dc, tt=tt: e.tensor_sub(
                          evc[:], obufC[dc][:, tt * QW:(tt + 1) * QW],
                          mbc[:]),
                      wait=[(chap, dr_)] + ev5, inc=True)
            v2 = b.do("vector",
                      lambda e, dc=dc: e.scalar_tensor_tensor(
                          evc[:], evc[:], colsC[:, 48 + dc:49 + dc], rbc[:],
                          Alu.mult, Alu.mult),
                      wait=[(V, v1)], inc=True)
            v3 = b.do("vector",
                      lambda e, dc=dc, tt=tt: e.tensor_scalar(
                          obufC[dc][:, tt * QW:(tt + 1) * QW], evc[:],
                          colsC[:, 56 + dc:57 + dc], None, Alu.add),
                      wait=[(V, v2)], inc=True)
            ev5 = [(V, v3)]
            vl = v3
        ap5 = [(V, vl)]
    b.barrier()

    if dev_tap == "a1":
        chy = b.dch("yst")
        for c in range(KC):
            for t2 in range(2):
                b.dma_start("yst",
                            h_af[c, :, t2 * 1024:(t2 + 1) * 1024],
                            obufC[c][:, t2 * 1024:(t2 + 1) * 1024],
                            wait=[(V, V.n)])
        for t in range(TT):
            b.dma_start("yst", y_out.ap()[t * P:(t + 1) * P, :], xs[t][:],
                        wait=[(V, V.n)])
        b.do("sync", lambda e: e.nop(), wait=[(b.dch("yst"), b.dch("yst").n)])
        b.finalize()
        return nc

    colsD = b.sb("colsD", [P, 40], F32, scope=esC4)
    w6 = b.sb("w6", [P, 2 * DM], F32, scope=esC4)
    for c in range(32):
        b.dma_start("c1", colsD[:, c:c + 1],
                    bass.AP(W["L1_attn_b1"], c * P, [[1, P], [1, 1]]))
    for c in range(KC):
        b.dma_start("c1", colsD[:, 32 + c:33 + c],
                    bass.AP(W["L1_attn_b2"], c * P, [[1, P], [1, 1]]))
    ncD = chc.n
    w6_free = [[(T, 0)], [(T, 0)]]
    ev6 = [(chsp2, 0)]
    for nb in range(32):
        slot = nb % 2
        wld = 0
        for kc in range(KC):
            wld = b.dma_start(
                "wc%d" % slot,
                w6[:, slot * DM + kc * P:slot * DM + (kc + 1) * P],
                W["L1_attn_W1"].ap()[kc * P:(kc + 1) * P,
                                     nb * P:(nb + 1) * P],
                wait=w6_free[slot])
        tn = 0
        for tt in range(NQ):
            bank_i = (nb * NQ + tt) % 2
            for kc in range(KC):
                tn = b.do("tensor",
                          lambda e, kc=kc, s=slot, bi=bank_i, tt=tt:
                          e.matmul(
                              pbank(bi),
                              w6[:, s * DM + kc * P:s * DM + (kc + 1) * P],
                              obufC[kc][:, tt * QW:(tt + 1) * QW],
                              start=(kc == 0), stop=(kc == KC - 1)),
                          wait=(bank_free[bank_i] +
                                [(chwc[slot], wld)]) if kc == 0 else (),
                          inc=(kc == KC - 1))
            sn = b.do("scalar",
                      lambda e, bi=bank_i, nb=nb: e.activation(
                          evc[:], pbank(bi), Act.Gelu,
                          bias=colsD[:, nb:nb + 1]),
                      wait=[(T, tn), (chc, ncD)] + ev6, inc=True)
            bank_free[bank_i] = [(Sc, sn)]
            dsp = b.dma_start("spQK2", h_h1[nb, :, tt * QW:(tt + 1) * QW],
                              evc[:], wait=[(Sc, sn)])
            ev6 = [(chsp2, dsp)]
        w6_free[slot] = [(T, tn)]
    b.barrier()

    gbuf = [b.sb(f"gbuf{i}", [P, QW], F32, scope=esC4) for i in range(2)]
    chgb = [b.dch("gb0"), b.dch("gb1")]
    gb_free = [[(T, 0)], [(T, 0)]]
    w7_free = [[(T, 0)], [(T, 0)]]
    ev7 = [(T, 0)]

    def psub(nb2, j):
        t = ps4a if nb2 < 4 else ps4b
        c = (nb2 % 4) * 512 + j * P
        return t[0:P, c:c + P]

    for tt in range(NQ):
        tl = 0
        for kc2 in range(32):
            slot = kc2 % 2
            gl = b.dma_start("gb%d" % slot, gbuf[slot][:],
                             h_h1[kc2, :, tt * QW:(tt + 1) * QW],
                             wait=gb_free[slot])
            wld = 0
            for nb2 in range(KC):
                wld = b.dma_start(
                    "wc%d" % slot,
                    w6[:, slot * DM + nb2 * P:slot * DM + (nb2 + 1) * P],
                    W["L1_attn_W2"].ap()[kc2 * P:(kc2 + 1) * P,
                                         nb2 * P:(nb2 + 1) * P],
                    wait=w7_free[slot])
            for nb2 in range(KC):
                tl = b.do("tensor",
                          lambda e, s=slot, nb2=nb2, kc2=kc2: e.matmul(
                              pbank(nb2),
                              w6[:, s * DM + nb2 * P:s * DM + (nb2 + 1) * P],
                              gbuf[s][:],
                              start=(kc2 == 0), stop=(kc2 == 31)),
                          wait=(([(chwc[slot], wld), (chgb[slot], gl)]
                                 if nb2 == 0 else []) +
                                (bank_free[nb2] if kc2 == 0 else [])),
                          inc=(nb2 == KC - 1))
            gb_free[slot] = [(T, tl)]
            w7_free[slot] = [(T, tl)]
        for nb2 in range(KC):
            ve = b.do("vector",
                      lambda e, nb2=nb2: e.tensor_scalar(
                          evc[:], pbank(nb2), colsD[:, 32 + nb2:33 + nb2],
                          None, Alu.add),
                      wait=[(T, tl), (chc, ncD)] + ev7, inc=True)
            va = b.do("vector",
                      lambda e, nb2=nb2, tt=tt: e.tensor_add(
                          evc[:], evc[:],
                          obufC[nb2][:, tt * QW:(tt + 1) * QW]),
                      wait=[(V, ve)], inc=True)
            tj = 0
            vstt = 0
            for j in range(NQ):
                tj = b.do("tensor",
                          lambda e, nb2=nb2, j=j: e.transpose(
                              pbank(nb2, P, P), evc[:, j * P:(j + 1) * P],
                              idF[:]),
                          wait=[(V, va)] if j == 0 else [(V, vstt)],
                          inc=True)
                vstt = b.do("vector",
                            lambda e, nb2=nb2, tt=tt, j=j:
                            e.scalar_tensor_tensor(
                                xs[tt * NQ + j][:, nb2 * P:(nb2 + 1) * P],
                                xs[tt * NQ + j][:, nb2 * P:(nb2 + 1) * P],
                                2.0, pbank(nb2, P, P),
                                Alu.mult, Alu.add),
                            wait=[(T, tj)], inc=True)
            bank_free[nb2] = [(V, vstt)]
            ev7 = [(T, tj), (V, vstt)]

    for t in range(TT):
        emit_ln_tile(xs[t][:], None, [(V, V.n), (T, T.n)])
    b.barrier()

    if dev_tap == "x3":
        chy = b.dch("yst")
        for t in range(TT):
            b.dma_start("yst", y_out.ap()[t * P:(t + 1) * P, :], xs[t][:],
                        wait=[(V, V.n)])
        b.do("sync", lambda e: e.nop(), wait=[(b.dch("yst"), b.dch("yst").n)])
        b.finalize()
        return nc

    esC4.close()
    esCmid.close()
    esD = ExitStack()
    idB = b.sb("idB", [P, P], BF16, scope=esD)
    swcol = b.sb("swcol", [P, 6], F32, scope=esD)
    macc = b.sb("macc", [P, 1], F32, scope=esD)
    mtmp = b.sb("mtmp", [P, 1], F32, scope=esD)
    m1c = b.sb("m1c", [1, 2], F32, scope=esD)
    vid = b.do("vector", lambda e: e.tensor_copy(idB[:], idF[:]), inc=True)
    v127 = b.do("vector", lambda e: e.memset(swcol[:, 4:5], 127.0), inc=True)
    v127 = b.do("vector", lambda e: e.reciprocal(
        swcol[:, 4:5], swcol[:, 4:5]), wait=[(V, v127)], inc=True)
    chD = b.dch("dld")
    chq2s = b.dch("q2s")

    esD0 = ExitStack()
    cwf = b.sb("cwf", [1, 12288], F32, scope=esD0)
    cpc = b.sb("cpc", [1, 4096], F32, scope=esD0)
    ldc = b.dma_start("dld", cwf[0:1, :],
                      bass.AP(W["L1_ffn_conv_w"], 0, [[12288, 1], [1, 12288]]))
    cp_free = [(b.dch("dld"), 0)]
    for j in range(3):
        vj = b.do("vector", lambda e, j=j: e.tensor_copy(
            cpc[0:1, :], cwf[0:1, j:12288:3]),
            wait=[(chD, ldc)] + cp_free, inc=True)
        dj = b.dma_start("dld", h_cw.ap()[j:j + 1, :], cpc[0:1, :],
                         wait=[(V, vj)])
        cp_free = [(chD, dj)]
    b.barrier()
    esD0.close()

    def d_weight_scale(wname, nt, wid, stage, col0):
        mz = b.do("vector", lambda e: e.memset(macc[:], 0.0),
                  wait=[(V, V.n)], inc=True)
        mdep = [(V, mz)]
        st_free = [(Sc, 0)]
        nh = wid // 2048 if wid > 2048 else 1
        cw = wid // nh
        for ti in range(nt):
            for hf in range(nh):
                ld = b.dma_start("dld", stage[:, 0:cw],
                                 W[wname].ap()[ti * P:(ti + 1) * P,
                                               hf * cw:(hf + 1) * cw],
                                 wait=st_free)
                sa = b.do("scalar",
                          lambda e, cw=cw: e.activation(
                              stage[:, cw:2 * cw], stage[:, 0:cw], Act.Abs,
                              accum_out=mtmp[:, 0:1]),
                          wait=[(chD, ld)] + mdep, inc=True)
                va = b.do("vector", lambda e: e.tensor_add(
                    macc[:], macc[:], mtmp[:, 0:1]),
                    wait=[(Sc, sa)] + mdep, inc=True)
                mdep = [(V, va)]
                st_free = [(Sc, sa)]
        tmm = b.do("tensor", lambda e: e.matmul(
            pbank(5, 1, 1), macc[:], cONE[:, 0:1], start=True, stop=True),
            wait=mdep + bank_free[5], inc=True)
        vm = b.do("vector", lambda e, nt=nt, wid=wid: e.tensor_scalar(
            m1c[0:1, 0:1], pbank(5, 1, 1), 1.0 / (nt * P * wid), None,
            Alu.mult), wait=[(T, tmm)], inc=True)
        bank_free[5] = [(V, vm)]
        vm = b.do("vector", lambda e: e.tensor_scalar(
            m1c[0:1, 0:1], m1c[0:1, 0:1], 1e-5, None, Alu.max),
            wait=[(V, vm)], inc=True)
        vm2 = b.do("vector", lambda e: e.reciprocal(
            m1c[0:1, 1:2], m1c[0:1, 0:1]), wait=[(V, vm)], inc=True)
        dsw = b.dma_start("q2s", h_rows.ap()[5:6, 0:2], m1c[0:1, 0:2],
                          wait=[(V, vm2), (chq2s, chq2s.n)])
        dbc = b.dma_start("dld", swcol[:, col0:col0 + 2],
                          bass.AP(h_rows, 5 * S, [[0, P], [1, 2]]),
                          wait=[(chq2s, dsw)])
        return dbc

    def d_ternary(src_ap, ta, tb, tc, sw_col, wait):
        g = b.do("vector", lambda e: e.tensor_scalar(
            ta, src_ap, sw_col, None, Alu.mult), wait=wait, inc=True)
        g = b.do("vector", lambda e: e.tensor_scalar(
            ta, ta, -2.0, 2.0, Alu.max, Alu.min), wait=[(G, g)], inc=True)
        g = b.do("vector", lambda e: e.tensor_scalar(
            ta, ta, 2.5, None, Alu.add), wait=[(G, g)], inc=True)
        g = b.do("vector", lambda e: e.tensor_copy(tb, ta),
                 wait=[(G, g)], inc=True)
        for s in (4.0, 2.0, 1.0):
            g = b.do("vector", lambda e, s=s: e.tensor_scalar(
                tc, tb, s, s, Alu.is_ge, Alu.mult), wait=[(G, g)], inc=True)
            g = b.do("vector", lambda e: e.tensor_sub(tb, tb, tc),
                     wait=[(G, g)], inc=True)
        g = b.do("vector", lambda e: e.tensor_sub(ta, ta, tb),
                 wait=[(G, g)], inc=True)
        g = b.do("vector", lambda e: e.tensor_scalar(
            ta, ta, -2.0, None, Alu.add), wait=[(G, g)], inc=True)
        g = b.do("vector", lambda e: e.tensor_scalar(
            ta, ta, -1.0, 1.0, Alu.max, Alu.min), wait=[(G, g)], inc=True)
        return g

    def d_round127(src_ap, sc_col, ta, tb, tc, dst_ap, wait):
        g = b.do("vector", lambda e: e.tensor_scalar(
            ta, src_ap, sc_col, None, Alu.mult), wait=wait, inc=True)
        g = b.do("vector", lambda e: e.tensor_scalar(
            ta, ta, 128.5, None, Alu.add), wait=[(G, g)], inc=True)
        g = b.do("vector", lambda e: e.tensor_copy(tb, ta),
                 wait=[(G, g)], inc=True)
        for s in (128.0, 64.0, 32.0, 16.0, 8.0, 4.0, 2.0, 1.0):
            g = b.do("vector", lambda e, s=s: e.tensor_scalar(
                tc, tb, s, s, Alu.is_ge, Alu.mult), wait=[(G, g)], inc=True)
            g = b.do("vector", lambda e: e.tensor_sub(
                tb, tb, tc), wait=[(G, g)], inc=True)
        g = b.do("vector", lambda e: e.tensor_sub(ta, ta, tb),
                 wait=[(G, g)], inc=True)
        g = b.do("vector", lambda e: e.tensor_scalar(
            ta, ta, -128.0, None, Alu.add), wait=[(G, g)], inc=True)
        v = b.do("vector", lambda e: e.tensor_copy(dst_ap, ta),
                 wait=[(G, g)], inc=True)
        return v

    esD1 = ExitStack()
    w1qres = b.sb("w1qres", [P, KC * 4096], BF16, scope=esD1)
    wst = b.sb("wst", [P, 2 * DM], F32, scope=esD1)
    wq1 = b.sb("wq1", [P, DM], F32, scope=esD1)
    wq2 = b.sb("wq2", [P, DM], F32, scope=esD1)
    wq3 = b.sb("wq3", [P, DM], F32, scope=esD1)
    dbc1 = d_weight_scale("L1_ffn_w1", 32, DM, wst, 0)
    qw_free = [(G, 0)]
    wqb_free = [(T, 0)]
    for ht in range(32):
        ld = b.dma_start("dld", wst[:, 0:DM],
                         W["L1_ffn_w1"].ap()[ht * P:(ht + 1) * P, :],
                         wait=qw_free)
        vq = d_ternary(wst[:, 0:DM], wq1[:], wq2[:], wq3[:],
                       swcol[:, 1:2],
                       [(chD, ld), (chD, dbc1)] + wqb_free)
        qw_free = [(G, G.n)]
        if dev_dump and ht == 0:
            dn_dbg = b.dma_start("dbg", h_dbg[0, :, 0:DM], wq1[:],
                                 wait=[(G, vq)])
            qw_free = qw_free + [(b.dch("dbg"), dn_dbg)]
        tn = 0
        for dc in range(KC):
            bi = 6 + (dc % 2)
            tn = b.do("tensor", lambda e, dc=dc, bi=bi: e.transpose(
                pbank(bi, P, P), wq1[:, dc * P:(dc + 1) * P], idF[:]),
                wait=[(G, vq)] + bank_free[bi], inc=True)
            vw = b.do("vector",
                      lambda e, dc=dc, ht=ht, bi=bi: e.tensor_copy(
                          w1qres[:, dc * 4096 + ht * P:
                                 dc * 4096 + (ht + 1) * P],
                          pbank(bi, P, P)),
                      wait=[(T, tn)], inc=True)
            bank_free[bi] = [(V, vw)]
        wqb_free = [(T, tn)]
    b.barrier()

    qxc = b.sb("qxc", [P, 8], F32, scope=esD1)
    qxb = b.sb("qxb", [P, DM], BF16, scope=esD1)
    qxT = b.sb("qxT", [P, DM], BF16, scope=esD1)
    hb1 = b.sb("hb1", [P, QW], F32, scope=esD1)
    chqtD = b.dch("qtrD")
    chqtD2 = b.dch("qtrD2")
    chspD = b.dch("spD")
    qxT_cons = [(T, 0)]
    hb1_free = [(chspD, 0)]
    qx_free = [(b.dch("qtrD"), 0)]
    dbg_gate = []
    for t in range(TT):
        v = b.do("vector", lambda e, t=t: e.tensor_reduce(
            qxc[:, 0:1], xs[t][:], AxX, Alu.max, apply_absolute_value=True),
            wait=[(V, V.n), (G, G.n)] + dbg_gate, inc=True)
        v = b.do("vector", lambda e: e.tensor_scalar(
            qxc[:, 1:2], qxc[:, 0:1], 1e-5, None, Alu.max),
            wait=[(V, v)], inc=True)
        v2 = b.do("vector", lambda e: e.reciprocal(qxc[:, 2:3], qxc[:, 1:2]),
                  wait=[(V, v)], inc=True)
        v3 = b.do("vector", lambda e: e.tensor_scalar(
            qxc[:, 3:4], qxc[:, 2:3], 127.0, None, Alu.mult),
            wait=[(V, v2)], inc=True)
        v4 = b.do("vector", lambda e: e.tensor_scalar(
            qxc[:, 4:5], qxc[:, 1:2], swcol[:, 4:5], None, Alu.mult),
            wait=[(V, v), (V, v127)], inc=True)
        vq = d_round127(xs[t][:], qxc[:, 3:4], wq1[:], wq2[:], wq3[:],
                        qxb[:], [(V, v3)] + qx_free)
        if dev_dump and t == 0:
            dn_dbg = b.dma_start("dbg", h_dbg[1, :, 0:DM], wq1[:],
                                 wait=[(G, G.n)])
            dn_dbg = b.dma_start("dbg", h_dbg[2, :, 0:5], qxc[:, 0:5],
                                 wait=[(V, vq)])
            qx_free = qx_free + [(b.dch("dbg"), dn_dbg)]
            dbg_gate = [(b.dch("dbg"), dn_dbg)]
        trn = 0
        for kc in range(KC):
            trn = b.dma_start("qtrD", qxT[:, kc * P:(kc + 1) * P],
                              qxb[:, kc * P:(kc + 1) * P],
                              wait=[(V, vq)] + qxT_cons, transpose=True)
        qx_free = [(chqtD, trn)]
        for hb in range(KC):
            tn = 0
            for kc in range(KC):
                tn = b.do("tensor", lambda e, kc=kc, hb=hb: e.matmul(
                    pbank(hb), qxT[:, kc * P:(kc + 1) * P],
                    w1qres[:, kc * 4096 + hb * QW:kc * 4096 + (hb + 1) * QW],
                    start=(kc == 0), stop=(kc == KC - 1)),
                    wait=(bank_free[hb] + [(chqtD, trn)]) if kc == 0 else (),
                    inc=(kc == KC - 1))
            ve = b.do("vector", lambda e, hb=hb: e.tensor_scalar(
                hb1[:], pbank(hb), qxc[:, 4:5], swcol[:, 0:1],
                Alu.mult, Alu.mult),
                wait=[(T, tn)] + hb1_free, inc=True)
            bank_free[hb] = [(V, ve)]
            dsp = b.dma_start("spD",
                              bass.AP(h_h1, (hb * S + t * P) * QW,
                                      [[QW, P], [1, QW]]),
                              hb1[:], wait=[(V, ve)])
            hb1_free = [(chspD, dsp)]
        qxT_cons = [(T, T.n)]
    b.barrier()
    esD1.close()

    esD2 = ExitStack()
    w2qres = b.sb("w2qres", [P, 32 * DM], BF16, scope=esD2)
    esD2p = ExitStack()
    wst2 = b.sb("wst2", [P, 4096], F32, scope=esD2p)
    wqa = b.sb("wqa", [P, 2048], F32, scope=esD2p)
    wqc = b.sb("wqc", [P, 2048], F32, scope=esD2p)
    wqd = b.sb("wqd", [P, 2048], F32, scope=esD2p)
    dbc2_ = d_weight_scale("L1_ffn_w2", 8, 4096, wst2, 2)
    qw_free = [(G, G.n)]
    wqb2_free = [(T, 0)]
    for dt in range(KC):
        for hf in range(2):
            ld = b.dma_start("dld", wst2[:, 0:2048],
                             W["L1_ffn_w2"].ap()[dt * P:(dt + 1) * P,
                                                 hf * 2048:(hf + 1) * 2048],
                             wait=qw_free)
            vq = d_ternary(wst2[:, 0:2048], wqa[:], wqc[:], wqd[:],
                           swcol[:, 3:4],
                           [(chD, ld), (chD, dbc2_)] + wqb2_free)
            qw_free = [(G, G.n)]
            tn = 0
            for blk in range(16):
                kc2 = hf * 16 + blk
                bi = 6 + (blk % 2)
                tn = b.do("tensor", lambda e, blk=blk, bi=bi: e.transpose(
                    pbank(bi, P, P), wqa[:, blk * P:(blk + 1) * P], idF[:]),
                    wait=[(G, vq)] + bank_free[bi], inc=True)
                vw = b.do("vector",
                          lambda e, kc2=kc2, dt=dt, bi=bi: e.tensor_copy(
                              w2qres[:, kc2 * DM + dt * P:
                                     kc2 * DM + (dt + 1) * P],
                              pbank(bi, P, P)),
                          wait=[(T, tn)], inc=True)
                bank_free[bi] = [(V, vw)]
            wqb2_free = [(T, tn)]
    b.barrier()
    esD2p.close()

    convW = b.sb("convW", [P, 4 * QW], F32, scope=esD2)
    ccm = b.sb("ccm", [P, QW], F32, scope=esD2)
    ccc = b.sb("ccc", [P, QW], F32, scope=esD2)
    ccp = b.sb("ccp", [P, QW], F32, scope=esD2)
    ctm = b.sb("ctm", [P, QW], F32, scope=esD2)
    hfull = b.sb("hfull", [P, 4096], F32, scope=esD2)
    uq = b.sb("uq", [P, DM], F32, scope=esD2)
    cq = b.sb("cq", [P, DM], F32, scope=esD2)
    dtm = b.sb("dtm", [P, DM], F32, scope=esD2)
    qsbw = [b.sb(f"qsbw{i}", [P, DM], BF16, scope=esD2) for i in range(2)]
    qhT = b.sb("qhT", [P, 4096], BF16, scope=esD2)
    qhc = b.sb("qhc", [P, 8], F32, scope=esD2)
    chcw = b.dch("cwld")
    chcv = b.dch("cvld")
    chy = b.dch("yst")
    cw_free = [(V, 0)]
    cv_free = [(V, 0)]
    hf_free = [(V, 0), (G, 0)]
    qhT_cons = [(T, 0)]
    qsb_free = {0: [(chqtD, chqtD.n)], 1: [(chqtD2, chqtD2.n)]}
    mo_free = [(V, 0)]
    conv_w = W["L1_ffn_conv_w"]
    conv_b = W["L1_ffn_conv_b"]
    for t in range(TT):
        svl = 0
        for e8 in range(KC):
            f0 = e8 * QW
            lw = 0
            for j in range(3):
                lw = b.dma_start("cwld", convW[:, j * QW:(j + 1) * QW],
                                 bass.AP(h_cw, j * 4096 + f0,
                                         [[0, P], [1, QW]]),
                                 wait=cw_free)
            lw = b.dma_start("cwld", convW[:, 3 * QW:4 * QW],
                             bass.AP(conv_b, f0, [[0, P], [1, QW]]),
                             wait=cw_free)
            blk0 = e8 * S * QW
            if t == 0:
                vz = b.do("vector", lambda e: e.memset(ccm[:], 0.0),
                          wait=cv_free, inc=True)
                lm = b.dma_start("cvld", ccm[1:P, :],
                                 bass.AP(h_h1, blk0, [[QW, P - 1], [1, QW]]),
                                 wait=[(V, vz)])
            else:
                lm = b.dma_start("cvld", ccm[:],
                                 bass.AP(h_h1, blk0 + (t * P - 1) * QW,
                                         [[QW, P], [1, QW]]),
                                 wait=cv_free)
            lc = b.dma_start("cvld", ccc[:],
                             bass.AP(h_h1, blk0 + t * P * QW,
                                     [[QW, P], [1, QW]]),
                             wait=cv_free)
            if t == TT - 1:
                vz = b.do("vector", lambda e: e.memset(ccp[:], 0.0),
                          wait=cv_free, inc=True)
                lp = b.dma_start("cvld", ccp[0:P - 1, :],
                                 bass.AP(h_h1, blk0 + (t * P + 1) * QW,
                                         [[QW, P - 1], [1, QW]]),
                                 wait=[(V, vz)])
            else:
                lp = b.dma_start("cvld", ccp[:],
                                 bass.AP(h_h1, blk0 + (t * P + 1) * QW,
                                         [[QW, P], [1, QW]]),
                                 wait=cv_free)
            v = b.do("vector", lambda e: e.tensor_mul(
                ccm[:], ccm[:], convW[:, 0:QW]),
                wait=[(chcv, lp), (chcw, lw)], inc=True)
            v2 = b.do("vector", lambda e: e.tensor_mul(
                ctm[:], ccc[:], convW[:, QW:2 * QW]),
                wait=[(chcv, lp), (chcw, lw)] + mo_free, inc=True)
            v = b.do("vector", lambda e: e.tensor_add(ccm[:], ccm[:], ctm[:]),
                     wait=[(V, v2), (V, v)], inc=True)
            v2 = b.do("vector", lambda e: e.tensor_mul(
                ctm[:], ccp[:], convW[:, 2 * QW:3 * QW]),
                wait=[(V, v)], inc=True)
            v = b.do("vector", lambda e: e.tensor_add(ccm[:], ccm[:], ctm[:]),
                     wait=[(V, v2)], inc=True)
            mo_free = [(V, v)]
            v = b.do("vector", lambda e: e.tensor_add(
                ccm[:], ccm[:], convW[:, 3 * QW:4 * QW]),
                wait=[(V, v)], inc=True)
            cw_free = [(V, v)]
            svl = b.do("scalar", lambda e, f0=f0: e.activation(
                hfull[:, f0:f0 + QW], ccm[:], Act.Silu),
                wait=[(V, v)] + hf_free, inc=True)
            cv_free = [(V, v), (Sc, svl)]
        v = b.do("vector", lambda e: e.tensor_reduce(
            qhc[:, 0:1], hfull[:], AxX, Alu.max, apply_absolute_value=True),
            wait=[(Sc, svl)], inc=True)
        v = b.do("vector", lambda e: e.tensor_scalar(
            qhc[:, 1:2], qhc[:, 0:1], 1e-5, None, Alu.max),
            wait=[(V, v)], inc=True)
        v2 = b.do("vector", lambda e: e.reciprocal(qhc[:, 2:3], qhc[:, 1:2]),
                  wait=[(V, v)], inc=True)
        v3 = b.do("vector", lambda e: e.tensor_scalar(
            qhc[:, 3:4], qhc[:, 2:3], 127.0, None, Alu.mult),
            wait=[(V, v2)], inc=True)
        v4 = b.do("vector", lambda e: e.tensor_scalar(
            qhc[:, 4:5], qhc[:, 1:2], swcol[:, 4:5], None, Alu.mult),
            wait=[(V, v)], inc=True)
        hq_dep = [(V, v3)]
        trn2l = [0, 0]
        for qt in range(4):
            vq = d_round127(hfull[:, qt * DM:(qt + 1) * DM], qhc[:, 3:4],
                            uq[:], cq[:], dtm[:], dtm[:], hq_dep)
            slq = qt % 2
            vc = b.do("vector", lambda e, slq=slq: e.tensor_copy(
                qsbw[slq][:], dtm[:]),
                wait=[(V, vq)] + qsb_free[slq], inc=True)
            cch = "qtrD" if slq == 0 else "qtrD2"
            for blk in range(KC):
                trn2l[slq] = b.dma_start(
                    cch,
                    qhT[:, (qt * KC + blk) * P:(qt * KC + blk + 1) * P],
                    qsbw[slq][:, blk * P:(blk + 1) * P],
                    wait=[(V, vc)] + qhT_cons,
                    transpose=True)
            qsb_free[slq] = [(chqtD if slq == 0 else chqtD2, trn2l[slq])]
            hq_dep = [(V, vc)]
        qhT_cons = []
        hf_free = [(G, G.n)]
        lnadds = []
        for n2 in range(2):
            tn = 0
            for kc2 in range(32):
                tn = b.do("tensor", lambda e, kc2=kc2, n2=n2: e.matmul(
                    pbank(n2), qhT[:, kc2 * P:(kc2 + 1) * P],
                    w2qres[:, kc2 * DM + n2 * QW:kc2 * DM + (n2 + 1) * QW],
                    start=(kc2 == 0), stop=(kc2 == 31)),
                    wait=(bank_free[n2] + [(chqtD, chqtD.n),
                                           (chqtD2, chqtD2.n)])
                    if kc2 == 0 else (), inc=(kc2 == 31))
            ve = b.do("vector", lambda e, n2=n2: e.tensor_scalar(
                ctm[:], pbank(n2), qhc[:, 4:5], swcol[:, 2:3],
                Alu.mult, Alu.mult),
                wait=[(T, tn)] + mo_free, inc=True)
            bank_free[n2] = [(V, ve)]
            va = b.do("vector", lambda e, t=t, n2=n2: e.tensor_add(
                xs[t][:, n2 * QW:(n2 + 1) * QW],
                xs[t][:, n2 * QW:(n2 + 1) * QW], ctm[:]),
                wait=[(V, ve)], inc=True)
            mo_free = [(V, va)]
            lnadds.append(va)
        qhT_cons = [(T, T.n)]
        vln = emit_ln_tile(xs[t][:], None, [(V, lnadds[-1])])
        b.dma_start("yst", y_out.ap()[t * P:(t + 1) * P, :], xs[t][:],
                    wait=[(V, vln)])

    b.do("sync", lambda e: e.nop(), wait=[(chy, chy.n)])
    b.finalize()
    return nc


def kernel(**inputs):
    nc = build(dev_tap=None)
    x = np.ascontiguousarray(inputs["L0_attn_x"])
    ident = np.eye(P, dtype=np.float32)
    base = {}
    for k, v in inputs.items():
        if k in ("L0_attn_x", "L0_attn_G"):
            continue
        a = np.ascontiguousarray(v)
        if k == "L0_ffn_weight":
            a = a.view(np.uint8)
        elif k == "L1_ffn_conv_w":
            a = a.reshape(-1)
        elif k == "L0_ffn_weight_scale":
            a = a.reshape(1)
        base[k] = a
    in_maps = []
    for c in range(8):
        m = dict(base)
        m["x"] = x[c]
        m["ident"] = ident
        in_maps.append(m)
    res = run_bass_kernel_spmd(nc, in_maps, core_ids=list(range(8)))
    return np.stack([r["y"] for r in res.results], axis=0)


# version 09
# speedup vs baseline: 1.0560x; 1.0560x over previous
import numpy as np
from contextlib import ExitStack

import concourse.bass as bass
import concourse.mybir as mybir
from concourse.bass_utils import run_bass_kernel_spmd

F32 = mybir.dt.float32
BF16 = mybir.dt.bfloat16
U8 = mybir.dt.uint8
Alu = mybir.AluOpType
Act = mybir.ActivationFunctionType
AxX = mybir.AxisListType.X

P = 128
S = 2048
DM = 1024
NH, DH = 16, 64
KC = DM // P
TT = S // P
NQ = 4
QW = 512
LN2 = float(np.log(2.0))


class _Chan:
    __slots__ = ("h", "n")

    def __init__(self, h):
        self.h = h
        self.n = 0


class B:

    ENGS = ("sync", "tensor", "vector", "scalar", "gpsimd")

    def __init__(self, nc, es):
        self.nc = nc
        self.es = es
        self.ops = {e: [] for e in self.ENGS}
        self.chan = {e: _Chan(es.enter_context(nc.semaphore("c_" + e)))
                     for e in self.ENGS}
        self.dma = {}
        self._seen = {}

    def dch(self, name):
        if name not in self.dma:
            self.dma[name] = _Chan(
                self.es.enter_context(self.nc.semaphore("d_" + name)))
        return self.dma[name]

    def sb(self, name, shape, dtype=F32, scope=None):
        scope = self.es if scope is None else scope
        return scope.enter_context(self.nc.sbuf_tensor(name, list(shape), dtype))

    def ps(self, name, shape):
        return self.es.enter_context(self.nc.psum_tensor(name, list(shape), F32))

    def _wait(self, ename, eng, waits):
        for w in waits:
            if w is None:
                continue
            c, n = w
            if n is None or n <= 0:
                continue
            key = (ename, id(c))
            if self._seen.get(key, -1) >= n:
                continue
            self._seen[key] = n
            eng.wait_ge(c.h, n)

    def do(self, ename, f, wait=(), inc=False):
        ch = self.chan[ename]
        prev = ch.n
        if inc:
            ch.n += 2 if ename == "scalar" else 1
        wait = tuple(wait)

        def g(eng, f=f, wait=wait, inc=inc, ch=ch, ename=ename, prev=prev):
            self._wait(ename, eng, wait)
            if inc and prev > 0:
                self._wait(ename, eng, [(ch, prev)])
            ins = f(eng)
            if inc:
                ins.then_inc(ch.h, 1)
                if ename == "scalar":
                    self._wait(ename, eng, [(ch, prev + 1)])
                    eng.nop().then_inc(ch.h, 1)
        self.ops[ename].append(g)
        return ch.n

    qmap = {}

    def dma_start(self, chan, dst, src, wait=(), transpose=False, q="sync"):
        ch = self.dch(chan)
        ch.n += 16
        if not transpose:
            q = self.qmap.get(chan, q)
        wait = tuple(wait)

        def g(eng, dst=dst, src=src, wait=wait, ch=ch, q=q, tr=transpose):
            self._wait(q, eng, wait)
            if tr:
                eng.dma_start(dst, src, transpose=True).then_inc(ch.h, 16)
            else:
                eng.dma_start(dst, src).then_inc(ch.h, 16)
        self.ops[q].append(g)
        return ch.n

    def barrier(self):
        targets = []
        for e in self.ENGS:
            self.do(e, lambda eng: eng.nop(), inc=True)
            targets.append((self.chan[e], self.chan[e].n))
        for ch in self.dma.values():
            targets.append((ch, ch.n))
        for e in self.ENGS:
            self.do(e, lambda eng: eng.nop(), wait=list(targets))

    def finalize(self):
        nc = self.nc
        with nc.Block() as blk:
            @blk.sync
            def _(eng):
                for g in self.ops["sync"]:
                    g(eng)

            @blk.tensor
            def _(eng):
                for g in self.ops["tensor"]:
                    g(eng)

            @blk.vector
            def _(eng):
                for g in self.ops["vector"]:
                    g(eng)

            @blk.scalar
            def _(eng):
                for g in self.ops["scalar"]:
                    g(eng)

            @blk.gpsimd
            def _(eng):
                for g in self.ops["gpsimd"]:
                    g(eng)


def build(dev_tap=None, dev_dump=False):
    nc = bass.Bass("TRN2", target_bir_lowering=False)
    es = ExitStack()
    b = B(nc, es)
    b.qmap = {c: "gpsimd" for c in (
        "wc0", "wc1", "xf", "vtl", "wv", "gb0", "gb1", "kvld", "kvld2",
        "pqld", "cvld", "cwld", "dld", "c4", "c_ap", "dbcl", "dbc2l",
        "ldx0", "ldx1", "kqld", "hq2", "xsl")}

    x_in = nc.dram_tensor("x", [S, DM], F32, kind="ExternalInput")
    y_out = nc.dram_tensor("y", [S, DM], F32, kind="ExternalOutput")
    ident_in = nc.dram_tensor("ident", [P, P], F32, kind="ExternalInput")
    W = {}
    for nm, shp in [
        ("L0_attn_Wq", [DM, DM]), ("L0_attn_bq", [DM]),
        ("L0_attn_Wk", [DM, DM]), ("L0_attn_bk", [DM]),
        ("L0_attn_Wv", [DM, DM]), ("L0_attn_bv", [DM]),
        ("L0_attn_Wo", [DM, DM]), ("L0_attn_bo", [DM]),
        ("L0_ffn_weight", [DM, DM]), ("L0_ffn_weight_scale", [1]),
        ("L0_ffn_bias", [DM]),
        ("L1_attn_Wq", [DM, DM]), ("L1_attn_bq", [DM]),
        ("L1_attn_gq", [DM]), ("L1_attn_betaq", [DM]),
        ("L1_attn_Wk", [DM, DM]), ("L1_attn_bk", [DM]),
        ("L1_attn_gk", [DM]), ("L1_attn_betak", [DM]),
        ("L1_attn_Wv", [DM, DM]), ("L1_attn_bv", [DM]),
        ("L1_attn_gv", [DM]), ("L1_attn_betav", [DM]),
        ("L1_attn_Wfq", [DM, DM]), ("L1_attn_bfq", [DM]),
        ("L1_attn_Wfk", [DM, DM]), ("L1_attn_bfk", [DM]),
        ("L1_attn_ga", [DM]), ("L1_attn_ba", [DM]),
        ("L1_attn_W1", [DM, 4096]), ("L1_attn_b1", [4096]),
        ("L1_attn_W2", [4096, DM]), ("L1_attn_b2", [DM]),
        ("L1_ffn_w1", [4096, DM]), ("L1_ffn_conv_w", [4096 * 3]),
        ("L1_ffn_conv_b", [4096]), ("L1_ffn_w2", [DM, 4096]),
    ]:
        dt = U8 if nm == "L0_ffn_weight" else F32
        W[nm] = nc.dram_tensor(nm, shp, dt, kind="ExternalInput")

    KIND = "ExternalOutput" if dev_dump else "Internal"
    h_xf = nc.dram_tensor("h_xf", [KC, P, S], F32, kind=KIND)
    h_wT = {w: nc.dram_tensor("h_" + w, [KC, P, DM], F32, kind="Internal")
            for w in ("wq0", "wk0", "wv0", "wo0")}
    h_qf = nc.dram_tensor("h_qf", [NH, DH, S], F32, kind=KIND)
    h_kf = nc.dram_tensor("h_kf", [NH, DH, S], F32, kind=KIND)
    h_vt = nc.dram_tensor("h_vt", [NH, S, DH], F32, kind=KIND)
    h_q2 = nc.dram_tensor("h_q2", [NH, S], F32, kind=KIND)
    h_k2 = nc.dram_tensor("h_k2", [NH, S], F32, kind=KIND)
    h_den = nc.dram_tensor("h_den", [NH, S], F32, kind=KIND)
    h_x2f = nc.dram_tensor("h_x2f", [KC, P, S], F32, kind="Internal")
    h_qraw = nc.dram_tensor("h_qraw", [KC, P, S], F32, kind=KIND)
    h_kraw = nc.dram_tensor("h_kraw", [KC, P, S], F32, kind="Internal")
    h_q1 = nc.dram_tensor("h_q1", [KC, P, S], F32, kind=KIND)
    h_k1 = nc.dram_tensor("h_k1", [KC, P, S], F32, kind=KIND)
    h_v1t = nc.dram_tensor("h_v1t", [S, DM], F32, kind=KIND)
    h_pqf = nc.dram_tensor("h_pqf", [KC, P, S], F32, kind=KIND)
    h_pkt = nc.dram_tensor("h_pkt", [S, DM], F32, kind=KIND)
    h_af = nc.dram_tensor("h_af", [KC, P, S], F32, kind=KIND)
    h_h1 = nc.dram_tensor("h_h1", [32, P, S], F32, kind=KIND)
    h_rows = nc.dram_tensor("h_rows", [8, S], F32, kind=KIND)
    h_cw = nc.dram_tensor("h_cw", [3, 4096], F32, kind=KIND)
    h_dbg = nc.dram_tensor("h_dbg", [8, P, S], F32, kind=KIND) if dev_dump else None

    idF = b.sb("idF", [P, P], F32)
    xs = [b.sb(f"xs{t}", [P, DM], F32) for t in range(TT)]
    cols = b.sb("cols", [P, 32], F32)
    cONE = b.sb("cONE", [P, 1], F32)
    cEPS = b.sb("cEPS", [P, 4], F32)

    T = b.chan["tensor"]
    V = b.chan["vector"]
    Sc = b.chan["scalar"]
    G = b.chan["vector"]

    cCH = b.dch("const")
    b.dma_start("const", idF[:], ident_in.ap())
    for i, nm in enumerate(("L0_attn_bq", "L0_attn_bk", "L0_attn_bv", "L0_attn_bo")):
        for c in range(KC):
            b.dma_start("const", cols[:, 8 * i + c:8 * i + c + 1],
                        bass.AP(W[nm], c * P, [[1, P], [1, 1]]))
    n_cols = cCH.n
    b.do("vector", lambda e: e.memset(cONE[:, 0:1], 1.0))
    b.do("vector", lambda e: e.memset(cEPS[:, 0:1], 1e-8))
    b.do("vector", lambda e: e.memset(cEPS[:, 1:2], 1e-5), inc=True)
    n_const = V.n

    ps4a = b.ps("ps4a", [P, 2048])
    ps4b = b.ps("ps4b", [P, 2048])

    def pbank(i, rows=P, w=512):
        t = ps4a if i < 4 else ps4b
        c = (i % 4) * 512
        return t[0:rows, c:c + w]

    pb = [pbank(i) for i in range(8)]
    bank_free = {i: [(V, 0)] for i in range(8)}
    bank_ready = {i: (T, 0) for i in range(8)}

    tr_ev = b.sb("tr_ev", [P, 2 * P], F32)
    lnsq = b.sb("lnsq", [P, DM], F32)
    stat = b.sb("stat", [P, 8], F32)
    xadd = b.sb("xadd", [P, DM], F32)

    esA0 = ExitStack()
    ld_a = b.sb("ld_a", [P, DM], F32, scope=esA0)
    ld_b = b.sb("ld_b", [P, DM], F32, scope=esA0)
    bufs = (ld_a, ld_b)
    spt = [b.dch("spt0"), b.dch("spt1")]
    ldx = [b.dch("ldx0"), b.dch("ldx1")]
    ev_slot_busy = {0: 0, 1: 0}

    def a0_tiles(h_dst, dst_col0, buf, bslot):
        for kc in range(KC):
            bank_i = 6 + (kc % 2)
            slot = kc % 2
            tn = b.do("tensor",
                      lambda e, kc=kc, bi=bank_i:
                      e.transpose(pb[bi][0:P, 0:P],
                                  buf[:, kc * P:(kc + 1) * P], idF[:]),
                      wait=bank_free[bank_i] + [(cCH, n_cols),
                                               (ldx[bslot], ldx[bslot].n)],
                      inc=True)
            vn = b.do("vector",
                      lambda e, bi=bank_i, sl=slot: e.tensor_copy(
                          tr_ev[:, sl * P:(sl + 1) * P], pb[bi][0:P, 0:P]),
                      wait=[(T, tn), (spt[slot], ev_slot_busy[slot])],
                      inc=True)
            bank_free[bank_i] = [(V, vn)]
            dn = b.dma_start("spt%d" % slot,
                             h_dst[kc, :, dst_col0:dst_col0 + P],
                             tr_ev[:, slot * P:(slot + 1) * P],
                             wait=[(V, vn)])
            ev_slot_busy[slot] = dn

    ld_cons = [(T, 0), (T, 0)]
    for t in range(TT):
        bslot = t % 2
        b.dma_start("ldx%d" % bslot, bufs[bslot][:],
                    x_in.ap()[t * P:(t + 1) * P, :], wait=[ld_cons[bslot]])
        a0_tiles(h_xf, t * P, bufs[bslot], bslot)
        ld_cons[bslot] = (T, T.n)
    for wi, (wname, hname) in enumerate((("L0_attn_Wq", "wq0"),
                                         ("L0_attn_Wk", "wk0"),
                                         ("L0_attn_Wv", "wv0"),
                                         ("L0_attn_Wo", "wo0"))):
        for dc in range(KC):
            bslot = dc % 2
            b.dma_start("ldx%d" % bslot, bufs[bslot][:],
                        W[wname].ap()[dc * P:(dc + 1) * P, :],
                        wait=[ld_cons[bslot]])
            a0_tiles(h_wT[hname], dc * P, bufs[bslot], bslot)
            ld_cons[bslot] = (T, T.n)

    b.barrier()

    esA0.close()
    mcol = b.sb("mcol", [P, TT * NH], F32)
    esA1 = ExitStack()
    xsl = [b.sb(f"xsl{kc}", [P, QW], F32, scope=esA1) for kc in range(KC)]
    wti = b.sb("wti", [P, 2 * DM], F32, scope=esA1)
    ev1 = b.sb("ev1", [P, QW], F32, scope=esA1)
    sq1 = b.sb("sq1", [P, QW], F32, scope=esA1)
    q2r = b.sb("q2r", [NH, S], F32, scope=esA1)
    k2r = b.sb("k2r", [NH, S], F32, scope=esA1)
    r1tmp = b.sb("r1tmp", [1, QW], F32, scope=esA1)
    mrows = b.sb("mrows", [NH, S], F32, scope=esA1)

    wslot_free = [[(T, 0)], [(T, 0)]]
    ev_free = [(T, 0)]
    sq_free = [(T, 0)]
    r1_free = [(T, 0)]

    projs = (("Q", "wq0", 0, h_qf, q2r), ("K", "wk0", 8, h_kf, k2r),
             ("V", "wv0", 16, h_vt, None))

    chx = b.dch("xsl")
    chw = [b.dch("wti0"), b.dch("wti1")]
    chqk = b.dch("spqk")
    chq2s = b.dch("q2s")
    for tq in range(NQ):
        xw = (T, T.n)
        xsl_last = 0
        for kc in range(KC):
            xsl_last = b.dma_start("xsl", xsl[kc][:],
                                   h_xf[kc, :, tq * QW:(tq + 1) * QW],
                                   wait=[xw])
        for (pj, wn, bcol0, hdst, rows) in projs:
            for dc in range(KC):
                slot = dc % 2
                wld = 0
                for kc in range(KC):
                    wld = b.dma_start(
                        "wti%d" % slot,
                        wti[:, slot * DM + kc * P:slot * DM + (kc + 1) * P],
                        h_wT[wn][kc, :, dc * P:(dc + 1) * P],
                        wait=wslot_free[slot])
                bank_i = dc % 2
                tn = 0
                for kc in range(KC):
                    tn = b.do(
                        "tensor",
                        lambda e, kc=kc, s=slot, bi=bank_i:
                        e.matmul(pb[bi][:, 0:QW],
                                 wti[:, s * DM + kc * P:s * DM + (kc + 1) * P],
                                 xsl[kc][:],
                                 start=(kc == 0), stop=(kc == KC - 1)),
                        wait=(bank_free[bank_i] + [(chw[slot], wld),
                                                   (chx, xsl_last)])
                        if kc == 0 else (),
                        inc=(kc == KC - 1))
                wslot_free[slot] = [(T, tn)]
                vn = b.do("vector",
                          lambda e, bi=bank_i, c0=bcol0 + dc:
                          e.tensor_scalar(ev1[:], pb[bi][:, 0:QW],
                                          cols[:, c0:c0 + 1], None, Alu.add),
                          wait=[(T, tn), (cCH, n_cols)] + ev_free, inc=True)
                bank_free[bank_i] = [(V, vn)]
                if pj in ("Q", "K"):
                    sn = b.do("scalar",
                              lambda e: e.activation(sq1[:], ev1[:], Act.Square),
                              wait=[(V, vn)] + sq_free, inc=True)
                    b.dma_start("spqk",
                                hdst[2 * dc, :, tq * QW:(tq + 1) * QW],
                                ev1[0:DH, :], wait=[(V, vn)])
                    d2 = b.dma_start("spqk",
                                     hdst[2 * dc + 1, :, tq * QW:(tq + 1) * QW],
                                     ev1[DH:P, :], wait=[(V, vn)])
                    ev_free = [(chqk, d2), (Sc, sn)]
                    tq2 = 0
                    for half in range(2):
                        h = 2 * dc + half
                        tq2 = b.do("tensor",
                                   lambda e, hf=half: e.matmul(
                                       pb[5][0:1, 0:QW],
                                       cONE[hf * DH:(hf + 1) * DH, 0:1],
                                       sq1[hf * DH:(hf + 1) * DH, :],
                                       start=True, stop=True),
                                   wait=[(Sc, sn)] + bank_free[5], inc=True)
                        vq2 = b.do("vector",
                                   lambda e: e.tensor_copy(r1tmp[:],
                                                           pb[5][0:1, 0:QW]),
                                   wait=[(T, tq2)] + r1_free, inc=True)
                        bank_free[5] = [(V, vq2)]
                        dsh = b.dma_start("q2s",
                                          rows[h:h + 1, tq * QW:(tq + 1) * QW],
                                          r1tmp[:], wait=[(V, vq2)])
                        r1_free = [(chq2s, dsh)]
                    sq_free = [(T, tq2)]
                else:
                    for sub in range(4):
                        bank_i2 = 6 + (sub % 2)
                        slot2 = sub % 2
                        tn2 = b.do("tensor",
                                   lambda e, s=sub, bi=bank_i2:
                                   e.transpose(pb[bi][0:P, 0:P],
                                               ev1[:, s * P:(s + 1) * P],
                                               idF[:]),
                                   wait=bank_free[bank_i2] + [(V, vn)],
                                   inc=True)
                        vn2 = b.do("vector",
                                   lambda e, bi=bank_i2, sl=slot2:
                                   e.tensor_copy(
                                       tr_ev[:, sl * P:(sl + 1) * P],
                                       pb[bi][0:P, 0:P]),
                                   wait=[(T, tn2),
                                         (spt[slot2], ev_slot_busy[slot2])],
                                   inc=True)
                        bank_free[bank_i2] = [(V, vn2)]
                        row0 = tq * QW + sub * P
                        b.dma_start("spt%d" % slot2,
                                    h_vt[2 * dc, row0:row0 + P, :],
                                    tr_ev[:, slot2 * P:slot2 * P + DH],
                                    wait=[(V, vn2)])
                        d2 = b.dma_start("spt%d" % slot2,
                                         h_vt[2 * dc + 1, row0:row0 + P, :],
                                         tr_ev[:, slot2 * P + DH:
                                               (slot2 + 1) * P],
                                         wait=[(V, vn2)])
                        ev_slot_busy[slot2] = d2
                    ev_free = [(T, T.n)]

    b.barrier()
    vq2n = b.do("vector", lambda e: e.tensor_scalar(
        q2r[:], q2r[:], -0.5, None, Alu.mult), inc=True)
    vk2n = b.do("vector", lambda e: e.tensor_scalar(
        k2r[:], k2r[:], -0.5, None, Alu.mult), inc=True)
    chhq2 = b.dch("hq2")
    dq2 = b.dma_start("hq2", h_q2.ap()[:, :], q2r[:], wait=[(V, vq2n)])
    dk2 = b.dma_start("hq2", h_k2.ap()[:, :], k2r[:], wait=[(V, vk2n)])
    msq = b.do("scalar", lambda e: e.activation(mrows[:], k2r[:], Act.Sqrt,
                                                scale=-2.0),
               wait=[(V, vk2n)], inc=True)
    b.barrier()
    for jc in range(TT):
        bank_i = 6 + (jc % 2)
        tn = b.do("tensor",
                  lambda e, j=jc, bi=bank_i:
                  e.transpose(pbank(bi, P, NH),
                              mrows[0:NH, j * P:(j + 1) * P],
                              idF[0:NH, 0:NH]),
                  wait=bank_free[bank_i] + [(Sc, msq)],
                  inc=True)
        vn = b.do("vector",
                  lambda e, j=jc, bi=bank_i:
                  e.tensor_copy(mcol[:, j * NH:(j + 1) * NH],
                                pbank(bi, P, NH)),
                  wait=[(T, tn)], inc=True)
        bank_free[bank_i] = [(V, vn)]

    b.barrier()
    if dev_dump:
        b.dma_start("dbg", h_dbg[0, :, 0:TT * NH], mcol[:])

    esA1.close()
    esA2O = ExitStack()
    obuf = [b.sb(f"obuf{c}", [P, S], F32, scope=esA2O) for c in range(KC)]
    esA2 = ExitStack()
    kaug = b.sb("kaug", [DH + 2, S], F32, scope=esA2)
    qaug = b.sb("qaug", [DH + 2, S], F32, scope=esA2)
    vt = [b.sb(f"vt{i}", [P, DH], F32, scope=esA2) for i in range(2)]
    aug = [b.sb(f"aug{i}", [P, DH + 1], F32, scope=esA2) for i in range(2)]
    ub = [b.sb(f"ub{i}", [P, S], F32, scope=esA2) for i in range(2)]
    tbw = [b.sb(f"tbw{i}", [P, S], F32, scope=esA2) for i in range(2)]
    dtmp = b.sb("dtmp", [1, S], F32, scope=esA2)
    dbc = b.sb("dbc", [DH, QW], F32, scope=esA2)
    osh = b.sb("osh", [DH, QW], F32, scope=esA2)

    vload = b.do("vector", lambda e: e.memset(kaug[DH:DH + 2, :], 1.0),
                 inc=True)
    vload = b.do("vector", lambda e: e.memset(qaug[DH:DH + 2, :], 1.0),
                 inc=True)

    chkq = b.dch("kqld")
    chvt = [b.dch("vt0"), b.dch("vt1")]
    chden = b.dch("den")
    chdbc = b.dch("dbcl")
    chosh = b.dch("osh")
    kv_free = [[(V, 0)], [(V, 0)]]
    aug_free = [[(T, 0)], [(T, 0)]]
    ub_free = [[(Sc, 0)], [(Sc, 0)]]
    tb_free = [[(T, 0)], [(T, 0)]]
    kq_cons = [(T, 0)]
    dtmp_free = {q: [(chden, 0)] for q in range(NQ)}
    dbc_free = [(V, 0)]
    osh_free = [(chosh, 0)]

    for h in range(NH):
        k1 = b.dma_start("kqld", kaug[0:DH, :], h_kf[h, :, :], wait=kq_cons)
        k2_ = b.dma_start("kqld", kaug[DH + 1:DH + 2, :],
                          h_k2.ap()[h:h + 1, :],
                          wait=[(chhq2, dk2), (V, vload)])
        q1 = b.dma_start("kqld", qaug[0:DH, :], h_qf[h, :, :], wait=kq_cons)
        q2_ = b.dma_start("kqld", qaug[DH:DH + 1, :],
                          h_q2.ap()[h:h + 1, :],
                          wait=[(chhq2, dq2), (V, vload)])
        kqn = q2_
        pv_started = False

        def a2_prep(jc):
            sl = jc % 2
            vn_ld = b.dma_start("vt%d" % sl, vt[sl][:],
                                h_vt[h, jc * P:(jc + 1) * P, :],
                                wait=kv_free[sl])
            av = b.do("vector",
                      lambda e, s=sl, j=jc, h=h: e.tensor_scalar(
                          aug[s][:, 0:DH], vt[s][:],
                          mcol[:, j * NH + h:j * NH + h + 1], None,
                          Alu.mult),
                      wait=[(chvt[sl], vn_ld), (V, vload)] + aug_free[sl],
                      inc=True)
            av2 = b.do("vector",
                       lambda e, s=sl, j=jc, h=h: e.tensor_copy(
                           aug[s][:, DH:DH + 1],
                           mcol[:, j * NH + h:j * NH + h + 1]),
                       inc=True)
            kv_free[sl] = [(V, av2)]
            return av2

        def a2_score(jc):
            ts = 0
            for tq in range(NQ):
                ts = b.do("tensor",
                          lambda e, j=jc, q=tq: e.matmul(
                              ps4a[:, q * QW:(q + 1) * QW],
                              kaug[:, j * P:(j + 1) * P],
                              qaug[:, q * QW:(q + 1) * QW],
                              start=True, stop=True),
                          wait=(bank_free[0] + [(chkq, kqn)])
                          if tq == 0 else (),
                          inc=(tq == NQ - 1))
            return ts

        av2_cur = a2_prep(0)
        ts_cur = a2_score(0)
        for jc in range(TT):
            sl = jc % 2
            s1 = b.do("scalar",
                      lambda e, s=sl: e.activation(
                          ub[s][:], ps4a[:], Act.Sqrt, scale=-2.0),
                      wait=[(T, ts_cur)] + ub_free[sl], inc=True)
            bank_free[0] = [(Sc, s1)]
            if jc + 1 < TT:
                av2_next = a2_prep(jc + 1)
                ts_next = a2_score(jc + 1)
            s2 = b.do("scalar",
                      lambda e, s=sl: e.activation(
                          ub[s][:], ub[s][:], Act.Ln, bias=cEPS[:, 0:1]),
                      wait=[(V, n_const)], inc=True)
            vr = b.do("scalar",
                      lambda e, s=sl: e.activation(
                          tbw[s][:], ub[s][:], Act.Exp, scale=-2.0),
                      wait=tb_free[sl], inc=True)
            ub_free[sl] = [(Sc, vr)]
            pvw = [(Sc, vr), (V, av2_cur)]
            if not pv_started:
                pvw = pvw + bank_free[4] + bank_free[5] + bank_free[6] \
                    + bank_free[7]
            tp = 0
            for tq in range(NQ):
                tp = b.do("tensor",
                          lambda e, s=sl, q=tq, first=(not pv_started),
                          last=(jc == TT - 1): e.matmul(
                              ps4b[0:DH + 1, q * QW:(q + 1) * QW],
                              aug[s][:], tbw[s][:, q * QW:(q + 1) * QW],
                              start=first, stop=last),
                          wait=pvw if tq == 0 else (),
                          inc=(tq == NQ - 1))
            pv_started = True
            tb_free[sl] = [(T, tp)]
            aug_free[sl] = [(T, tp)]
            if jc + 1 < TT:
                av2_cur, ts_cur = av2_next, ts_next
        kq_cons = [(T, tp)]
        bank_ready[4] = (T, tp)
        hc, half = h // 2, h % 2
        for tq in range(NQ):
            vd = b.do("vector",
                      lambda e, q=tq: e.reciprocal(
                          dtmp[:, q * QW:(q + 1) * QW],
                          ps4b[DH:DH + 1, q * QW:(q + 1) * QW]),
                      wait=[bank_ready[4]] + dtmp_free[tq], inc=True)
            dwr = b.dma_start("den",
                              h_den.ap()[h:h + 1, tq * QW:(tq + 1) * QW],
                              dtmp[:, tq * QW:(tq + 1) * QW],
                              wait=[(V, vd)])
            dtmp_free[tq] = [(chden, dwr)]
            dbn = b.dma_start("dbcl", dbc[:],
                              bass.AP(h_den, h * S + tq * QW,
                                      [[0, DH], [1, QW]]),
                              wait=[(chden, dwr)] + dbc_free)
            if half == 0:
                vo = b.do("vector",
                          lambda e, q=tq, c=hc: e.tensor_mul(
                              obuf[c][0:DH, q * QW:(q + 1) * QW],
                              ps4b[0:DH, q * QW:(q + 1) * QW], dbc[:]),
                          wait=[(chdbc, dbn)], inc=True)
                dbc_free = [(V, vo)]
            else:
                vo = b.do("vector",
                          lambda e, q=tq: e.tensor_mul(
                              osh[:], ps4b[0:DH, q * QW:(q + 1) * QW],
                              dbc[:]),
                          wait=[(chdbc, dbn)] + osh_free, inc=True)
                dbc_free = [(V, vo)]
                dsh = b.dma_start("osh",
                                  obuf[hc][DH:P, tq * QW:(tq + 1) * QW],
                                  osh[:], wait=[(V, vo)])
                osh_free = [(chosh, dsh)]
        for i in (4, 5, 6, 7):
            bank_free[i] = [(V, vo)]

    b.barrier()

    esA2.close()
    esA3 = ExitStack()
    wo_sl = b.sb("wo_sl", [P, 2 * P], F32, scope=esA3)
    wev = b.sb("wev", [P, QW], F32, scope=esA3)
    wo_free = [[(T, 0)], [(T, 0)]]
    wev_free = [(T, 0)]

    for tq in range(NQ):
        for dc in range(KC):
            bank_i = dc % 2
            tn = 0
            for ec in range(KC):
                slot = ec % 2
                wl = b.dma_start("wo%d" % slot,
                                 wo_sl[:, slot * P:(slot + 1) * P],
                                 h_wT["wo0"][ec, :, dc * P:(dc + 1) * P],
                                 wait=wo_free[slot])
                tn = b.do("tensor",
                          lambda e, ecc=ec, s=slot, bi=bank_i, q=tq: e.matmul(
                              pb[bi][:, 0:QW],
                              wo_sl[:, s * P:(s + 1) * P],
                              obuf[ecc][:, q * QW:(q + 1) * QW],
                              start=(ecc == 0), stop=(ecc == KC - 1)),
                          wait=(bank_free[bank_i] +
                                [(b.dch("wo%d" % slot), wl)])
                          if ec == 0 else [(b.dch("wo%d" % slot), wl)],
                          inc=True)
                wo_free[slot] = [(T, tn)]
            vn = b.do("vector",
                      lambda e, bi=bank_i, c0=24 + dc: e.tensor_scalar(
                          wev[:], pb[bi][:, 0:QW],
                          cols[:, c0:c0 + 1], None, Alu.add),
                      wait=[(T, tn)] + wev_free, inc=True)
            bank_free[bank_i] = [(V, vn)]
            for sub in range(4):
                t_idx = tq * 4 + sub
                bank_i2 = 6 + (sub % 2)
                tn2 = b.do("tensor",
                           lambda e, s=sub, bi=bank_i2: e.transpose(
                               pb[bi][0:P, 0:P], wev[:, s * P:(s + 1) * P],
                               idF[:]),
                           wait=bank_free[bank_i2] + [(V, vn)], inc=True)
                vn2 = b.do("vector",
                           lambda e, ti=t_idx, d=dc, bi=bank_i2:
                           e.tensor_copy(xs[ti][:, d * P:(d + 1) * P],
                                         pb[bi][0:P, 0:P]),
                           wait=[(T, tn2)], inc=True)
                bank_free[bank_i2] = [(V, vn2)]
            wev_free = [(T, T.n)]

    def emit_ln_tile(tile_ap, add_src, add_wait):
        wl = list(add_wait)
        if add_src is not None:
            va = b.do("vector",
                      lambda e: e.tensor_add(tile_ap, tile_ap, add_src),
                      wait=wl, inc=True)
            wl = [(V, va)]
        vs = b.do("vector", lambda e: e.tensor_reduce(
            stat[:, 0:1], tile_ap, AxX, Alu.add), wait=wl, inc=True)
        ss = b.do("scalar", lambda e: e.activation(
            lnsq[:], tile_ap, Act.Square, accum_out=stat[:, 1:2]),
            wait=wl, inc=True)
        b.do("vector", lambda e: e.tensor_scalar(
            stat[:, 2:3], stat[:, 0:1], 1.0 / DM, None, Alu.mult),
            wait=[(V, vs)], inc=True)
        b.do("vector", lambda e: e.tensor_mul(
            stat[:, 3:4], stat[:, 2:3], stat[:, 2:3]), inc=True)
        v3 = b.do("vector", lambda e: e.scalar_tensor_tensor(
            stat[:, 4:5], stat[:, 1:2], 1.0 / DM, stat[:, 3:4],
            Alu.mult, Alu.subtract), wait=[(Sc, ss)], inc=True)
        s2 = b.do("scalar", lambda e: e.activation(
            stat[:, 5:6], stat[:, 4:5], Act.Sqrt, bias=cEPS[:, 1:2]),
            wait=[(V, v3)], inc=True)
        vr = b.do("vector", lambda e: e.reciprocal(
            stat[:, 6:7], stat[:, 5:6]), wait=[(Sc, s2)], inc=True)
        v4 = b.do("vector", lambda e: e.tensor_scalar(
            tile_ap, tile_ap, stat[:, 2:3], stat[:, 6:7],
            Alu.subtract, Alu.mult), inc=True)
        return v4

    chxa = b.dch("xadd")
    for t in range(TT):
        xl = b.dma_start("xadd", xadd[:], x_in.ap()[t * P:(t + 1) * P, :],
                         wait=[(V, V.n)])
        emit_ln_tile(xs[t][:], xadd[:], [(chxa, xl)])

    b.barrier()

    if dev_tap == "x1":
        chy = b.dch("yst")
        for t in range(TT):
            b.dma_start("yst", y_out.ap()[t * P:(t + 1) * P, :], xs[t][:],
                        wait=[(V, V.n)])
        b.do("sync", lambda e: e.nop(), wait=[(chy, chy.n)])
        b.finalize()
        return nc

    esA3.close()
    esA2O.close()
    esB = ExitStack()
    w8T = b.sb("w8T", [P, KC * DM], BF16, scope=esB)
    biasb = b.sb("biasb", [P, DM], F32, scope=esB)
    wsc = b.sb("wsc", [P, 1], F32, scope=esB)
    cB = b.sb("cB", [P, 4], F32, scope=esB)
    u8a = b.sb("u8a", [P, DM], U8, scope=esB)
    u8b = b.sb("u8b", [P, DM], U8, scope=esB)
    qb = [b.sb(f"qb{i}", [P, DM], F32, scope=esB) for i in range(6)]
    qcs = b.sb("qcs", [P, 8], F32, scope=esB)
    xscs = b.sb("xscs", [P, TT], F32, scope=esB)
    qtb = b.sb("qtb", [P, DM], BF16, scope=esB)
    qft = b.sb("qft", [P, DM], BF16, scope=esB)
    qdmp = b.sb("qdmp", [P, DM], F32, scope=esB) if dev_dump else None

    vcb = b.do("vector", lambda e: e.memset(cB[:, 0:1], -10.0 * LN2))
    vcb = b.do("vector", lambda e: e.memset(cB[:, 1:2], 9.0 * LN2))
    vcb = b.do("vector", lambda e: e.memset(cB[:, 2:3], -9.0 * LN2), inc=True)
    vcb = b.do("vector", lambda e: e.memset(cB[:, 3:4], 448.0), inc=True)
    vcb = b.do("vector", lambda e: e.reciprocal(cB[:, 3:4], cB[:, 3:4]),
               wait=[(V, vcb)], inc=True)

    def emit_pow2(eng, dst, src, tc_, tf_, wait):
        ch = b.chan[eng]
        g = 0
        first = True
        for s in (8.0, 4.0, 2.0, 1.0):
            g = b.do(eng, lambda e, s=s: e.tensor_scalar(
                tc_, src, s, s, Alu.is_ge, Alu.mult),
                wait=wait if first else [(ch, g)], inc=True)
            g = b.do(eng, lambda e: e.tensor_sub(src, src, tc_),
                     wait=[(ch, g)], inc=True)
            g = b.do(eng, lambda e, s=s: e.tensor_scalar(
                tf_, tc_, (2.0 ** s - 1.0) / s, 1.0, Alu.mult, Alu.add),
                wait=[(ch, g)], inc=True)
            if first:
                g = b.do(eng, lambda e: e.tensor_copy(dst, tf_),
                         wait=[(ch, g)], inc=True)
            else:
                g = b.do(eng, lambda e: e.tensor_mul(dst, dst, tf_),
                         wait=[(ch, g)], inc=True)
            first = False
        return g

    chw8 = b.dch("w8ld")
    chtr8 = b.dch("w8tr")
    b.dma_start("w8ld", biasb[:], bass.AP(W["L0_ffn_bias"], 0, [[0, P], [1, DM]]))
    b.dma_start("w8ld", wsc[:], bass.AP(W["L0_ffn_weight_scale"], 0,
                                        [[0, P], [1, 1]]))
    w8_cons = [(V, 0)]
    tr_busy = [(chtr8, 0)]
    for nc2 in range(KC):
        ldn = b.dma_start("w8ld", u8a[:],
                          W["L0_ffn_weight"].ap()[nc2 * P:(nc2 + 1) * P, :],
                          wait=w8_cons)
        v = b.do("vector", lambda e: e.tensor_scalar(
            u8b[:], u8a[:], 3, 15, Alu.logical_shift_right, Alu.bitwise_and),
            wait=[(chw8, ldn)], inc=True)
        v = b.do("vector", lambda e: e.tensor_copy(qb[0][:], u8b[:]),
                 wait=[(V, v)], inc=True)
        v = b.do("vector", lambda e: e.tensor_scalar(
            u8b[:], u8a[:], 7, None, Alu.bitwise_and), wait=[(V, v)], inc=True)
        v = b.do("vector", lambda e: e.tensor_copy(qb[1][:], u8b[:]),
                 wait=[(V, v)], inc=True)
        v = b.do("vector", lambda e: e.tensor_scalar(
            u8b[:], u8a[:], 7, None, Alu.logical_shift_right),
            wait=[(V, v)], inc=True)
        vdec = b.do("vector", lambda e: e.tensor_copy(qb[2][:], u8b[:]),
                    wait=[(V, v)], inc=True)
        w8_cons = [(V, vdec)]
        v = b.do("vector", lambda e: e.tensor_scalar(
            qb[3][:], qb[0][:], 1.0, 8.0, Alu.is_ge, Alu.mult),
            wait=[(V, vdec)], inc=True)
        v = b.do("vector", lambda e: e.tensor_add(qb[1][:], qb[1][:], qb[3][:]),
                 wait=[(V, v)], inc=True)
        vbase = v
        vemax = b.do("vector", lambda e: e.tensor_scalar(
            qb[0][:], qb[0][:], 1.0, None, Alu.max), wait=[(V, v)], inc=True)
        vpw = emit_pow2("vector", qb[4][:], qb[0][:], qb[3][:], qb[5][:],
                        [(V, vemax)])
        vpw = b.do("vector", lambda e: e.tensor_scalar(
            qb[4][:], qb[4][:], 1.0 / 1024.0, None, Alu.mult),
            wait=[(V, vpw)], inc=True)
        vsgn = b.do("vector", lambda e: e.tensor_scalar(
            qb[2][:], qb[2][:], -2.0, 1.0, Alu.mult, Alu.add),
            wait=[(V, vemax)], inc=True)
        v = b.do("vector", lambda e: e.tensor_mul(qb[1][:], qb[1][:], qb[4][:]),
                 wait=[(V, vpw), (V, vsgn)], inc=True)
        v = b.do("vector", lambda e: e.tensor_mul(qb[1][:], qb[1][:], qb[2][:]),
                 wait=[(V, v)], inc=True)
        vbf = b.do("vector", lambda e: e.tensor_copy(qtb[:], qb[1][:]),
                   wait=[(V, v)] + tr_busy, inc=True)
        for kc in range(KC):
            dn = b.dma_start("w8tr",
                             w8T[:, kc * DM + nc2 * P:kc * DM + (nc2 + 1) * P],
                             qtb[:, kc * P:(kc + 1) * P],
                             wait=[(V, vbf)] + tr_busy, transpose=True)
        tr_busy = [(chtr8, dn)]

    b.barrier()

    chqtr = b.dch("qtr")
    INVLN2 = 1.0 / LN2
    qft_cons = [(T, 0)]
    qtb_free = [(chqtr, 0)]
    import os as _os
    _BT = int(_os.environ.get('BTILES', TT))
    for t in range(_BT):
        tile = xs[t][:]
        v = b.do("vector", lambda e, tile=tile: e.tensor_reduce(
            qcs[:, 0:1], tile, AxX, Alu.max, apply_absolute_value=True),
            wait=[(V, V.n), (Sc, Sc.n), (T, T.n)], inc=True)
        v = b.do("vector", lambda e: e.tensor_scalar(
            qcs[:, 1:2], qcs[:, 0:1], 1e-12, None, Alu.max),
            wait=[(V, v)], inc=True)
        vamax = v
        v = b.do("vector", lambda e: e.reciprocal(qcs[:, 2:3], qcs[:, 1:2]),
                 wait=[(V, v)], inc=True)
        v = b.do("vector", lambda e: e.tensor_scalar(
            qcs[:, 3:4], qcs[:, 2:3], 448.0, None, Alu.mult),
            wait=[(V, v)], inc=True)
        vysc = v
        b.do("vector", lambda e, t=t: e.tensor_scalar(
            xscs[:, t:t + 1], qcs[:, 1:2], cB[:, 3:4], None, Alu.mult),
            wait=[(V, vamax)], inc=True)
        vy = b.do("vector", lambda e, tile=tile: e.tensor_scalar(
            qb[0][:], tile, qcs[:, 3:4], None, Alu.mult),
            wait=[(V, vysc)] + ([(b.dch("dbg"), b.dch("dbg").n)]
                                if dev_dump else []), inc=True)
        if dev_dump and t == 0:
            b.dma_start("dbg", h_dbg[4, :, 0:DM], qb[0][:], wait=[(V, vy)])
        su = b.do("scalar", lambda e: e.activation(
            qb[1][:], qb[0][:], Act.Abs), wait=[(V, vy)], inc=True)
        vu = b.do("vector", lambda e: e.tensor_scalar(
            qb[1][:], qb[1][:], 2.0 ** -30, None, Alu.max),
            wait=[(Sc, su)], inc=True)
        sl1 = b.do("scalar", lambda e: e.activation(
            qb[1][:], qb[1][:], Act.Ln), wait=[(V, vu)], inc=True)
        g = b.do("vector", lambda e: e.tensor_scalar(
            qb[2][:], qb[1][:], INVLN2, 6.0, Alu.mult, Alu.add),
            wait=[(Sc, sl1)] + ([(b.dch("dbg"), b.dch("dbg").n)]
                                if dev_dump else []), inc=True)
        g = b.do("vector", lambda e: e.tensor_scalar(
            qb[2][:], qb[2][:], 0.0, 14.99, Alu.max, Alu.min),
            wait=[(G, g)], inc=True)
        g = b.do("vector", lambda e: e.tensor_copy(qb[3][:], qb[2][:]),
                 wait=[(G, g)], inc=True)
        for s in (8.0, 4.0, 2.0, 1.0):
            g = b.do("vector", lambda e, s=s: e.tensor_scalar(
                qb[4][:], qb[3][:], s, s, Alu.is_ge, Alu.mult),
                wait=[(G, g)] + ([(b.dch("dbg"), b.dch("dbg").n)]
                                 if dev_dump else []), inc=True)
            g = b.do("vector", lambda e: e.tensor_sub(
                qb[3][:], qb[3][:], qb[4][:]), wait=[(G, g)], inc=True)
        gfl = b.do("vector", lambda e: e.tensor_sub(
            qb[2][:], qb[2][:], qb[3][:]), wait=[(G, g)], inc=True)
        if dev_dump and t == 0:
            b.dma_start("dbg", h_dbg[5, :, 0:DM], qb[2][:], wait=[(G, gfl)])
        g = b.do("vector", lambda e: e.tensor_scalar(
            qb[3][:], qb[2][:], -1.0, 14.0, Alu.mult, Alu.add),
            wait=[(G, gfl)], inc=True)
        gpw = emit_pow2("vector", qb[4][:], qb[3][:], qb[1][:], qb[5][:],
                        [(G, g)])
        gpw = b.do("vector", lambda e: e.tensor_scalar(
            qb[3][:], qb[4][:], 1.0 / 32.0, None, Alu.mult),
            wait=[(G, gpw)], inc=True)
        vz = b.do("vector", lambda e: e.tensor_mul(
            qb[4][:], qb[0][:], qb[3][:]), wait=[(G, gpw)], inc=True)
        if dev_dump and t == 0:
            b.dma_start("dbg", h_dbg[6, :, 0:DM], qb[4][:], wait=[(V, vz)])
        sw_ = b.do("scalar", lambda e: e.activation(
            qb[5][:], qb[4][:], Act.Abs), wait=[(V, vz)], inc=True)
        vw = b.do("vector", lambda e: e.tensor_scalar(
            qb[5][:], qb[5][:], 0.5, None, Alu.add),
            wait=[(Sc, sw_)], inc=True)
        g = b.do("vector", lambda e: e.tensor_copy(qb[0][:], qb[5][:]),
                 wait=[(V, vw)] + ([(b.dch("dbg"), b.dch("dbg").n)]
                                   if dev_dump else []), inc=True)
        for s in (16.0, 8.0, 4.0, 2.0, 1.0):
            g = b.do("vector", lambda e, s=s: e.tensor_scalar(
                qb[1][:], qb[0][:], s, s, Alu.is_ge, Alu.mult),
                wait=[(G, g)], inc=True)
            g = b.do("vector", lambda e: e.tensor_sub(
                qb[0][:], qb[0][:], qb[1][:]), wait=[(G, g)], inc=True)
        g3 = b.do("vector", lambda e: e.tensor_sub(
            qb[5][:], qb[5][:], qb[0][:]), wait=[(G, g)], inc=True)
        if dev_dump and t == 0:
            b.dma_start("dbg", h_dbg[7, :, 0:DM], qb[5][:], wait=[(G, g3)])
        ssg = b.do("scalar", lambda e: e.activation(
            qb[1][:], qb[4][:], Act.Sign), wait=[(V, vz), (G, g3)], inc=True)
        gpi = emit_pow2("vector", qb[3][:], qb[2][:], qb[0][:], qb[4][:],
                        [(Sc, ssg), (G, g3)])
        gpi = b.do("vector", lambda e: e.tensor_scalar(
            qb[3][:], qb[3][:], 1.0 / 512.0, None, Alu.mult),
            wait=[(G, gpi)], inc=True)
        vq = b.do("vector", lambda e: e.tensor_mul(
            qb[5][:], qb[5][:], qb[1][:]),
            wait=[(G, g3), (Sc, ssg)] + ([(b.dch("dbg"), b.dch("dbg").n)]
                                         if dev_dump else []), inc=True)
        vq = b.do("vector", lambda e: e.tensor_mul(
            qb[5][:], qb[5][:], qb[3][:]), wait=[(V, vq), (G, gpi)],
            inc=True)
        vqc = b.do("vector", lambda e: e.tensor_copy(qtb[:], qb[5][:]),
                   wait=[(V, vq)] + qtb_free, inc=True)
        if dev_dump and t == 0:
            vdq = b.do("vector", lambda e: e.tensor_copy(qdmp[:], qtb[:]),
                       wait=[(V, vqc)], inc=True)
            b.dma_start("dbg", h_dbg[1, :, 0:DM], qdmp[:], wait=[(V, vdq)])
            b.dma_start("dbg", h_dbg[2, :, 0:DM], tile, wait=[(V, vqc)])
        trn = 0
        for kc in range(KC):
            trn = b.dma_start("qtr", qft[:, kc * P:(kc + 1) * P],
                              qtb[:, kc * P:(kc + 1) * P],
                              wait=[(V, vqc)] + qft_cons, transpose=True)
        qtb_free = [(b.dch("qtr"), trn)]
        lnadds = []
        for n2 in range(2):
            bank_i = n2
            tn = 0
            for kc in range(KC):
                tn = b.do("tensor",
                          lambda e, kc=kc, n2=n2, bi=bank_i: e.matmul(
                              pbank(bi),
                              qft[:, kc * P:(kc + 1) * P],
                              w8T[:, kc * DM + n2 * QW:kc * DM + (n2 + 1) * QW],
                              start=(kc == 0), stop=(kc == KC - 1)),
                          wait=(bank_free[bank_i] + [(b.dch("qtr"), trn),
                                                     (chtr8, chtr8.n)])
                          if kc == 0 else (),
                          inc=(kc == KC - 1))
            ve = b.do("vector",
                      lambda e, t=t, bi=bank_i: e.tensor_scalar(
                          qb[0][:, 0:QW], pbank(bi),
                          xscs[:, t:t + 1], wsc[:, 0:1],
                          Alu.mult, Alu.mult),
                      wait=[(T, tn), (chw8, chw8.n)], inc=True)
            bank_free[bank_i] = [(V, ve)]
            ve = b.do("vector", lambda e, n2=n2: e.tensor_add(
                qb[0][:, 0:QW], qb[0][:, 0:QW],
                biasb[:, n2 * QW:(n2 + 1) * QW]), wait=[(V, ve)], inc=True)
            if dev_dump and t == 0:
                vdq2 = b.do("vector",
                            lambda e, n2=n2: e.tensor_copy(
                                qdmp[:, n2 * QW:(n2 + 1) * QW],
                                qb[0][:, 0:QW]),
                            wait=[(V, ve),
                                  (b.dch("dbg"), b.dch("dbg").n)], inc=True)
                b.dma_start("dbg", h_dbg[3, :, n2 * QW:(n2 + 1) * QW],
                            qdmp[:, n2 * QW:(n2 + 1) * QW],
                            wait=[(V, vdq2)])
            va = b.do("vector", lambda e, t=t, n2=n2: e.tensor_add(
                xs[t][:, n2 * QW:(n2 + 1) * QW],
                xs[t][:, n2 * QW:(n2 + 1) * QW], qb[0][:, 0:QW]),
                wait=[(V, ve), (b.dch("dbg"), b.dch("dbg").n)]
                if dev_dump and t == 0 else [(V, ve)], inc=True)
            lnadds.append(va)
        qft_cons = [(T, T.n)]
        emit_ln_tile(xs[t][:], None, [(V, lnadds[-1])])

    b.barrier()

    if dev_tap == "x2":
        chy = b.dch("yst")
        for t in range(TT):
            b.dma_start("yst", y_out.ap()[t * P:(t + 1) * P, :], xs[t][:],
                        wait=[(V, V.n)])
        b.do("sync", lambda e: e.nop(), wait=[(chy, chy.n)])
        b.finalize()
        return nc

    esB.close()

    for t in range(TT):
        for kc in range(KC):
            bank_i = 6 + (kc % 2)
            slot = kc % 2
            tn = b.do("tensor",
                      lambda e, t=t, kc=kc, bi=bank_i: e.transpose(
                          pbank(bi, P, P), xs[t][:, kc * P:(kc + 1) * P],
                          idF[:]),
                      wait=bank_free[bank_i] + [(V, V.n)], inc=True)
            vn = b.do("vector",
                      lambda e, bi=bank_i, sl=slot: e.tensor_copy(
                          tr_ev[:, sl * P:(sl + 1) * P], pbank(bi, P, P)),
                      wait=[(T, tn), (spt[slot], ev_slot_busy[slot])],
                      inc=True)
            bank_free[bank_i] = [(V, vn)]
            dn = b.dma_start("spt%d" % slot, h_x2f[kc, :, t * P:(t + 1) * P],
                             tr_ev[:, slot * P:(slot + 1) * P],
                             wait=[(V, vn)])
            ev_slot_busy[slot] = dn

    b.barrier()

    esCmid = ExitStack()
    evc = b.sb("evc", [P, QW], F32, scope=esCmid)
    sqc = b.sb("sqc", [P, QW], F32, scope=esCmid)
    evcB = b.sb("evcB", [P, QW], F32, scope=esCmid)
    sqcB = b.sb("sqcB", [P, QW], F32, scope=esCmid)
    rtmpB = b.sb("rtmpB", [1, QW], F32, scope=esCmid)
    srows = b.sb("srows", [1, 2 * S], F32, scope=esCmid)
    rtmp = b.sb("rtmp", [1, QW], F32, scope=esCmid)
    colsC = b.sb("colsC", [P, 64], F32, scope=esCmid)
    mbc = b.sb("mbc", [P, QW], F32, scope=esCmid)
    rbc = b.sb("rbc", [P, QW], F32, scope=esCmid)
    qpc = b.sb("qpc", [P, KC], F32, scope=esCmid)
    esC1 = ExitStack()
    xf = [b.sb(f"xf{kc}", [P, QW], F32, scope=esC1) for kc in range(KC)]
    wtc = b.sb("wtc", [P, 2 * DM], F32, scope=esC1)

    chc = b.dch("c1")
    for i, nm in enumerate(("L1_attn_bq", "L1_attn_gq", "L1_attn_betaq",
                            "L1_attn_bk", "L1_attn_gk", "L1_attn_betak",
                            "L1_attn_ga", "L1_attn_ba")):
        for c in range(KC):
            b.dma_start("c1", colsC[:, 8 * i + c:8 * i + c + 1],
                        bass.AP(W[nm], c * P, [[1, P], [1, 1]]))
    nc1 = chc.n
    vz0 = b.do("vector", lambda e: e.memset(qpc[:], 0.0), inc=True)

    chxf = b.dch("xf")
    chwc = [b.dch("wc0"), b.dch("wc1")]
    wc_free = [[(T, 0)], [(T, 0)]]
    evc_free = [(Sc, 0), (b.dch("spQK2"), 0)]
    sq_free2 = [(T, 0)]
    rt_free = [(b.dch("q2s"), 0)]
    chsp2 = b.dch("spQK2")

    evsl = (evc, evcB)
    sqsl = (sqc, sqcB)
    rtsl = (rtmp, rtmpB)

    def c_proj_fmajor(wname, hraw, bcol0):
        vsz = b.do("vector", lambda e: e.memset(srows[:], 0.0),
                   wait=[(V, V.n), (T, T.n)], inc=True)
        srows_dep = [(V, vsz)]
        evfL = {0: list(evc_free), 1: list(evc_free)}
        sqfL = {0: list(sq_free2), 1: list(sq_free2)}
        rtfL = {0: list(rt_free), 1: list(rt_free)}
        for tt in range(NQ):
            xw = (T, T.n)
            xl = 0
            for kc in range(KC):
                xl = b.dma_start("xf", xf[kc][:, 0:QW],
                                 h_x2f[kc, :, tt * QW:(tt + 1) * QW],
                                 wait=[xw])
            for dc in range(KC):
                slot = dc % 2
                wld = 0
                for kc in range(KC):
                    wld = b.dma_start(
                        "wc%d" % slot,
                        wtc[:, slot * DM + kc * P:slot * DM + (kc + 1) * P],
                        W[wname].ap()[kc * P:(kc + 1) * P,
                                      dc * P:(dc + 1) * P],
                        wait=wc_free[slot])
                bank_i = dc % 2
                tn = 0
                for kc in range(KC):
                    tn = b.do("tensor",
                              lambda e, kc=kc, s=slot, bi=bank_i: e.matmul(
                                  pbank(bi),
                                  wtc[:, s * DM + kc * P:
                                      s * DM + (kc + 1) * P],
                                  xf[kc][:, 0:QW],
                                  start=(kc == 0), stop=(kc == KC - 1)),
                              wait=(bank_free[bank_i] +
                                    [(chwc[slot], wld), (chxf, xl)])
                              if kc == 0 else (), inc=(kc == KC - 1))
                wc_free[slot] = [(T, tn)]
                ev_ = evsl[slot]
                sq_ = sqsl[slot]
                rt_ = rtsl[slot]
                stb = 5 - slot
                vn = b.do("vector",
                          lambda e, bi=bank_i, c0=bcol0 + dc, ev_=ev_:
                          e.tensor_scalar(
                              ev_[:], pbank(bi), colsC[:, c0:c0 + 1], None,
                              Alu.add),
                          wait=[(T, tn), (chc, nc1)] + evfL[slot], inc=True)
                bank_free[bank_i] = [(V, vn)]
                spch = "spQK2" if slot == 0 else "spQK2b"
                d1 = b.dma_start(spch, hraw[dc, :, tt * QW:(tt + 1) * QW],
                                 ev_[:], wait=[(V, vn)])
                sn = b.do("scalar",
                          lambda e, ev_=ev_, sq_=sq_: e.activation(
                              sq_[:], ev_[:], Act.Square),
                          wait=[(V, vn)] + sqfL[slot], inc=True)
                tq2 = 0
                for (src_t, row) in ((ev_, 0), (sq_, 1)):
                    tq2 = b.do("tensor",
                               lambda e, s=src_t, stb=stb: e.matmul(
                                   pbank(stb, 1), cONE[:, 0:1], s[:],
                                   start=True, stop=True),
                               wait=([(V, vn)] if row == 0 else [(Sc, sn)])
                               + bank_free[stb], inc=True)
                    vq2 = b.do("vector",
                               lambda e, rt_=rt_, stb=stb: e.tensor_copy(
                                   rt_[:], pbank(stb, 1)),
                               wait=[(T, tq2)] + rtfL[slot], inc=True)
                    bank_free[stb] = [(V, vq2)]
                    va = b.do("vector",
                              lambda e, row=row, tt=tt, rt_=rt_:
                              e.tensor_add(
                                  srows[0:1, row * S + tt * QW:
                                        row * S + (tt + 1) * QW],
                                  srows[0:1, row * S + tt * QW:
                                        row * S + (tt + 1) * QW],
                                  rt_[:]),
                              wait=srows_dep, inc=True)
                    srows_dep = [(V, va)]
                    rtfL[slot] = [(V, va)]
                evfL[slot] = [(Sc, sn), (b.dch(spch), d1), (T, tq2)]
                sqfL[slot] = [(T, T.n)]
        evc_free[:] = evfL[0] + evfL[1]
        sq_free2[:] = sqfL[0] + sqfL[1]
        rt_free[:] = rtfL[0] + rtfL[1]
        return srows_dep

    def c_rows_stats(srows_dep):
        v1 = b.do("vector", lambda e: e.tensor_scalar(
            srows[0:1, 0:S], srows[0:1, 0:S], 1.0 / DM, None, Alu.mult),
            wait=srows_dep, inc=True)
        vlast = b.do("vector", lambda e: e.tensor_scalar(
            srows[0:1, S:2 * S], srows[0:1, S:2 * S], 1.0 / DM, None,
            Alu.mult), wait=srows_dep, inc=True)
        for tt in range(NQ):
            va = b.do("vector", lambda e, tt=tt: e.tensor_mul(
                rtmp[:], srows[0:1, tt * QW:(tt + 1) * QW],
                srows[0:1, tt * QW:(tt + 1) * QW]),
                wait=[(V, v1), (V, vlast)] + rt_free, inc=True)
            vlast = b.do("vector", lambda e, tt=tt: e.tensor_sub(
                srows[0:1, S + tt * QW:S + (tt + 1) * QW],
                srows[0:1, S + tt * QW:S + (tt + 1) * QW], rtmp[:]),
                wait=[(V, va)], inc=True)
        s1 = b.do("scalar", lambda e: e.activation(
            srows[0:1, S:2 * S], srows[0:1, S:2 * S], Act.Sqrt,
            bias=cEPS[0:1, 1:2]), wait=[(V, vlast)], inc=True)
        v5 = b.do("vector", lambda e: e.reciprocal(
            srows[0:1, S:2 * S], srows[0:1, S:2 * S]),
            wait=[(Sc, s1)], inc=True)
        d1 = b.dma_start("q2s", h_rows.ap()[0:1, :], srows[0:1, 0:S],
                         wait=[(V, v5)])
        d2 = b.dma_start("q2s", h_rows.ap()[1:2, :], srows[0:1, S:2 * S],
                         wait=[(V, v5)])
        rt_free[:] = [(V, vlast)]
        return d2

    def c_apply_fmajor(hraw, hout, gcol0, bcol0b, probe=False):
        chap = b.dch("c_ap")
        ap_free = [(V, 0)]
        for tt in range(NQ):
            dm_ = b.dma_start("c_ap", mbc[:],
                              bass.AP(h_rows, tt * QW, [[0, P], [1, QW]]),
                              wait=ap_free)
            dr_ = b.dma_start("c_ap", rbc[:],
                              bass.AP(h_rows, S + tt * QW, [[0, P], [1, QW]]),
                              wait=ap_free)
            for dc in range(KC):
                ld = b.dma_start("xf", xf[dc][:, 0:QW],
                                 hraw[dc, :, tt * QW:(tt + 1) * QW],
                                 wait=[(V, V.n)])
                v1 = b.do("vector",
                          lambda e, dc=dc: e.tensor_sub(
                              evc[:], xf[dc][:, 0:QW], mbc[:]),
                          wait=[(chxf, ld), (b.dch("c_ap"), dr_)] + evc_free,
                          inc=True)
                v2 = b.do("vector",
                          lambda e, c0=gcol0 + dc: e.scalar_tensor_tensor(
                              evc[:], evc[:], colsC[:, c0:c0 + 1], rbc[:],
                              Alu.mult, Alu.mult),
                          wait=[(V, v1)], inc=True)
                v3 = b.do("vector",
                          lambda e, c0=bcol0b + dc: e.tensor_scalar(
                              evc[:], evc[:], colsC[:, c0:c0 + 1], None,
                              Alu.add),
                          wait=[(V, v2)], inc=True)
                dsp = b.dma_start("spQK2", hout[dc, :, tt * QW:(tt + 1) * QW],
                                  evc[:], wait=[(V, v3)])
                evc_free[:] = [(chsp2, dsp)]
                if probe:
                    v4 = b.do("vector",
                              lambda e, dc=dc: e.tensor_reduce(
                                  rtmp[0:1, 0:1] if False else sqc[:, 0:1],
                                  evc[:], AxX, Alu.add),
                              wait=[(V, v3)], inc=True)
                    v5 = b.do("vector",
                              lambda e, dc=dc: e.tensor_add(
                                  qpc[:, dc:dc + 1], qpc[:, dc:dc + 1],
                                  sqc[:, 0:1]),
                              wait=[(V, v4), (V, vz0)], inc=True)
                    evc_free[:] = [(chsp2, dsp), (V, v5)]
            ap_free = [(V, V.n)]

    sd = c_proj_fmajor("L1_attn_Wq", h_qraw, 0)
    dn_ = c_rows_stats(sd)
    b.barrier()
    c_apply_fmajor(h_qraw, h_q1, 8, 16, probe=True)
    b.barrier()
    vqp = b.do("vector", lambda e: e.tensor_scalar(
        qpc[:], qpc[:], 1.0 / (S * 8.0), None, Alu.mult), inc=True)
    sd = c_proj_fmajor("L1_attn_Wk", h_kraw, 24)
    dn_ = c_rows_stats(sd)
    b.barrier()
    c_apply_fmajor(h_kraw, h_k1, 32, 40)
    b.barrier()

    esC1b = ExitStack()
    vrows = b.sb("vrows", [P, 3 * DM], F32, scope=esC1b)
    vtile = b.sb("vtile", [P, DM], F32, scope=esC1b)
    vbuf = b.sb("vbuf", [P, DM], F32, scope=esC1b)
    chvr = b.dch("vrows")
    b.dma_start("vrows", vrows[:, 0:DM],
                bass.AP(W["L1_attn_gv"], 0, [[0, P], [1, DM]]))
    b.dma_start("vrows", vrows[:, DM:2 * DM],
                bass.AP(W["L1_attn_betav"], 0, [[0, P], [1, DM]]))
    b.dma_start("vrows", vrows[:, 2 * DM:3 * DM],
                bass.AP(W["L1_attn_bv"], 0, [[0, P], [1, DM]]))
    nvr = chvr.n
    chvt2 = b.dch("vtl")
    wvres = b.sb("wvres", [P, KC * DM], F32, scope=esC1b)
    chwv = b.dch("wv")
    for kc in range(KC):
        b.dma_start("wv", wvres[:, kc * DM:(kc + 1) * DM],
                    W["L1_attn_Wv"].ap()[kc * P:(kc + 1) * P, :],
                    wait=[(T, T.n)])
    nwv = chwv.n
    vb_free = [(b.dch("spQK2"), 0)]
    for ts in range(TT):
        xl = 0
        for kc in range(KC):
            xl = b.dma_start("vtl", vtile[:, kc * P:(kc + 1) * P],
                             h_x2f[kc, :, ts * P:(ts + 1) * P],
                             wait=[(T, T.n)])
        for n2 in range(2):
            bank_i = n2
            tn = 0
            for kc in range(KC):
                tn = b.do("tensor",
                          lambda e, kc=kc, n2=n2, bi=bank_i: e.matmul(
                              pbank(bi),
                              vtile[:, kc * P:(kc + 1) * P],
                              wvres[:, kc * DM + n2 * QW:
                                    kc * DM + (n2 + 1) * QW],
                              start=(kc == 0), stop=(kc == KC - 1)),
                          wait=(bank_free[bank_i] + [(chwv, nwv),
                                                     (chvt2, xl)])
                          if kc == 0 else (), inc=(kc == KC - 1))
            vn = b.do("vector",
                      lambda e, n2=n2, bi=bank_i: e.tensor_add(
                          vbuf[:, n2 * QW:(n2 + 1) * QW], pbank(bi),
                          vrows[:, 2 * DM + n2 * QW:2 * DM + (n2 + 1) * QW]),
                      wait=[(T, tn), (chvr, nvr)] + vb_free, inc=True)
            bank_free[bank_i] = [(V, vn)]
        vs = b.do("vector", lambda e: e.tensor_reduce(
            stat[:, 0:1], vbuf[:], AxX, Alu.add), wait=[(V, vn)], inc=True)
        ss = b.do("scalar", lambda e: e.activation(
            lnsq[:], vbuf[:], Act.Square, accum_out=stat[:, 1:2]),
            wait=[(V, vn)], inc=True)
        b.do("vector", lambda e: e.tensor_scalar(
            stat[:, 2:3], stat[:, 0:1], 1.0 / DM, None, Alu.mult),
            wait=[(V, vs)], inc=True)
        b.do("vector", lambda e: e.tensor_mul(
            stat[:, 3:4], stat[:, 2:3], stat[:, 2:3]), inc=True)
        v3 = b.do("vector", lambda e: e.scalar_tensor_tensor(
            stat[:, 4:5], stat[:, 1:2], 1.0 / DM, stat[:, 3:4],
            Alu.mult, Alu.subtract), wait=[(Sc, ss)], inc=True)
        s2 = b.do("scalar", lambda e: e.activation(
            stat[:, 5:6], stat[:, 4:5], Act.Sqrt, bias=cEPS[:, 1:2]),
            wait=[(V, v3)], inc=True)
        vr_ = b.do("vector", lambda e: e.reciprocal(
            stat[:, 6:7], stat[:, 5:6]), wait=[(Sc, s2)], inc=True)
        v4 = b.do("vector", lambda e: e.tensor_scalar(
            vbuf[:], vbuf[:], stat[:, 2:3], stat[:, 6:7],
            Alu.subtract, Alu.mult), wait=[(V, vr_)], inc=True)
        v5 = b.do("vector", lambda e: e.tensor_mul(
            vbuf[:], vbuf[:], vrows[:, 0:DM]), wait=[(V, v4)], inc=True)
        v6 = b.do("vector", lambda e: e.tensor_add(
            vbuf[:], vbuf[:], vrows[:, DM:2 * DM]), wait=[(V, v5)], inc=True)
        dv = b.dma_start("spQK2", h_v1t.ap()[ts * P:(ts + 1) * P, :],
                         vbuf[:], wait=[(V, v6)])
        vb_free = [(chsp2, dv)]

    b.barrier()

    colsC2 = b.sb("colsC2", [P, 8], F32, scope=esC1b)
    for c in range(KC):
        b.dma_start("c1", colsC2[:, c:c + 1],
                    bass.AP(W["L1_attn_bfq"], c * P, [[1, P], [1, 1]]))
    nc2_ = chc.n
    for tt in range(NQ):
        xw = (T, T.n)
        xl = 0
        for kc in range(KC):
            xl = b.dma_start("xf", xf[kc][:, 0:QW],
                             h_q1[kc, :, tt * QW:(tt + 1) * QW], wait=[xw])
        for dc in range(KC):
            slot = dc % 2
            wld = 0
            for kc in range(KC):
                wld = b.dma_start(
                    "wc%d" % slot,
                    wtc[:, slot * DM + kc * P:slot * DM + (kc + 1) * P],
                    W["L1_attn_Wfq"].ap()[kc * P:(kc + 1) * P,
                                          dc * P:(dc + 1) * P],
                    wait=wc_free[slot])
            bank_i = dc % 2
            tn = 0
            for kc in range(KC):
                tn = b.do("tensor",
                          lambda e, kc=kc, s=slot, bi=bank_i: e.matmul(
                              pbank(bi),
                              wtc[:, s * DM + kc * P:s * DM + (kc + 1) * P],
                              xf[kc][:, 0:QW],
                              start=(kc == 0), stop=(kc == KC - 1)),
                          wait=(bank_free[bank_i] + [(chwc[slot], wld),
                                                     (chxf, xl)])
                          if kc == 0 else (), inc=(kc == KC - 1))
            wc_free[slot] = [(T, tn)]
            sn = b.do("scalar",
                      lambda e, bi=bank_i, c0=dc: e.activation(
                          evc[:], pbank(bi), Act.Tanh,
                          bias=colsC2[:, c0:c0 + 1]),
                      wait=[(T, tn), (chc, nc2_)] + evc_free, inc=True)
            bank_free[bank_i] = [(Sc, sn)]
            dsp = b.dma_start("spQK2", h_pqf[dc, :, tt * QW:(tt + 1) * QW],
                              evc[:], wait=[(Sc, sn)])
            evc_free[:] = [(chsp2, dsp)]

    b.dma_start("vrows", vrows[:, 0:DM],
                bass.AP(W["L1_attn_bfk"], 0, [[0, P], [1, DM]]),
                wait=[(V, V.n)])
    nvr2 = chvr.n
    for kc in range(KC):
        b.dma_start("wv", wvres[:, kc * DM:(kc + 1) * DM],
                    W["L1_attn_Wfk"].ap()[kc * P:(kc + 1) * P, :],
                    wait=[(T, T.n)])
    nwv2 = chwv.n
    for ts in range(TT):
        xl = 0
        for kc in range(KC):
            xl = b.dma_start("vtl", vtile[:, kc * P:(kc + 1) * P],
                             h_k1[kc, :, ts * P:(ts + 1) * P],
                             wait=[(T, T.n)])
        for n2 in range(2):
            bank_i = n2
            tn = 0
            for kc in range(KC):
                tn = b.do("tensor",
                          lambda e, kc=kc, n2=n2, bi=bank_i: e.matmul(
                              pbank(bi),
                              vtile[:, kc * P:(kc + 1) * P],
                              wvres[:, kc * DM + n2 * QW:
                                    kc * DM + (n2 + 1) * QW],
                              start=(kc == 0), stop=(kc == KC - 1)),
                          wait=(bank_free[bank_i] + [(chwv, nwv2),
                                                     (chvt2, xl)])
                          if kc == 0 else (), inc=(kc == KC - 1))
            vn = b.do("vector",
                      lambda e, n2=n2, bi=bank_i: e.tensor_add(
                          vbuf[:, n2 * QW:(n2 + 1) * QW], pbank(bi),
                          vrows[:, n2 * QW:(n2 + 1) * QW]),
                      wait=[(T, tn), (chvr, nvr2)] + vb_free, inc=True)
            bank_free[bank_i] = [(V, vn)]
            sn = b.do("scalar",
                      lambda e, n2=n2: e.activation(
                          vbuf[:, n2 * QW:(n2 + 1) * QW],
                          vbuf[:, n2 * QW:(n2 + 1) * QW], Act.Tanh),
                      wait=[(V, vn)], inc=True)
        dv = b.dma_start("spQK2", h_pkt.ap()[ts * P:(ts + 1) * P, :],
                         vbuf[:], wait=[(Sc, sn)])
        vb_free = [(chsp2, dv)]

    b.barrier()

    if dev_tap == "c3":
        chy = b.dch("yst")
        for t in range(TT):
            b.dma_start("yst", y_out.ap()[t * P:(t + 1) * P, :], xs[t][:],
                        wait=[(V, V.n)])
        b.do("sync", lambda e: e.nop(), wait=[(b.dch("yst"), b.dch("yst").n)])
        b.finalize()
        return nc

    esC1b.close()
    esC1.close()
    esC4 = ExitStack()
    obufC = [b.sb(f"obc{c}", [P, S], F32, scope=esC4) for c in range(KC)]
    sstat = b.sb("sstat", [NH, 8], F32, scope=esC4)
    sccol = b.sb("sccol", [P, TT * NH], F32, scope=esC4)
    s0b = b.sb("s0b", [P, NH], F32, scope=esC4)
    pk0h = b.sb("pk0h", [DH, NH], F32, scope=esC4)
    esC4a = ExitStack()
    lgr = b.sb("lgr", [NH, S], F32, scope=esC4a)
    kf2 = b.sb("kf2", [P, QW], F32, scope=esC4a)
    rt2 = b.sb("rt2", [1, QW], F32, scope=esC4a)

    chc4 = b.dch("c4")
    chq2s = b.dch("q2s")
    rt2_free = [(chq2s, 0)]
    kf_free = [(T, 0)]
    for hc in range(KC):
        for tt in range(NQ):
            kl = b.dma_start("c4", kf2[:], h_k1[hc, :, tt * QW:(tt + 1) * QW],
                             wait=kf_free)
            for half in range(2):
                h = 2 * hc + half
                tq2 = b.do("tensor",
                           lambda e, hf=half, hc=hc: e.matmul(
                               pbank(5, 1),
                               qpc[hf * DH:(hf + 1) * DH, hc:hc + 1],
                               kf2[hf * DH:(hf + 1) * DH, :],
                               start=True, stop=True),
                           wait=[(chc4, kl), (V, vqp)] + bank_free[5],
                           inc=True)
                vq2 = b.do("vector",
                           lambda e: e.tensor_copy(rt2[:], pbank(5, 1)),
                           wait=[(T, tq2)] + rt2_free, inc=True)
                bank_free[5] = [(V, vq2)]
                dsh = b.dma_start("q2s", lgr[h:h + 1, tt * QW:(tt + 1) * QW],
                                  rt2[:], wait=[(V, vq2)])
                rt2_free = [(chq2s, dsh)]
            kf_free = [(T, T.n)]
    b.barrier()
    v1 = b.do("vector", lambda e: e.tensor_reduce(
        sstat[:, 0:1], lgr[:], AxX, Alu.max), inc=True)
    v1 = b.do("vector", lambda e: e.tensor_scalar(
        sstat[:, 0:1], sstat[:, 0:1], 0.0, None, Alu.max),
        wait=[(V, v1)], inc=True)
    v1 = b.do("vector", lambda e: e.tensor_scalar(
        sstat[:, 1:2], sstat[:, 0:1], -1.0, None, Alu.mult),
        wait=[(V, v1)], inc=True)
    s1 = b.do("scalar", lambda e: e.activation(
        lgr[:], lgr[:], Act.Exp, bias=sstat[:, 1:2],
        accum_out=sstat[:, 2:3]), wait=[(V, v1)], inc=True)
    s2 = b.do("scalar", lambda e: e.activation(
        sstat[:, 3:4], sstat[:, 1:2], Act.Exp), wait=[(V, v1)], inc=True)
    v2 = b.do("vector", lambda e: e.tensor_add(
        sstat[:, 4:5], sstat[:, 2:3], sstat[:, 3:4]),
        wait=[(Sc, s1), (Sc, s2)], inc=True)
    v2 = b.do("vector", lambda e: e.reciprocal(
        sstat[:, 5:6], sstat[:, 4:5]), wait=[(V, v2)], inc=True)
    v3 = b.do("vector", lambda e: e.tensor_scalar(
        lgr[:], lgr[:], sstat[:, 5:6], None, Alu.mult),
        wait=[(V, v2)], inc=True)
    v4 = b.do("vector", lambda e: e.tensor_mul(
        sstat[:, 6:7], sstat[:, 3:4], sstat[:, 5:6]),
        wait=[(V, v2)], inc=True)
    d0 = b.dma_start("q2s", h_rows.ap()[2:3, 0:NH], sstat[:, 6:7],
                     wait=[(V, v4)] + rt2_free)
    dsb = b.dma_start("c4", s0b[:], bass.AP(h_rows, 2 * S, [[0, P], [1, NH]]),
                      wait=[(chq2s, d0)])
    for jc in range(TT):
        bank_i = 6 + (jc % 2)
        tn = b.do("tensor",
                  lambda e, j=jc, bi=bank_i: e.transpose(
                      pbank(bi, P, NH), lgr[0:NH, j * P:(j + 1) * P],
                      idF[0:NH, 0:NH]),
                  wait=bank_free[bank_i] + [(V, v3)], inc=True)
        vn = b.do("vector",
                  lambda e, j=jc, bi=bank_i: e.tensor_copy(
                      sccol[:, j * NH:(j + 1) * NH], pbank(bi, P, NH)),
                  wait=[(T, tn)], inc=True)
        bank_free[bank_i] = [(V, vn)]
    for h in range(NH):
        b.dma_start("c4", pk0h[:, h:h + 1],
                    bass.AP(W["L1_attn_bfk"], h * DH, [[1, DH], [1, 1]]))
    npk0 = chc4.n
    sp0 = b.do("scalar", lambda e: e.activation(pk0h[:], pk0h[:], Act.Tanh),
               wait=[(chc4, npk0)], inc=True)
    b.barrier()

    esC4a.close()
    esC4b = ExitStack()
    pkt2 = [b.sb(f"pkt2_{i}", [P, DM], F32, scope=esC4b) for i in range(2)]
    vt2b = [b.sb(f"vt2b_{i}", [P, DM], F32, scope=esC4b) for i in range(2)]
    apk = [b.sb(f"apk{i}", [P, DH + 1], F32, scope=esC4b) for i in range(2)]
    svb2 = [b.sb(f"svb2_{i}", [P, DH + 1], F32, scope=esC4b)
            for i in range(2)]
    kvbs = b.sb("kvbs", [P, KC * (DH + 1)], F32, scope=esC4b)
    csb = b.sb("csb", [DH, DH + 1], F32, scope=esC4b)
    ctmp = b.sb("ctmp", [DH, 2], F32, scope=esC4b)
    ccol = b.sb("ccol", [DH + 1, 1], F32, scope=esC4b)
    pqs = b.sb("pqs", [DH, QW], F32, scope=esC4b)
    ndb = b.sb("ndb", [DH + 1, QW], F32, scope=esC4b)
    dr2b = b.sb("dr2b", [1, QW], F32, scope=esC4b)
    dbc2 = b.sb("dbc2", [DH, QW], F32, scope=esC4b)
    osh2 = b.sb("osh2", [DH, QW], F32, scope=esC4b)
    chkv = b.dch("kvld")
    chkv2 = b.dch("kvld2")
    chpq = b.dch("pqld")
    chden2 = b.dch("den2")
    chdenq = b.dch("denq")
    chdb2 = b.dch("dbc2l")
    chosh2 = b.dch("osh2l")
    csb_free = [(V, 0)]
    ccol_free = [(V, 0)]
    nd_free = [(V, 0)]
    dr2_free = [(chdenq, 0)]
    db2_free = [(V, 0)]
    osh2_free = [(chosh2, 0)]
    ct_free = [(V, 0)]
    hr3_free = [(chden2, 0)]
    kvbs_free = [(T, 0), (V, 0)]
    ld_free = {0: [(V, 0)], 1: [(V, 0)]}
    slot_free = {0: [(T, 0)], 1: [(T, 0)]}
    vz1 = b.do("vector", lambda e: e.memset(apk[0][:, DH:DH + 1], 1.0),
               inc=True)
    vz1 = b.do("vector", lambda e: e.memset(apk[1][:, DH:DH + 1], 1.0),
               inc=True)
    for hp in range(2):
        tkv = {}
        for jc in range(TT):
            sl = jc % 2
            l1 = b.dma_start("kvld%d" % sl, pkt2[sl][:],
                             h_pkt.ap()[jc * P:(jc + 1) * P, :],
                             wait=ld_free[sl])
            l2 = b.dma_start("kvl2%d" % sl, vt2b[sl][:],
                             h_v1t.ap()[jc * P:(jc + 1) * P, :],
                             wait=ld_free[sl])
            avl = 0
            for h8 in range(KC):
                h = hp * KC + h8
                s2_ = h8 % 2
                a1 = b.do("vector",
                          lambda e, sl=sl, s2_=s2_, h=h: e.tensor_copy(
                              apk[s2_][:, 0:DH],
                              pkt2[sl][:, h * DH:(h + 1) * DH]),
                          wait=[(b.dch("kvld%d" % sl), l1), (V, vz1)]
                          + slot_free[s2_], inc=True)
                a2 = b.do("vector",
                          lambda e, sl=sl, s2_=s2_, h=h, j=jc:
                          e.tensor_scalar(
                              svb2[s2_][:, 0:DH],
                              vt2b[sl][:, h * DH:(h + 1) * DH],
                              sccol[:, j * NH + h:j * NH + h + 1], None,
                              Alu.mult),
                          wait=[(b.dch("kvl2%d" % sl), l2)]
                          + slot_free[s2_], inc=True)
                a3 = b.do("vector",
                          lambda e, s2_=s2_, h=h, j=jc: e.tensor_copy(
                              svb2[s2_][:, DH:DH + 1],
                              sccol[:, j * NH + h:j * NH + h + 1]),
                          wait=[(V, a2)], inc=True)
                w0 = [(V, a3), (V, a1)]
                if jc == 0:
                    w0 = w0 + bank_free[h8] + kvbs_free
                tm = b.do("tensor",
                          lambda e, s2_=s2_, h8=h8, first=(jc == 0),
                          last=(jc == TT - 1): e.matmul(
                              pbank(h8, DH + 1, DH + 1),
                              apk[s2_][:], svb2[s2_][:],
                              start=first, stop=last),
                          wait=w0, inc=True)
                slot_free[s2_] = [(T, tm)]
                tkv[h8] = tm
                avl = a3
            ld_free[sl] = [(V, avl)]
        vel = 0
        for h8 in range(KC):
            vel = b.do("vector",
                       lambda e, h8=h8: e.tensor_copy(
                           kvbs[0:DH + 1, h8 * 65:(h8 + 1) * 65],
                           pbank(h8, DH + 1, DH + 1)),
                       wait=[(T, tkv[h8])] + (kvbs_free if h8 == 0 else []),
                       inc=True)
            bank_free[h8] = [(V, vel)]
        for h8 in range(KC):
            h = hp * KC + h8
            hc, half = h // 2, h % 2
            kvsl = kvbs[0:DH, h8 * 65:(h8 + 1) * 65]
            dwr = b.dma_start("den2", h_rows.ap()[3:4, 0:DH + 1],
                              kvbs[DH:DH + 1, h8 * 65:(h8 + 1) * 65],
                              wait=[(V, vel)] + hr3_free)
            dcs = b.dma_start("den2", csb[:],
                              bass.AP(h_rows, 3 * S, [[0, DH], [1, DH + 1]]),
                              wait=[(chden2, dwr)] + csb_free)
            hr3_free = [(chden2, dcs)]
            vc1 = b.do("vector",
                       lambda e, kvsl=kvsl: e.tensor_add(
                           kvsl, kvsl, csb[:]),
                       wait=[(V, vel), (chden2, dcs)], inc=True)
            csb_free = [(V, vc1)]
            vc2 = b.do("vector",
                       lambda e, h=h: e.tensor_mul(
                           ctmp[:, 0:1], pk0h[:, h:h + 1],
                           s0b[0:DH, h:h + 1]),
                       wait=[(Sc, sp0), (chc4, dsb)] + ct_free, inc=True)
            vc3 = b.do("vector",
                       lambda e, h=h: e.tensor_add(
                           ctmp[:, 0:1], ctmp[:, 0:1], s0b[0:DH, h:h + 1]),
                       wait=[(V, vc2)], inc=True)
            vc4 = b.do("vector",
                       lambda e, h8=h8: e.tensor_add(
                           kvbs[0:DH, h8 * 65 + DH:h8 * 65 + DH + 1],
                           kvbs[0:DH, h8 * 65 + DH:h8 * 65 + DH + 1],
                           ctmp[:, 0:1]),
                       wait=[(V, vc3), (V, vc1)], inc=True)
            ct_free = [(V, vc4)]
            tcc = b.do("tensor", lambda e, kvsl=kvsl: e.matmul(
                pbank(5, DH + 1, 1), kvsl, cONE[0:DH, 0:1],
                start=True, stop=True),
                wait=[(V, vc4)] + bank_free[5], inc=True)
            vcc = b.do("vector", lambda e: e.tensor_copy(
                ccol[:], pbank(5, DH + 1, 1)), wait=[(T, tcc)] + ccol_free,
                inc=True)
            bank_free[5] = [(V, vcc)]
            ccol_free = [(V, vcc)]
            for tt in range(NQ):
                lq = b.dma_start("pqld", pqs[:],
                                 h_pqf[hc, half * DH:(half + 1) * DH,
                                       tt * QW:(tt + 1) * QW],
                                 wait=[(T, T.n)])
                tnd = b.do("tensor", lambda e, kvsl=kvsl: e.matmul(
                    pbank(4, DH + 1), kvsl, pqs[:],
                    start=True, stop=True),
                    wait=[(chpq, lq), (V, vc4)] + bank_free[4], inc=True)
                vnd = b.do("vector", lambda e: e.tensor_scalar(
                    ndb[:], pbank(4, DH + 1), ccol[:], None, Alu.add),
                    wait=[(T, tnd), (V, vcc)] + nd_free, inc=True)
                bank_free[4] = [(V, vnd)]
                vdd = b.do("vector", lambda e: e.tensor_scalar(
                    dr2b[:], ndb[DH:DH + 1, :], 1e-6, None, Alu.add),
                    wait=[(V, vnd)] + dr2_free, inc=True)
                vdr = b.do("vector", lambda e: e.reciprocal(dr2b[:], dr2b[:]),
                           wait=[(V, vdd)], inc=True)
                dwr2 = b.dma_start("denq", h_rows.ap()[4:5, 0:QW], dr2b[:],
                                   wait=[(V, vdr)])
                dr2_free = [(chdenq, dwr2)]
                dbn2 = b.dma_start("dbc2l", dbc2[:],
                                   bass.AP(h_rows, 4 * S, [[0, DH], [1, QW]]),
                                   wait=[(chdenq, dwr2)] + db2_free)
                if half == 0:
                    vo = b.do("vector",
                              lambda e, q=tt, c=hc: e.tensor_mul(
                                  obufC[c][0:DH, q * QW:(q + 1) * QW],
                                  ndb[0:DH, :], dbc2[:]),
                              wait=[(chdb2, dbn2), (V, vnd)], inc=True)
                    db2_free = [(V, vo)]
                    nd_free = [(V, vo)]
                else:
                    vo = b.do("vector",
                              lambda e: e.tensor_mul(osh2[:], ndb[0:DH, :],
                                                     dbc2[:]),
                              wait=[(chdb2, dbn2), (V, vnd)] + osh2_free,
                              inc=True)
                    db2_free = [(V, vo)]
                    nd_free = [(V, vo)]
                    dsh2 = b.dma_start("osh2l",
                                       obufC[hc][DH:P,
                                                 tt * QW:(tt + 1) * QW],
                                       osh2[:], wait=[(V, vo)])
                    osh2_free = [(chosh2, dsh2)]
        kvbs_free = [(T, T.n), (V, V.n)]

    b.barrier()
    esC4b.close()

    if dev_tap == "attno":
        chy = b.dch("yst")
        for c in range(KC):
            for t2 in range(2):
                b.dma_start("yst",
                            h_af[c, :, t2 * 1024:(t2 + 1) * 1024],
                            obufC[c][:, t2 * 1024:(t2 + 1) * 1024],
                            wait=[(V, V.n)])
        for t in range(TT):
            b.dma_start("yst", y_out.ap()[t * P:(t + 1) * P, :], xs[t][:],
                        wait=[(V, V.n)])
        b.do("sync", lambda e: e.nop(), wait=[(b.dch("yst"), b.dch("yst").n)])
        b.finalize()
        return nc

    vsz = b.do("vector", lambda e: e.memset(srows[:], 0.0),
               wait=[(V, V.n), (T, T.n)], inc=True)
    srows_dep = [(V, vsz)]
    sq5 = [(T, 0)]
    for tt in range(NQ):
        tsum = 0
        for dc in range(KC):
            tsum = b.do("tensor",
                        lambda e, dc=dc, tt=tt: e.matmul(
                            pbank(5, 1), cONE[:, 0:1],
                            obufC[dc][:, tt * QW:(tt + 1) * QW],
                            start=(dc == 0), stop=(dc == KC - 1)),
                        wait=bank_free[5] if dc == 0 else (),
                        inc=(dc == KC - 1))
        tsq = 0
        for dc in range(KC):
            sn = b.do("scalar",
                      lambda e, dc=dc, tt=tt: e.activation(
                          sqc[:], obufC[dc][:, tt * QW:(tt + 1) * QW],
                          Act.Square),
                      wait=sq5, inc=True)
            tsq = b.do("tensor",
                       lambda e, dc=dc: e.matmul(
                           pbank(4, 1), cONE[:, 0:1], sqc[:],
                           start=(dc == 0), stop=(dc == KC - 1)),
                       wait=[(Sc, sn)] + (bank_free[4] if dc == 0 else []),
                       inc=True)
            sq5 = [(T, tsq)]
        v1 = b.do("vector",
                  lambda e, tt=tt: e.tensor_copy(
                      srows[0:1, tt * QW:(tt + 1) * QW], pbank(5, 1)),
                  wait=[(T, tsum)] + srows_dep, inc=True)
        bank_free[5] = [(V, v1)]
        v2 = b.do("vector",
                  lambda e, tt=tt: e.tensor_copy(
                      srows[0:1, S + tt * QW:S + (tt + 1) * QW], pbank(4, 1)),
                  wait=[(T, tsq)], inc=True)
        bank_free[4] = [(V, v2)]
        srows_dep = [(V, v2)]
    dn5 = c_rows_stats(srows_dep)
    b.barrier()
    ap5 = [(V, 0)]
    ev5 = [(V, 0)]
    chap = b.dch("c_ap")
    for tt in range(NQ):
        dm_ = b.dma_start("c_ap", mbc[:],
                          bass.AP(h_rows, tt * QW, [[0, P], [1, QW]]),
                          wait=ap5)
        dr_ = b.dma_start("c_ap", rbc[:],
                          bass.AP(h_rows, S + tt * QW, [[0, P], [1, QW]]),
                          wait=ap5)
        vl = 0
        for dc in range(KC):
            v1 = b.do("vector",
                      lambda e, dc=dc, tt=tt: e.tensor_sub(
                          evc[:], obufC[dc][:, tt * QW:(tt + 1) * QW],
                          mbc[:]),
                      wait=[(chap, dr_)] + ev5, inc=True)
            v2 = b.do("vector",
                      lambda e, dc=dc: e.scalar_tensor_tensor(
                          evc[:], evc[:], colsC[:, 48 + dc:49 + dc], rbc[:],
                          Alu.mult, Alu.mult),
                      wait=[(V, v1)], inc=True)
            v3 = b.do("vector",
                      lambda e, dc=dc, tt=tt: e.tensor_scalar(
                          obufC[dc][:, tt * QW:(tt + 1) * QW], evc[:],
                          colsC[:, 56 + dc:57 + dc], None, Alu.add),
                      wait=[(V, v2)], inc=True)
            ev5 = [(V, v3)]
            vl = v3
        ap5 = [(V, vl)]
    b.barrier()

    if dev_tap == "a1":
        chy = b.dch("yst")
        for c in range(KC):
            for t2 in range(2):
                b.dma_start("yst",
                            h_af[c, :, t2 * 1024:(t2 + 1) * 1024],
                            obufC[c][:, t2 * 1024:(t2 + 1) * 1024],
                            wait=[(V, V.n)])
        for t in range(TT):
            b.dma_start("yst", y_out.ap()[t * P:(t + 1) * P, :], xs[t][:],
                        wait=[(V, V.n)])
        b.do("sync", lambda e: e.nop(), wait=[(b.dch("yst"), b.dch("yst").n)])
        b.finalize()
        return nc

    colsD = b.sb("colsD", [P, 40], F32, scope=esC4)
    w6 = b.sb("w6", [P, 2 * DM], F32, scope=esC4)
    for c in range(32):
        b.dma_start("c1", colsD[:, c:c + 1],
                    bass.AP(W["L1_attn_b1"], c * P, [[1, P], [1, 1]]))
    for c in range(KC):
        b.dma_start("c1", colsD[:, 32 + c:33 + c],
                    bass.AP(W["L1_attn_b2"], c * P, [[1, P], [1, 1]]))
    ncD = chc.n
    w6_free = [[(T, 0)], [(T, 0)]]
    ev6 = [(chsp2, 0)]
    for nb in range(32):
        slot = nb % 2
        wld = 0
        for kc in range(KC):
            wld = b.dma_start(
                "wc%d" % slot,
                w6[:, slot * DM + kc * P:slot * DM + (kc + 1) * P],
                W["L1_attn_W1"].ap()[kc * P:(kc + 1) * P,
                                     nb * P:(nb + 1) * P],
                wait=w6_free[slot])
        tn = 0
        for tt in range(NQ):
            bank_i = (nb * NQ + tt) % 2
            for kc in range(KC):
                tn = b.do("tensor",
                          lambda e, kc=kc, s=slot, bi=bank_i, tt=tt:
                          e.matmul(
                              pbank(bi),
                              w6[:, s * DM + kc * P:s * DM + (kc + 1) * P],
                              obufC[kc][:, tt * QW:(tt + 1) * QW],
                              start=(kc == 0), stop=(kc == KC - 1)),
                          wait=(bank_free[bank_i] +
                                [(chwc[slot], wld)]) if kc == 0 else (),
                          inc=(kc == KC - 1))
            sn = b.do("scalar",
                      lambda e, bi=bank_i, nb=nb: e.activation(
                          evc[:], pbank(bi), Act.Gelu,
                          bias=colsD[:, nb:nb + 1]),
                      wait=[(T, tn), (chc, ncD)] + ev6, inc=True)
            bank_free[bank_i] = [(Sc, sn)]
            dsp = b.dma_start("spQK2", h_h1[nb, :, tt * QW:(tt + 1) * QW],
                              evc[:], wait=[(Sc, sn)])
            ev6 = [(chsp2, dsp)]
        w6_free[slot] = [(T, tn)]
    b.barrier()

    gbuf = [b.sb(f"gbuf{i}", [P, QW], F32, scope=esC4) for i in range(2)]
    chgb = [b.dch("gb0"), b.dch("gb1")]
    gb_free = [[(T, 0)], [(T, 0)]]
    w7_free = [[(T, 0)], [(T, 0)]]
    ev7 = [(T, 0)]

    def psub(nb2, j):
        t = ps4a if nb2 < 4 else ps4b
        c = (nb2 % 4) * 512 + j * P
        return t[0:P, c:c + P]

    for tt in range(NQ):
        tl = 0
        for kc2 in range(32):
            slot = kc2 % 2
            gl = b.dma_start("gb%d" % slot, gbuf[slot][:],
                             h_h1[kc2, :, tt * QW:(tt + 1) * QW],
                             wait=gb_free[slot])
            wld = 0
            for nb2 in range(KC):
                wld = b.dma_start(
                    "wc%d" % slot,
                    w6[:, slot * DM + nb2 * P:slot * DM + (nb2 + 1) * P],
                    W["L1_attn_W2"].ap()[kc2 * P:(kc2 + 1) * P,
                                         nb2 * P:(nb2 + 1) * P],
                    wait=w7_free[slot])
            for nb2 in range(KC):
                tl = b.do("tensor",
                          lambda e, s=slot, nb2=nb2, kc2=kc2: e.matmul(
                              pbank(nb2),
                              w6[:, s * DM + nb2 * P:s * DM + (nb2 + 1) * P],
                              gbuf[s][:],
                              start=(kc2 == 0), stop=(kc2 == 31)),
                          wait=(([(chwc[slot], wld), (chgb[slot], gl)]
                                 if nb2 == 0 else []) +
                                (bank_free[nb2] if kc2 == 0 else [])),
                          inc=(nb2 == KC - 1))
            gb_free[slot] = [(T, tl)]
            w7_free[slot] = [(T, tl)]
        for nb2 in range(KC):
            ve = b.do("vector",
                      lambda e, nb2=nb2: e.tensor_scalar(
                          evc[:], pbank(nb2), colsD[:, 32 + nb2:33 + nb2],
                          None, Alu.add),
                      wait=[(T, tl), (chc, ncD)] + ev7, inc=True)
            va = b.do("vector",
                      lambda e, nb2=nb2, tt=tt: e.tensor_add(
                          evc[:], evc[:],
                          obufC[nb2][:, tt * QW:(tt + 1) * QW]),
                      wait=[(V, ve)], inc=True)
            tj = 0
            vstt = 0
            for j in range(NQ):
                tj = b.do("tensor",
                          lambda e, nb2=nb2, j=j: e.transpose(
                              pbank(nb2, P, P), evc[:, j * P:(j + 1) * P],
                              idF[:]),
                          wait=[(V, va)] if j == 0 else [(V, vstt)],
                          inc=True)
                vstt = b.do("vector",
                            lambda e, nb2=nb2, tt=tt, j=j:
                            e.scalar_tensor_tensor(
                                xs[tt * NQ + j][:, nb2 * P:(nb2 + 1) * P],
                                xs[tt * NQ + j][:, nb2 * P:(nb2 + 1) * P],
                                2.0, pbank(nb2, P, P),
                                Alu.mult, Alu.add),
                            wait=[(T, tj)], inc=True)
            bank_free[nb2] = [(V, vstt)]
            ev7 = [(T, tj), (V, vstt)]

    for t in range(TT):
        emit_ln_tile(xs[t][:], None, [(V, V.n), (T, T.n)])
    b.barrier()

    if dev_tap == "x3":
        chy = b.dch("yst")
        for t in range(TT):
            b.dma_start("yst", y_out.ap()[t * P:(t + 1) * P, :], xs[t][:],
                        wait=[(V, V.n)])
        b.do("sync", lambda e: e.nop(), wait=[(b.dch("yst"), b.dch("yst").n)])
        b.finalize()
        return nc

    esC4.close()
    esCmid.close()
    esD = ExitStack()
    idB = b.sb("idB", [P, P], BF16, scope=esD)
    swcol = b.sb("swcol", [P, 6], F32, scope=esD)
    macc = b.sb("macc", [P, 1], F32, scope=esD)
    mtmp = b.sb("mtmp", [P, 1], F32, scope=esD)
    m1c = b.sb("m1c", [1, 2], F32, scope=esD)
    vid = b.do("vector", lambda e: e.tensor_copy(idB[:], idF[:]), inc=True)
    v127 = b.do("vector", lambda e: e.memset(swcol[:, 4:5], 127.0), inc=True)
    v127 = b.do("vector", lambda e: e.reciprocal(
        swcol[:, 4:5], swcol[:, 4:5]), wait=[(V, v127)], inc=True)
    chD = b.dch("dld")
    chq2s = b.dch("q2s")

    esD0 = ExitStack()
    cwf = b.sb("cwf", [1, 12288], F32, scope=esD0)
    cpc = b.sb("cpc", [1, 4096], F32, scope=esD0)
    ldc = b.dma_start("dld", cwf[0:1, :],
                      bass.AP(W["L1_ffn_conv_w"], 0, [[12288, 1], [1, 12288]]))
    cp_free = [(b.dch("dld"), 0)]
    for j in range(3):
        vj = b.do("vector", lambda e, j=j: e.tensor_copy(
            cpc[0:1, :], cwf[0:1, j:12288:3]),
            wait=[(chD, ldc)] + cp_free, inc=True)
        dj = b.dma_start("dld", h_cw.ap()[j:j + 1, :], cpc[0:1, :],
                         wait=[(V, vj)])
        cp_free = [(chD, dj)]
    b.barrier()
    esD0.close()

    def d_weight_scale(wname, nt, wid, stage, col0):
        mz = b.do("vector", lambda e: e.memset(macc[:], 0.0),
                  wait=[(V, V.n)], inc=True)
        mdep = [(V, mz)]
        st_free = [(Sc, 0)]
        nh = wid // 2048 if wid > 2048 else 1
        cw = wid // nh
        for ti in range(nt):
            for hf in range(nh):
                ld = b.dma_start("dld", stage[:, 0:cw],
                                 W[wname].ap()[ti * P:(ti + 1) * P,
                                               hf * cw:(hf + 1) * cw],
                                 wait=st_free)
                sa = b.do("scalar",
                          lambda e, cw=cw: e.activation(
                              stage[:, cw:2 * cw], stage[:, 0:cw], Act.Abs,
                              accum_out=mtmp[:, 0:1]),
                          wait=[(chD, ld)] + mdep, inc=True)
                va = b.do("vector", lambda e: e.tensor_add(
                    macc[:], macc[:], mtmp[:, 0:1]),
                    wait=[(Sc, sa)] + mdep, inc=True)
                mdep = [(V, va)]
                st_free = [(Sc, sa)]
        tmm = b.do("tensor", lambda e: e.matmul(
            pbank(5, 1, 1), macc[:], cONE[:, 0:1], start=True, stop=True),
            wait=mdep + bank_free[5], inc=True)
        vm = b.do("vector", lambda e, nt=nt, wid=wid: e.tensor_scalar(
            m1c[0:1, 0:1], pbank(5, 1, 1), 1.0 / (nt * P * wid), None,
            Alu.mult), wait=[(T, tmm)], inc=True)
        bank_free[5] = [(V, vm)]
        vm = b.do("vector", lambda e: e.tensor_scalar(
            m1c[0:1, 0:1], m1c[0:1, 0:1], 1e-5, None, Alu.max),
            wait=[(V, vm)], inc=True)
        vm2 = b.do("vector", lambda e: e.reciprocal(
            m1c[0:1, 1:2], m1c[0:1, 0:1]), wait=[(V, vm)], inc=True)
        dsw = b.dma_start("q2s", h_rows.ap()[5:6, 0:2], m1c[0:1, 0:2],
                          wait=[(V, vm2), (chq2s, chq2s.n)])
        dbc = b.dma_start("dld", swcol[:, col0:col0 + 2],
                          bass.AP(h_rows, 5 * S, [[0, P], [1, 2]]),
                          wait=[(chq2s, dsw)])
        return dbc

    def d_ternary(src_ap, ta, tb, tc, sw_col, wait):
        g = b.do("vector", lambda e: e.tensor_scalar(
            ta, src_ap, sw_col, None, Alu.mult), wait=wait, inc=True)
        g = b.do("vector", lambda e: e.tensor_scalar(
            ta, ta, -2.0, 2.0, Alu.max, Alu.min), wait=[(G, g)], inc=True)
        g = b.do("vector", lambda e: e.tensor_scalar(
            ta, ta, 2.5, None, Alu.add), wait=[(G, g)], inc=True)
        g = b.do("vector", lambda e: e.tensor_copy(tb, ta),
                 wait=[(G, g)], inc=True)
        for s in (4.0, 2.0, 1.0):
            g = b.do("vector", lambda e, s=s: e.tensor_scalar(
                tc, tb, s, s, Alu.is_ge, Alu.mult), wait=[(G, g)], inc=True)
            g = b.do("vector", lambda e: e.tensor_sub(tb, tb, tc),
                     wait=[(G, g)], inc=True)
        g = b.do("vector", lambda e: e.tensor_sub(ta, ta, tb),
                 wait=[(G, g)], inc=True)
        g = b.do("vector", lambda e: e.tensor_scalar(
            ta, ta, -2.0, None, Alu.add), wait=[(G, g)], inc=True)
        g = b.do("vector", lambda e: e.tensor_scalar(
            ta, ta, -1.0, 1.0, Alu.max, Alu.min), wait=[(G, g)], inc=True)
        return g

    def d_round127(src_ap, sc_col, ta, tb, tc, dst_ap, wait):
        g = b.do("vector", lambda e: e.tensor_scalar(
            ta, src_ap, sc_col, None, Alu.mult), wait=wait, inc=True)
        g = b.do("vector", lambda e: e.tensor_scalar(
            ta, ta, 128.5, None, Alu.add), wait=[(G, g)], inc=True)
        g = b.do("vector", lambda e: e.tensor_copy(tb, ta),
                 wait=[(G, g)], inc=True)
        for s in (128.0, 64.0, 32.0, 16.0, 8.0, 4.0, 2.0, 1.0):
            g = b.do("vector", lambda e, s=s: e.tensor_scalar(
                tc, tb, s, s, Alu.is_ge, Alu.mult), wait=[(G, g)], inc=True)
            g = b.do("vector", lambda e: e.tensor_sub(
                tb, tb, tc), wait=[(G, g)], inc=True)
        g = b.do("vector", lambda e: e.tensor_sub(ta, ta, tb),
                 wait=[(G, g)], inc=True)
        g = b.do("vector", lambda e: e.tensor_scalar(
            ta, ta, -128.0, None, Alu.add), wait=[(G, g)], inc=True)
        v = b.do("vector", lambda e: e.tensor_copy(dst_ap, ta),
                 wait=[(G, g)], inc=True)
        return v

    esD1 = ExitStack()
    w1qres = b.sb("w1qres", [P, KC * 4096], BF16, scope=esD1)
    wst = b.sb("wst", [P, 2 * DM], F32, scope=esD1)
    wq1 = b.sb("wq1", [P, DM], F32, scope=esD1)
    wq2 = b.sb("wq2", [P, DM], F32, scope=esD1)
    wq3 = b.sb("wq3", [P, DM], F32, scope=esD1)
    dbc1 = d_weight_scale("L1_ffn_w1", 32, DM, wst, 0)
    qw_free = [(G, 0)]
    wqb_free = [(T, 0)]
    for ht in range(32):
        ld = b.dma_start("dld", wst[:, 0:DM],
                         W["L1_ffn_w1"].ap()[ht * P:(ht + 1) * P, :],
                         wait=qw_free)
        vq = d_ternary(wst[:, 0:DM], wq1[:], wq2[:], wq3[:],
                       swcol[:, 1:2],
                       [(chD, ld), (chD, dbc1)] + wqb_free)
        qw_free = [(G, G.n)]
        if dev_dump and ht == 0:
            dn_dbg = b.dma_start("dbg", h_dbg[0, :, 0:DM], wq1[:],
                                 wait=[(G, vq)])
            qw_free = qw_free + [(b.dch("dbg"), dn_dbg)]
        tn = 0
        for dc in range(KC):
            bi = 6 + (dc % 2)
            tn = b.do("tensor", lambda e, dc=dc, bi=bi: e.transpose(
                pbank(bi, P, P), wq1[:, dc * P:(dc + 1) * P], idF[:]),
                wait=[(G, vq)] + bank_free[bi], inc=True)
            vw = b.do("vector",
                      lambda e, dc=dc, ht=ht, bi=bi: e.tensor_copy(
                          w1qres[:, dc * 4096 + ht * P:
                                 dc * 4096 + (ht + 1) * P],
                          pbank(bi, P, P)),
                      wait=[(T, tn)], inc=True)
            bank_free[bi] = [(V, vw)]
        wqb_free = [(T, tn)]
    b.barrier()

    qxc = b.sb("qxc", [P, 8], F32, scope=esD1)
    qxb = b.sb("qxb", [P, DM], BF16, scope=esD1)
    qxT = b.sb("qxT", [P, DM], BF16, scope=esD1)
    hb1 = b.sb("hb1", [P, QW], F32, scope=esD1)
    chqtD = b.dch("qtrD")
    chqtD2 = b.dch("qtrD2")
    chspD = b.dch("spD")
    qxT_cons = [(T, 0)]
    hb1_free = [(chspD, 0)]
    qx_free = [(b.dch("qtrD"), 0)]
    dbg_gate = []
    for t in range(TT):
        v = b.do("vector", lambda e, t=t: e.tensor_reduce(
            qxc[:, 0:1], xs[t][:], AxX, Alu.max, apply_absolute_value=True),
            wait=[(V, V.n), (G, G.n)] + dbg_gate, inc=True)
        v = b.do("vector", lambda e: e.tensor_scalar(
            qxc[:, 1:2], qxc[:, 0:1], 1e-5, None, Alu.max),
            wait=[(V, v)], inc=True)
        v2 = b.do("vector", lambda e: e.reciprocal(qxc[:, 2:3], qxc[:, 1:2]),
                  wait=[(V, v)], inc=True)
        v3 = b.do("vector", lambda e: e.tensor_scalar(
            qxc[:, 3:4], qxc[:, 2:3], 127.0, None, Alu.mult),
            wait=[(V, v2)], inc=True)
        v4 = b.do("vector", lambda e: e.tensor_scalar(
            qxc[:, 4:5], qxc[:, 1:2], swcol[:, 4:5], None, Alu.mult),
            wait=[(V, v), (V, v127)], inc=True)
        vq = d_round127(xs[t][:], qxc[:, 3:4], wq1[:], wq2[:], wq3[:],
                        qxb[:], [(V, v3)] + qx_free)
        if dev_dump and t == 0:
            dn_dbg = b.dma_start("dbg", h_dbg[1, :, 0:DM], wq1[:],
                                 wait=[(G, G.n)])
            dn_dbg = b.dma_start("dbg", h_dbg[2, :, 0:5], qxc[:, 0:5],
                                 wait=[(V, vq)])
            qx_free = qx_free + [(b.dch("dbg"), dn_dbg)]
            dbg_gate = [(b.dch("dbg"), dn_dbg)]
        trn = 0
        for kc in range(KC):
            trn = b.dma_start("qtrD", qxT[:, kc * P:(kc + 1) * P],
                              qxb[:, kc * P:(kc + 1) * P],
                              wait=[(V, vq)] + qxT_cons, transpose=True)
        qx_free = [(chqtD, trn)]
        for hb in range(KC):
            tn = 0
            for kc in range(KC):
                tn = b.do("tensor", lambda e, kc=kc, hb=hb: e.matmul(
                    pbank(hb), qxT[:, kc * P:(kc + 1) * P],
                    w1qres[:, kc * 4096 + hb * QW:kc * 4096 + (hb + 1) * QW],
                    start=(kc == 0), stop=(kc == KC - 1)),
                    wait=(bank_free[hb] + [(chqtD, trn)]) if kc == 0 else (),
                    inc=(kc == KC - 1))
            ve = b.do("vector", lambda e, hb=hb: e.tensor_scalar(
                hb1[:], pbank(hb), qxc[:, 4:5], swcol[:, 0:1],
                Alu.mult, Alu.mult),
                wait=[(T, tn)] + hb1_free, inc=True)
            bank_free[hb] = [(V, ve)]
            dsp = b.dma_start("spD",
                              bass.AP(h_h1, (hb * S + t * P) * QW,
                                      [[QW, P], [1, QW]]),
                              hb1[:], wait=[(V, ve)])
            hb1_free = [(chspD, dsp)]
        qxT_cons = [(T, T.n)]
    b.barrier()
    esD1.close()

    esD2 = ExitStack()
    w2qres = b.sb("w2qres", [P, 32 * DM], BF16, scope=esD2)
    esD2p = ExitStack()
    wst2 = b.sb("wst2", [P, 4096], F32, scope=esD2p)
    wqa = b.sb("wqa", [P, 2048], F32, scope=esD2p)
    wqc = b.sb("wqc", [P, 2048], F32, scope=esD2p)
    wqd = b.sb("wqd", [P, 2048], F32, scope=esD2p)
    dbc2_ = d_weight_scale("L1_ffn_w2", 8, 4096, wst2, 2)
    qw_free = [(G, G.n)]
    wqb2_free = [(T, 0)]
    for dt in range(KC):
        for hf in range(2):
            ld = b.dma_start("dld", wst2[:, 0:2048],
                             W["L1_ffn_w2"].ap()[dt * P:(dt + 1) * P,
                                                 hf * 2048:(hf + 1) * 2048],
                             wait=qw_free)
            vq = d_ternary(wst2[:, 0:2048], wqa[:], wqc[:], wqd[:],
                           swcol[:, 3:4],
                           [(chD, ld), (chD, dbc2_)] + wqb2_free)
            qw_free = [(G, G.n)]
            tn = 0
            for blk in range(16):
                kc2 = hf * 16 + blk
                bi = 6 + (blk % 2)
                tn = b.do("tensor", lambda e, blk=blk, bi=bi: e.transpose(
                    pbank(bi, P, P), wqa[:, blk * P:(blk + 1) * P], idF[:]),
                    wait=[(G, vq)] + bank_free[bi], inc=True)
                vw = b.do("vector",
                          lambda e, kc2=kc2, dt=dt, bi=bi: e.tensor_copy(
                              w2qres[:, kc2 * DM + dt * P:
                                     kc2 * DM + (dt + 1) * P],
                              pbank(bi, P, P)),
                          wait=[(T, tn)], inc=True)
                bank_free[bi] = [(V, vw)]
            wqb2_free = [(T, tn)]
    b.barrier()
    esD2p.close()

    convW = b.sb("convW", [P, 4 * QW], F32, scope=esD2)
    ccm = b.sb("ccm", [P, QW], F32, scope=esD2)
    ccc = b.sb("ccc", [P, QW], F32, scope=esD2)
    ccp = b.sb("ccp", [P, QW], F32, scope=esD2)
    ctm = b.sb("ctm", [P, QW], F32, scope=esD2)
    hfull = b.sb("hfull", [P, 4096], F32, scope=esD2)
    uq = b.sb("uq", [P, DM], F32, scope=esD2)
    cq = b.sb("cq", [P, DM], F32, scope=esD2)
    dtm = b.sb("dtm", [P, DM], F32, scope=esD2)
    qsbw = [b.sb(f"qsbw{i}", [P, DM], BF16, scope=esD2) for i in range(2)]
    qhT = b.sb("qhT", [P, 4096], BF16, scope=esD2)
    qhc = b.sb("qhc", [P, 8], F32, scope=esD2)
    chcw = b.dch("cwld")
    chcv = b.dch("cvld")
    chy = b.dch("yst")
    cw_free = [(V, 0)]
    cv_free = [(V, 0)]
    hf_free = [(V, 0), (G, 0)]
    qhT_cons = [(T, 0)]
    qsb_free = {0: [(chqtD, chqtD.n)], 1: [(chqtD2, chqtD2.n)]}
    mo_free = [(V, 0)]
    conv_w = W["L1_ffn_conv_w"]
    conv_b = W["L1_ffn_conv_b"]
    for t in range(TT):
        svl = 0
        for e8 in range(KC):
            f0 = e8 * QW
            lw = 0
            for j in range(3):
                lw = b.dma_start("cwld", convW[:, j * QW:(j + 1) * QW],
                                 bass.AP(h_cw, j * 4096 + f0,
                                         [[0, P], [1, QW]]),
                                 wait=cw_free)
            lw = b.dma_start("cwld", convW[:, 3 * QW:4 * QW],
                             bass.AP(conv_b, f0, [[0, P], [1, QW]]),
                             wait=cw_free)
            blk0 = e8 * S * QW
            if t == 0:
                vz = b.do("vector", lambda e: e.memset(ccm[:], 0.0),
                          wait=cv_free, inc=True)
                lm = b.dma_start("cvld", ccm[1:P, :],
                                 bass.AP(h_h1, blk0, [[QW, P - 1], [1, QW]]),
                                 wait=[(V, vz)])
            else:
                lm = b.dma_start("cvld", ccm[:],
                                 bass.AP(h_h1, blk0 + (t * P - 1) * QW,
                                         [[QW, P], [1, QW]]),
                                 wait=cv_free)
            lc = b.dma_start("cvld", ccc[:],
                             bass.AP(h_h1, blk0 + t * P * QW,
                                     [[QW, P], [1, QW]]),
                             wait=cv_free)
            if t == TT - 1:
                vz = b.do("vector", lambda e: e.memset(ccp[:], 0.0),
                          wait=cv_free, inc=True)
                lp = b.dma_start("cvld", ccp[0:P - 1, :],
                                 bass.AP(h_h1, blk0 + (t * P + 1) * QW,
                                         [[QW, P - 1], [1, QW]]),
                                 wait=[(V, vz)])
            else:
                lp = b.dma_start("cvld", ccp[:],
                                 bass.AP(h_h1, blk0 + (t * P + 1) * QW,
                                         [[QW, P], [1, QW]]),
                                 wait=cv_free)
            v = b.do("vector", lambda e: e.tensor_mul(
                ccm[:], ccm[:], convW[:, 0:QW]),
                wait=[(chcv, lp), (chcw, lw)], inc=True)
            v2 = b.do("vector", lambda e: e.tensor_mul(
                ctm[:], ccc[:], convW[:, QW:2 * QW]),
                wait=[(chcv, lp), (chcw, lw)] + mo_free, inc=True)
            v = b.do("vector", lambda e: e.tensor_add(ccm[:], ccm[:], ctm[:]),
                     wait=[(V, v2), (V, v)], inc=True)
            v2 = b.do("vector", lambda e: e.tensor_mul(
                ctm[:], ccp[:], convW[:, 2 * QW:3 * QW]),
                wait=[(V, v)], inc=True)
            v = b.do("vector", lambda e: e.tensor_add(ccm[:], ccm[:], ctm[:]),
                     wait=[(V, v2)], inc=True)
            mo_free = [(V, v)]
            v = b.do("vector", lambda e: e.tensor_add(
                ccm[:], ccm[:], convW[:, 3 * QW:4 * QW]),
                wait=[(V, v)], inc=True)
            cw_free = [(V, v)]
            svl = b.do("scalar", lambda e, f0=f0: e.activation(
                hfull[:, f0:f0 + QW], ccm[:], Act.Silu),
                wait=[(V, v)] + hf_free, inc=True)
            cv_free = [(V, v), (Sc, svl)]
        v = b.do("vector", lambda e: e.tensor_reduce(
            qhc[:, 0:1], hfull[:], AxX, Alu.max, apply_absolute_value=True),
            wait=[(Sc, svl)], inc=True)
        v = b.do("vector", lambda e: e.tensor_scalar(
            qhc[:, 1:2], qhc[:, 0:1], 1e-5, None, Alu.max),
            wait=[(V, v)], inc=True)
        v2 = b.do("vector", lambda e: e.reciprocal(qhc[:, 2:3], qhc[:, 1:2]),
                  wait=[(V, v)], inc=True)
        v3 = b.do("vector", lambda e: e.tensor_scalar(
            qhc[:, 3:4], qhc[:, 2:3], 127.0, None, Alu.mult),
            wait=[(V, v2)], inc=True)
        v4 = b.do("vector", lambda e: e.tensor_scalar(
            qhc[:, 4:5], qhc[:, 1:2], swcol[:, 4:5], None, Alu.mult),
            wait=[(V, v)], inc=True)
        hq_dep = [(V, v3)]
        trn2l = [0, 0]
        for qt in range(4):
            vq = d_round127(hfull[:, qt * DM:(qt + 1) * DM], qhc[:, 3:4],
                            uq[:], cq[:], dtm[:], dtm[:], hq_dep)
            slq = qt % 2
            vc = b.do("vector", lambda e, slq=slq: e.tensor_copy(
                qsbw[slq][:], dtm[:]),
                wait=[(V, vq)] + qsb_free[slq], inc=True)
            cch = "qtrD" if slq == 0 else "qtrD2"
            for blk in range(KC):
                trn2l[slq] = b.dma_start(
                    cch,
                    qhT[:, (qt * KC + blk) * P:(qt * KC + blk + 1) * P],
                    qsbw[slq][:, blk * P:(blk + 1) * P],
                    wait=[(V, vc)] + qhT_cons,
                    transpose=True)
            qsb_free[slq] = [(chqtD if slq == 0 else chqtD2, trn2l[slq])]
            hq_dep = [(V, vc)]
        qhT_cons = []
        hf_free = [(G, G.n)]
        lnadds = []
        for n2 in range(2):
            tn = 0
            for kc2 in range(32):
                tn = b.do("tensor", lambda e, kc2=kc2, n2=n2: e.matmul(
                    pbank(n2), qhT[:, kc2 * P:(kc2 + 1) * P],
                    w2qres[:, kc2 * DM + n2 * QW:kc2 * DM + (n2 + 1) * QW],
                    start=(kc2 == 0), stop=(kc2 == 31)),
                    wait=(bank_free[n2] + [(chqtD, chqtD.n),
                                           (chqtD2, chqtD2.n)])
                    if kc2 == 0 else (), inc=(kc2 == 31))
            ve = b.do("vector", lambda e, n2=n2: e.tensor_scalar(
                ctm[:], pbank(n2), qhc[:, 4:5], swcol[:, 2:3],
                Alu.mult, Alu.mult),
                wait=[(T, tn)] + mo_free, inc=True)
            bank_free[n2] = [(V, ve)]
            va = b.do("vector", lambda e, t=t, n2=n2: e.tensor_add(
                xs[t][:, n2 * QW:(n2 + 1) * QW],
                xs[t][:, n2 * QW:(n2 + 1) * QW], ctm[:]),
                wait=[(V, ve)], inc=True)
            mo_free = [(V, va)]
            lnadds.append(va)
        qhT_cons = [(T, T.n)]
        vln = emit_ln_tile(xs[t][:], None, [(V, lnadds[-1])])
        b.dma_start("yst", y_out.ap()[t * P:(t + 1) * P, :], xs[t][:],
                    wait=[(V, vln)])

    b.do("sync", lambda e: e.nop(), wait=[(chy, chy.n)])
    b.finalize()
    return nc


def kernel(**inputs):
    nc = build(dev_tap=None)
    x = np.ascontiguousarray(inputs["L0_attn_x"])
    ident = np.eye(P, dtype=np.float32)
    base = {}
    for k, v in inputs.items():
        if k in ("L0_attn_x", "L0_attn_G"):
            continue
        a = np.ascontiguousarray(v)
        if k == "L0_ffn_weight":
            a = a.view(np.uint8)
        elif k == "L1_ffn_conv_w":
            a = a.reshape(-1)
        elif k == "L0_ffn_weight_scale":
            a = a.reshape(1)
        base[k] = a
    in_maps = []
    for c in range(8):
        m = dict(base)
        m["x"] = x[c]
        m["ident"] = ident
        in_maps.append(m)
    res = run_bass_kernel_spmd(nc, in_maps, core_ids=list(range(8)))
    return np.stack([r["y"] for r in res.results], axis=0)
